# version 01
# baseline (speedup 1.0000x reference)
import math
from contextlib import ExitStack
import numpy as np
import ml_dtypes
import concourse.bass as bass
import concourse.mybir as mybir
from concourse.bass_utils import run_bass_kernel_spmd

F32 = mybir.dt.float32
BF16 = mybir.dt.bfloat16
AF = mybir.ActivationFunctionType
ALU = mybir.AluOpType
AX = mybir.AxisListType

D = 2048
TOK = 2048
NCH = 16
NG = 4
TG = 512
NH = 16
HD = 128
SEQ = 4096
NST = SEQ // 128
NEXP = 16
EC_K = 1024
LN_EPS = 1e-5
ATT_SCALE = 1.0 / math.sqrt(HD)
MASKVAL = -30000.0
TMPLW = 6144
CONST_C = 4096

ENGS = ("sync", "gpsimd", "vector", "scalar", "tensor")
ESEM = {"sync": "sSY", "gpsimd": "sG", "vector": "sV", "scalar": "sC", "tensor": "sT"}
DSEM = {"sync": "sDS", "gpsimd": "sDG", "scalar": "sDC"}
AUTO_SERIAL = {"vector", "scalar", "gpsimd"}


class Plan:

    def __init__(self, nc):
        self.nc = nc
        self.ops = []
        self.tot = {}
        self._sems = {}
        for s in ESEM.values():
            self.sem(s)

    def sem(self, name):
        if name not in self._sems:
            self._sems[name] = self.nc.semaphore(name).__enter__()
            self.tot[name] = 0
        return self._sems[name]

    def _norm(self, waits):
        w = []
        flat = []
        def _fl(x):
            if x is None:
                return
            if isinstance(x, list):
                for y in x:
                    _fl(y)
            else:
                flat.append(x)
        for t in waits:
            _fl(t)
        for t in flat:
            if t is None:
                continue
            s, v = t
            if s.startswith("d_"):
                v = self.tot[s]
            w.append((s, v))
        return w

    def op(self, eng, fn, waits=(), inc=True):
        w = self._norm(waits)
        sem = ESEM[eng]
        if eng in AUTO_SERIAL and self.tot[sem] > 0:
            w.append((sem, self.tot[sem]))
        if inc:
            self.tot[sem] += 1
            self.ops.append((eng, fn, w, (sem, 1)))
            return (sem, self.tot[sem])
        self.ops.append((eng, fn, w, None))
        return None

    def dma(self, eng, fn, waits=(), sem="misc"):
        w = self._norm(waits)
        sem = "d_" + sem
        self.sem(sem)
        self.tot[sem] += 16
        self.ops.append((eng, fn, w, (sem, 16)))
        return (sem, self.tot[sem])

    def now(self, sem):
        return (sem, self.tot[sem])

    def fence(self):
        snap = [(s, v) for s, v in self.tot.items() if v > 0]
        for e in ENGS:
            self.ops.append((e, None, list(snap), None))

    def emit(self, block):
        waited = {e: {} for e in ENGS}

        def mk(engname):
            def section(eng):
                for (e, fn, waits, inc) in self.ops:
                    if e != engname:
                        continue
                    byname = {}
                    for (s, v) in waits:
                        byname[s] = max(byname.get(s, -1), v)
                    for s, v in byname.items():
                        if waited[e].get(s, -1) < v:
                            eng.wait_ge(self._sems[s], v)
                            waited[e][s] = v
                    if fn is None:
                        continue
                    ins = fn(eng)
                    if inc is not None:
                        ins.then_inc(self._sems[inc[0]], inc[1])
            return section

        for name in ENGS:
            if any(e == name for (e, fn, w, i) in self.ops):
                getattr(block, name)(mk(name))


def setup_consts(nc, P, ctx):
    c = {}
    c["ones1_f"] = ctx.enter_context(nc.sbuf_tensor("ones1_f", [1, 128], F32))
    c["ones1_b"] = ctx.enter_context(nc.sbuf_tensor("ones1_b", [1, 128], BF16))
    c["ones128_f"] = ctx.enter_context(nc.sbuf_tensor("ones128_f", [128, 1], F32))
    c["ones128_b"] = ctx.enter_context(nc.sbuf_tensor("ones128_b", [128, 1], BF16))
    c["eps"] = ctx.enter_context(nc.sbuf_tensor("eps_c", [128, 1], F32))
    P.op("vector", lambda v: v.memset(c["eps"][:], LN_EPS))
    P.op("vector", lambda v: v.memset(c["ones1_f"][:], 1.0))
    P.op("vector", lambda v: v.memset(c["ones1_b"][:], 1.0))
    P.op("vector", lambda v: v.memset(c["ones128_f"][:], 1.0))
    c["ready"] = P.op("vector", lambda v: v.memset(c["ones128_b"][:], 1.0))
    return c


def emit_ln_group(nc, P, C, xg, sq2, lng, rows, psA, psB, psS, psQ, t_x):
    ones128 = C["ones128_f"]
    ones1 = C["ones1_f"]
    for i in range(NCH):
        P.op("tensor", lambda t, i=i: t.matmul(psS[0:1, :], ones128[:], xg[:, i, :],
                                               start=(i == 0), stop=(i == NCH - 1)),
             waits=[t_x, C["ready"], rows.get("_prev_ln")] if i == 0 else (), inc=(i == NCH - 1))
    t_mmS = P.now("sT")
    t_qmm = [None, None]
    for i in range(NCH):
        sq = sq2[i % 2]
        t_sq = P.op("scalar", lambda s, i=i, sq=sq: s.activation(sq[:], xg[:, i, :], AF.Square),
                    waits=[t_x, t_qmm[i % 2]])
        t_qmm[i % 2] = P.op("tensor", lambda t, i=i, sq=sq: t.matmul(
            psQ[0:1, :], ones128[:], sq[:], start=(i == 0), stop=(i == NCH - 1)),
            waits=[t_sq])
    t_mmQ = t_qmm[(NCH - 1) % 2]
    P.op("vector", lambda v: v.tensor_scalar_mul(rows["mrow"][:], psS[0:1, :], 1.0 / D), waits=[t_mmS])
    P.op("vector", lambda v: v.tensor_scalar_mul(rows["var"][:], psQ[0:1, :], 1.0 / D), waits=[t_mmQ])
    P.op("vector", lambda v: v.tensor_tensor(rows["brow"][:], rows["mrow"][:], rows["mrow"][:], op=ALU.mult))
    t_var = P.op("vector", lambda v: v.tensor_sub(rows["var"][:], rows["var"][:], rows["brow"][:]))
    P.op("scalar", lambda s: s.activation(rows["rrow"][:], rows["var"][:], AF.Ln, bias=C["eps"][0:1, :]),
         waits=[t_var])
    t_rstd = P.op("scalar", lambda s: s.activation(rows["rrow"][:], rows["rrow"][:], AF.Exp, scale=-0.5))
    t_brow = P.op("vector", lambda v: v.scalar_tensor_tensor(rows["brow"][:], rows["mrow"][:], -1.0,
                                                             rows["rrow"][:], op0=ALU.mult, op1=ALU.mult),
                  waits=[t_rstd])
    P.op("tensor", lambda t: t.matmul(psA[:], ones1[:], rows["rrow"][:], start=True, stop=True),
         waits=[t_brow], inc=False)
    t_bc = P.op("tensor", lambda t: t.matmul(psB[:], ones1[:], rows["brow"][:], start=True, stop=True))
    P.op("vector", lambda v: v.tensor_tensor(lng[:], xg[:],
                                             psA[:].unsqueeze(1).to_broadcast([128, NCH, TG]),
                                             op=ALU.mult), waits=[t_bc])
    t_ln = P.op("vector", lambda v: v.tensor_tensor(lng[:], lng[:],
                                                    psB[:].unsqueeze(1).to_broadcast([128, NCH, TG]),
                                                    op=ALU.add))
    rows["_prev_ln"] = t_ln
    return t_ln


def ln_rows(nc, ctx, prefix=""):
    return {k: ctx.enter_context(nc.sbuf_tensor(prefix + k, [1, TG], F32))
            for k in ("mrow", "var", "rrow", "brow")}


def fm_view(hbm_ap, g):
    return hbm_ap.rearrange("(c p) t -> p c t", p=128)[:, :, g * TG:(g + 1) * TG]


def build_L1():
    nc = bass.Bass("TRN2", target_bir_lowering=False)
    xT = nc.dram_tensor("xT", [D, TOK], F32, kind="ExternalInput").ap()
    r0w = nc.dram_tensor("r0w", [D, NEXP], F32, kind="ExternalInput").ap()
    r0b = nc.dram_tensor("r0b", [1, NEXP], F32, kind="ExternalInput").ap()
    logits = nc.dram_tensor("logits", [TOK, NEXP], F32, kind="ExternalOutput").ap()
    lnx = nc.dram_tensor("lnx", [D, TOK], BF16, kind="ExternalOutput").ap()

    with ExitStack() as ctx:
        E = ctx.enter_context
        P = Plan(nc)
        xg = E(nc.sbuf_tensor("xg", [128, NCH, TG], F32))
        nsq = xg[:].rearrange("p c t -> p (c t)")[:, 0:SEQ]
        sq2 = [E(nc.sbuf_tensor(f"sq{i}", [128, TG], F32)) for i in range(2)]
        rb128 = sq2[0]
        lng = E(nc.sbuf_tensor("lng", [128, NCH, TG], BF16))
        rows = ln_rows(nc, ctx)
        srw = E(nc.sbuf_tensor("srw", [128, NCH, NEXP], BF16))
        srb = E(nc.sbuf_tensor("srb", [1, NEXP], BF16))
        slog = E(nc.sbuf_tensor("slog", [128, TG // 128, NEXP], F32))
        psS = E(nc.psum_tensor("psS", [128, TG], F32))
        psQ = E(nc.psum_tensor("psQ", [128, TG], F32))
        psA = E(nc.psum_tensor("psA", [128, TG], F32))
        psB = E(nc.psum_tensor("psB", [128, TG], F32))
        psL = E(nc.psum_tensor("psL", [128, NEXP], F32))
        block = E(nc.Block())

        C = setup_consts(nc, P, ctx)
        t_rw = P.dma("gpsimd", lambda g: g.dma_start(
            out=srw[:], in_=r0w.rearrange("(c p) e -> p c e", p=128)), sem="cst")
        t_rb = P.dma("gpsimd", lambda g: g.dma_start(out=srb[:], in_=r0b), sem="cst")

        for g in range(NG):
            t_x = P.dma("sync", lambda s, g=g: s.dma_start(out=xg[:], in_=fm_view(xT, g)), sem="xg")
            t_ln = emit_ln_group(nc, P, C, xg, sq2, lng, rows, psA, psB, psS, psQ, t_x)
            t_cp = None
            for tt in range(TG // 128):
                for i in range(NCH):
                    P.op("tensor", lambda t, i=i, tt=tt: t.matmul(
                        psL[:], lng[:, i, tt * 128:(tt + 1) * 128], srw[:, i, :],
                        start=(i == 0), stop=False),
                        waits=([t_ln, t_rw, t_rb, t_cp] if i == 0 else ()), inc=False)
                t_mm = P.op("tensor", lambda t: t.matmul(
                    psL[:], C["ones1_b"][:], srb[:], start=False, stop=True))
                t_cp = P.op("vector", lambda v, tt=tt: v.tensor_copy(slog[:, tt, :], psL[:]),
                            waits=[t_mm])
            t_st1 = P.dma("sync", lambda s, g=g: s.dma_start(out=fm_view(lnx, g), in_=lng[:]),
                          waits=[t_ln], sem="lng")
            t_st2 = P.dma("sync", lambda s, g=g: s.dma_start(
                out=logits.rearrange("(g tt p) e -> p (g tt) e", p=128, g=NG)[:, g * 4:(g + 1) * 4, :],
                in_=slog[:]), waits=[t_cp], sem="slog")
            P.fence()

        P.op("sync", lambda s: s.nop(), inc=False)
        P.emit(block)
    return nc


def emit_routing0(nc, P, C, ctx, E, logits_hbm, bufs):
    lgt3 = bufs["lgt"][:].rearrange("p (j e) -> p j e", j=128, e=NEXP)
    probs3 = bufs["probs"][:].rearrange("p (j e) -> p j e", j=128, e=NEXP)
    scr3 = bufs["scr"][:, 0:2048].rearrange("p (j e) -> p j e", j=128, e=NEXP)
    mneg = bufs["mneg"]
    ssum = bufs["ssum"]
    t_lg = P.dma("sync", lambda s: s.dma_start(out=bufs["lgt"][:], in_=logits_hbm), sem="lgt")
    P.op("vector", lambda v: v.reduce_max(mneg[:], lgt3, axis=AX.X, negate=True), waits=[t_lg])
    t1 = P.op("vector", lambda v: v.tensor_tensor(
        scr3, lgt3, mneg[:].unsqueeze(2).to_broadcast([128, 128, NEXP]), op=ALU.add))
    t2 = P.op("scalar", lambda s: s.activation(bufs["probs"][:], bufs["scr"][:, 0:2048], AF.Exp), waits=[t1])
    P.op("vector", lambda v: v.reduce_sum(ssum[:], probs3, axis=AX.X), waits=[t2])
    P.op("vector", lambda v: v.reciprocal(ssum[:], ssum[:]))
    t_pr = P.op("vector", lambda v: v.tensor_tensor(
        probs3, probs3, ssum[:].unsqueeze(2).to_broadcast([128, 128, NEXP]), op=ALU.mult))
    lo, hi, mid, cond, cnt, dd = (bufs[k] for k in ("lo", "hi", "mid", "cond", "cnt", "dd"))
    P.op("vector", lambda v: v.memset(lo[:], 0.0))
    t_init = P.op("vector", lambda v: v.memset(hi[:], 1.0))
    for it in range(45):
        P.op("vector", lambda v: v.tensor_add(mid[:], lo[:], hi[:]))
        t_mid = P.op("vector", lambda v: v.tensor_scalar_mul(mid[:], mid[:], 0.5))
        P.op("vector", lambda v: v.tensor_tensor(
            scr3, probs3, mid[:].unsqueeze(1).to_broadcast([128, 128, NEXP]), op=ALU.is_ge),
            waits=[t_pr, t_mid])
        t_cnt = P.op("vector", lambda v: v.reduce_sum(cnt[:], scr3.transpose([0, 2, 1]), axis=AX.X))
        t_mm = P.op("tensor", lambda t: t.matmul(bufs["psC"][0:1, 0:NEXP], C["ones128_f"][:], cnt[:],
                                                 start=True, stop=True),
                    waits=[t_cnt, bufs.get("t_row_prev"), bufs.get("t_cond_prev")])
        t_row = P.op("vector", lambda v: v.tensor_copy(bufs["row16"][:], bufs["psC"][0:1, 0:NEXP]), waits=[t_mm])
        bufs["t_row_prev"] = t_row
        t_bc = P.op("tensor", lambda t: t.matmul(bufs["psC2"][:, 0:NEXP], C["ones1_f"][:], bufs["row16"][:],
                                                 start=True, stop=True), waits=[t_row])
        t_cond = P.op("vector", lambda v: v.tensor_scalar(cond[:], bufs["psC2"][:, 0:NEXP], float(EC_K), None,
                                                 op0=ALU.is_gt), waits=[t_bc])
        bufs["t_cond_prev"] = t_cond
        P.op("vector", lambda v: v.tensor_sub(dd[:], mid[:], lo[:]))
        P.op("vector", lambda v: v.tensor_mul(dd[:], cond[:], dd[:]))
        P.op("vector", lambda v: v.tensor_add(lo[:], lo[:], dd[:]))
        P.op("vector", lambda v: v.tensor_sub(dd[:], hi[:], mid[:]))
        P.op("vector", lambda v: v.tensor_mul(dd[:], cond[:], dd[:]))
        P.op("vector", lambda v: v.tensor_add(hi[:], mid[:], dd[:]))
    P.op("vector", lambda v: v.tensor_tensor(
        scr3, probs3, hi[:].unsqueeze(1).to_broadcast([128, 128, NEXP]), op=ALU.is_ge))
    P.op("vector", lambda v: v.tensor_tensor(scr3, probs3, scr3, op=ALU.mult))
    P.op("vector", lambda v: v.reduce_sum(bufs["a05"][:], scr3[:, 0:NCH, :], axis=AX.X))
    t_a = P.op("vector", lambda v: v.tensor_scalar_mul(bufs["a05"][:], bufs["a05"][:], 0.5))
    return t_a


def emit_expert_apply(nc, P, C, lnsrc, xsrc, xdst, ew_sb, eb_row, a05, bufs, t_ready, t_w,
                      post_group=None):
    t_prev_st = None
    for g in range(NG):
        t_lnld = P.dma("sync", lambda s, g=g: s.dma_start(out=bufs["lng"][:], in_=fm_view(lnsrc, g)),
                       waits=[t_prev_st, P.now("sT")], sem="lng")
        t_xld = P.dma("sync", lambda s, g=g: s.dma_start(out=bufs["xg"][:], in_=fm_view(xsrc, g)),
                      waits=[t_prev_st], sem="xg")
        t_add = None
        for tt4 in range(4):
            tt = 4 * g + tt4
            for og in range(4):
                pb = (tt4 * 4 + og) % 2
                psE = bufs["psE"][pb]
                ytm = bufs["ytm"][pb]
                for i in range(NCH):
                    P.op("tensor", lambda t, i=i, tt4=tt4, og=og, psE=psE: t.matmul(
                        psE[:], bufs["lng"][:, i, tt4 * 128:(tt4 + 1) * 128],
                        ew_sb[:, i, og * 512:(og + 1) * 512],
                        start=(i == 0), stop=False),
                        waits=([t_lnld, t_ready, bufs["t_evac"][pb]] + t_w) if i == 0 else (), inc=False)
                t_mm = P.op("tensor", lambda t, og=og, psE=psE: t.matmul(
                    psE[:], C["ones1_b"][:], eb_row[0:1, og * 512:(og + 1) * 512],
                    start=False, stop=True))
                t_ev = P.op("vector", lambda v, psE=psE, ytm=ytm, tt=tt: v.tensor_scalar(
                    ytm[:], psE[:], a05[:, tt:tt + 1], None, op0=ALU.mult),
                    waits=[t_mm, bufs["t_tr"][pb]])
                bufs["t_evac"][pb] = t_ev
                t_trs = []
                for k in range(4):
                    t_tr = P.op("tensor", lambda t, k=k, ytm=ytm: t.transpose(
                        bufs["psT"][:, k * 128:(k + 1) * 128], ytm[:, k * 128:(k + 1) * 128],
                        C["identB"][:]),
                        waits=[t_ev, t_add] if k == 0 else ())
                    t_trs.append(t_tr)
                bufs["t_tr"][pb] = t_trs[-1]
                for k in range(4):
                    t_add = P.op("vector", lambda v, k=k, og=og, tt4=tt4: v.tensor_add(
                        bufs["xg"][:, og * 4 + k, tt4 * 128:(tt4 + 1) * 128],
                        bufs["xg"][:, og * 4 + k, tt4 * 128:(tt4 + 1) * 128],
                        bufs["psT"][:, k * 128:(k + 1) * 128]),
                        waits=[t_trs[-1], t_xld])
        t_prev_st = P.dma("sync", lambda s, g=g: s.dma_start(out=fm_view(xdst, g), in_=bufs["xg"][:]),
                          waits=[t_add], sem="xg")
        if post_group is not None:
            t_prev_st = [t_prev_st, post_group(g, t_add)]
    return t_prev_st


def emit_proj_head_fm(nc, P, C, lnall, wT_hbm, o0_w, head_out_hbm, nheads, bias_pp, scale,
                      rope, cosb, sinm, bufs, t_lnrdy, bias_col0=0):
    wT6 = wT_hbm.rearrange("(c p) o -> p c o", p=128)
    t_h = [None, None]
    t_ld = [None, None]
    for h in range(nheads):
        sb = h % 2
        strip = bufs["strips"][sb]
        hh = bufs["hh"][sb]
        t_ld[sb] = P.dma("gpsimd", lambda g, h=h, strip=strip: g.dma_start(
            out=strip[:], in_=wT6[:, :, o0_w + h * 128: o0_w + (h + 1) * 128]),
            waits=[t_h[sb]], sem=f"strip{sb}")
        t_ev = None
        for g in range(NG):
            pb = g % 2
            psP = bufs["psP"][pb]
            for i in range(NCH):
                P.op("tensor", lambda t, i=i, g=g, strip=strip, psP=psP: t.matmul(
                    psP[:], strip[:, i, :], lnall[:, i, g * TG:(g + 1) * TG],
                    start=(i == 0), stop=(i == NCH - 1)),
                    waits=[t_ld[sb], t_lnrdy, bufs["t_pev"][pb], t_h[sb]] if i == 0 else (),
                    inc=(i == NCH - 1))
            t_mm = (ESEM["tensor"], P.tot[ESEM["tensor"]])
            if bias_pp is not None:
                t_ev = P.op("scalar", lambda s, g=g, hh=hh, psP=psP, h=h: s.activation(
                    hh[:, g * TG:(g + 1) * TG], psP[:], AF.Identity,
                    bias=bias_pp[:, bias_col0 + h:bias_col0 + h + 1], scale=scale), waits=[t_mm])
            else:
                if scale != 1.0:
                    t_ev = P.op("scalar", lambda s, g=g, hh=hh, psP=psP: s.activation(
                        hh[:, g * TG:(g + 1) * TG], psP[:], AF.Identity, scale=scale), waits=[t_mm])
                else:
                    t_ev = P.op("scalar", lambda s, g=g, hh=hh, psP=psP: s.copy(
                        hh[:, g * TG:(g + 1) * TG], psP[:]), waits=[t_mm])
            bufs["t_pev"][pb] = t_ev
        if rope:
            t_s1 = P.dma("gpsimd", lambda g, hh=hh: g.dma_start(out=bufs["swp"][0:16, :], in_=hh[16:32, :]),
                         waits=[t_ev, bufs.get("t_swp_rd")], sem="swp")
            t_s2 = P.dma("gpsimd", lambda g, hh=hh: g.dma_start(out=bufs["swp"][16:32, :], in_=hh[0:16, :]),
                         sem="swp")
            t_r1 = P.op("vector", lambda v, hh=hh: v.tensor_tensor(
                bufs["rsc"][:], hh[0:32, :], cosb[:], op=ALU.mult), waits=[t_ev])
            t_r2 = P.op("vector", lambda v: v.tensor_tensor(
                bufs["swp"][:], bufs["swp"][:], sinm[:], op=ALU.mult), waits=[t_s1, t_s2])
            t_ev = P.op("vector", lambda v, hh=hh: v.tensor_add(
                hh[0:32, :], bufs["rsc"][:], bufs["swp"][:]), waits=[t_r1, t_r2])
            bufs["t_swp_rd"] = t_ev
        t_h[sb] = P.dma("sync", lambda s, h=h, hh=hh: s.dma_start(
            out=head_out_hbm[h * 128:(h + 1) * 128, :], in_=hh[:]), waits=[t_ev], sem=f"hh{sb}")
    return t_h


def emit_proj_v_tm(nc, P, C, lnall, wT_hbm, o0_w, v_out_hbm, bias_row, bufs, t_lnrdy):
    wT6 = wT_hbm.rearrange("(c p) o -> p c o", p=128)
    t_st = None
    t_ld = None
    t_vst = [None, None]
    for vog in range(4):
        t_ld = P.dma("gpsimd", lambda g, vog=vog: g.dma_start(
            out=bufs["vstrip"][:], in_=wT6[:, :, o0_w + vog * 512: o0_w + (vog + 1) * 512]),
            waits=[t_st, P.now("sT")], sem="vstrip")
        for tt in range(NTT):
            pb = tt % 2
            psP = bufs["psP"][pb]
            vtm = bufs["vtm"][pb]
            for i in range(NCH):
                P.op("tensor", lambda t, i=i, tt=tt, psP=psP: t.matmul(
                    psP[:], lnall[:, i, tt * 128:(tt + 1) * 128], bufs["vstrip"][:, i, :],
                    start=(i == 0), stop=(bias_row is None and i == NCH - 1)),
                    waits=[t_ld, t_lnrdy, bufs["t_vev"][pb]] if i == 0 else (),
                    inc=(bias_row is None and i == NCH - 1))
            if bias_row is not None:
                P.op("tensor", lambda t, vog=vog, psP=psP: t.matmul(
                    psP[:], C["ones1_b"][:], bias_row[0:1, vog * 512:(vog + 1) * 512],
                    start=False, stop=True))
            t_mm = (ESEM["tensor"], P.tot[ESEM["tensor"]])
            t_ev = P.op("vector", lambda v, psP=psP, vtm=vtm: v.tensor_copy(vtm[:], psP[:]),
                        waits=[t_mm, t_vst[pb]])
            bufs["t_vev"][pb] = t_ev
            t_st = P.dma("sync", lambda s, tt=tt, vog=vog, vtm=vtm: s.dma_start(
                out=v_out_hbm[tt * 128:(tt + 1) * 128, vog * 512:(vog + 1) * 512], in_=vtm[:]),
                waits=[t_ev], sem=f"vtm{pb}")
            t_vst[pb] = t_st
    return t_st


NTT = 16


def build_L2():
    nc = bass.Bass("TRN2", target_bir_lowering=False)
    xT = nc.dram_tensor("xT", [D, TOK], F32, kind="ExternalInput").ap()
    lnx0 = nc.dram_tensor("lnx0", [D, TOK], BF16, kind="ExternalInput").ap()
    lgt_full = nc.dram_tensor("lgt_full", [128, 2048], F32, kind="ExternalInput").ap()
    e0w = nc.dram_tensor("e0w", [D, D], F32, kind="ExternalInput").ap()
    e0b = nc.dram_tensor("e0b", [1, D], F32, kind="ExternalInput").ap()
    WqkvT = nc.dram_tensor("WqkvT", [D, 3 * D], F32, kind="ExternalInput").ap()
    qkvb_pp = nc.dram_tensor("qkvb_pp", [128, 48], F32, kind="ExternalInput").ap()
    vb_row = nc.dram_tensor("vb_row", [1, D], F32, kind="ExternalInput").ap()
    cosb = nc.dram_tensor("cosb", [32, TOK], F32, kind="ExternalInput").ap()
    sinm = nc.dram_tensor("sinm", [32, TOK], F32, kind="ExternalInput").ap()
    xT1 = nc.dram_tensor("xT1", [D, TOK], F32, kind="ExternalOutput").ap()
    q0 = nc.dram_tensor("q0", [D, TOK], BF16, kind="ExternalOutput").ap()
    k0 = nc.dram_tensor("k0", [D, TOK], BF16, kind="ExternalOutput").ap()
    v0 = nc.dram_tensor("v0", [TOK, D], BF16, kind="ExternalOutput").ap()

    with ExitStack() as ctx:
        E = ctx.enter_context
        P = Plan(nc)
        U8 = E(nc.sbuf_tensor("U8", [128, NCH * TOK], BF16))
        xg = E(nc.sbuf_tensor("xg", [128, NCH, TG], F32))
        scr_r = E(nc.sbuf_tensor("scr_r", [128, 2048], F32))
        sq2 = [E(nc.sbuf_tensor(f"sq{i}", [128, TG], F32)) for i in range(2)]
        lng = E(nc.sbuf_tensor("lng2", [128, NCH, TG], BF16))
        lgt = E(nc.sbuf_tensor("lgt", [128, 2048], F32))
        probs = E(nc.sbuf_tensor("probs", [128, 2048], F32))
        mneg = E(nc.sbuf_tensor("mneg", [128, 128], F32))
        ssum = E(nc.sbuf_tensor("ssum", [128, 128], F32))
        small16 = {k: E(nc.sbuf_tensor("r_" + k, [128, NEXP], F32))
                   for k in ("lo", "hi", "mid", "cond", "cnt", "dd", "a05")}
        row16 = E(nc.sbuf_tensor("row16", [1, NEXP], F32))
        eb_row = E(nc.sbuf_tensor("eb_row", [1, D], BF16))
        qkvb_sb = E(nc.sbuf_tensor("qkvb_sb", [128, 48], F32))
        qkvbs = E(nc.sbuf_tensor("qkvbs", [128, 48], F32))
        vb_row_sb = E(nc.sbuf_tensor("vb_row_sb", [1, D], BF16))
        cos_sb = E(nc.sbuf_tensor("cos_sb", [32, TOK], F32))
        sinm_sb = E(nc.sbuf_tensor("sinm_sb", [32, TOK], F32))
        identB = E(nc.sbuf_tensor("identB", [128, 128], BF16))
        ytm = [E(nc.sbuf_tensor(f"ytm{i}", [128, 512], BF16)) for i in range(2)]
        strips = [E(nc.sbuf_tensor(f"strip{i}", [128, NCH, 128], BF16)) for i in range(2)]
        hh = [E(nc.sbuf_tensor(f"hh{i}", [128, TOK], BF16)) for i in range(2)]
        swp = E(nc.sbuf_tensor("swp", [32, TOK], BF16))
        rsc = E(nc.sbuf_tensor("rsc", [32, TOK], BF16))
        vtm = [E(nc.sbuf_tensor(f"vtm{i}", [128, 512], BF16)) for i in range(2)]
        rows = ln_rows(nc, ctx)
        psE = [E(nc.psum_tensor(f"psE{i}", [128, 512], F32)) for i in range(2)]
        psT = E(nc.psum_tensor("psT", [128, 512], BF16))
        psS = E(nc.psum_tensor("psS", [128, TG], F32))
        psQ = E(nc.psum_tensor("psQ", [128, TG], F32))
        psA = E(nc.psum_tensor("psA", [128, TG], F32))
        psB = E(nc.psum_tensor("psB", [128, TG], F32))
        block = E(nc.Block())

        C = setup_consts(nc, P, ctx)
        C["identB"] = identB
        P.op("gpsimd", lambda g: g.memset(identB[:], 0.0))
        t_id = P.op("gpsimd", lambda g: g.affine_select(
            out=identB[:], in_=identB[:], compare_op=ALU.not_equal, fill=1.0,
            base=0, pattern=[[-1, 128]], channel_multiplier=1))

        e0w_sb = U8[:].rearrange("p (c o) -> p c o", c=NCH)
        t_ew = P.dma("gpsimd", lambda g: g.dma_start(
            out=e0w_sb, in_=e0w.rearrange("(c p) o -> p c o", p=128)), sem="e0w")
        t_eb = P.dma("gpsimd", lambda g: g.dma_start(out=eb_row[:], in_=e0b), sem="cstg")
        t_qb = P.dma("sync", lambda s: s.dma_start(out=qkvb_sb[:], in_=qkvb_pp), sem="cst")
        t_qbr = P.dma("gpsimd", lambda g: g.dma_start(out=vb_row_sb[:], in_=vb_row), sem="cstg")
        t_cos = P.dma("sync", lambda s: s.dma_start(out=cos_sb[:], in_=cosb), sem="cst")
        t_sin = P.dma("sync", lambda s: s.dma_start(out=sinm_sb[:], in_=sinm), sem="cst")
        t_qbs = P.op("vector", lambda v: v.tensor_scalar_mul(qkvbs[:], qkvb_sb[:], ATT_SCALE),
                     waits=[t_qb])

        rbufs = dict(lgt=lgt, probs=probs, scr=scr_r, mneg=mneg, ssum=ssum, row16=row16, **small16)
        rbufs["psC"] = psS
        rbufs["psC2"] = psA
        t_a05 = emit_routing0(nc, P, C, ctx, E, lgt_full, rbufs)
        P.fence()

        ebufs = dict(lng=lng, xg=xg, ytm=ytm, psE=psE, psT=psT, identB=identB,
                     t_evac=[None, None], t_tr=[None, None])
        emit_expert_apply(nc, P, C, lnx0, xT, xT1, e0w_sb, eb_row, small16["a05"][:],
                          ebufs, t_a05, [t_ew, t_eb])
        P.fence()

        lnall = U8[:].rearrange("p (c t) -> p c t", c=NCH)
        t_lnrdy = None
        for g in range(NG):
            t_x = P.dma("sync", lambda s, g=g: s.dma_start(out=xg[:], in_=fm_view(xT1, g)))
            t_lnrdy = emit_ln_group(nc, P, C, xg, sq2, lnall[:, :, g * TG:(g + 1) * TG],
                                    rows, psA, psB, psS, psQ, t_x)
            P.fence()

        vstrip = lng
        pbufs = dict(strips=strips, hh=hh, swp=swp, rsc=rsc, psP=psE,
                     t_pev=[None, None], vstrip=vstrip, vtm=vtm, t_vev=[None, None])
        emit_proj_head_fm(nc, P, C, lnall, WqkvT, 0, q0, NH, qkvb_sb, 1.0,
                          True, cos_sb[:], sinm_sb[:], pbufs, t_lnrdy)
        P.fence()
        emit_proj_head_fm(nc, P, C, lnall, WqkvT, D, k0, NH, qkvbs, ATT_SCALE,
                          True, cos_sb[:], sinm_sb[:], pbufs, t_lnrdy, bias_col0=16)
        P.fence()
        emit_proj_v_tm(nc, P, C, lnall, WqkvT, 2 * D, v0, vb_row_sb[:], pbufs, t_lnrdy)
        P.fence()
        P.op("sync", lambda s: s.nop(), inc=False)
        P.emit(block)
    return nc


def emit_attention(nc, P, C, bufs, k_seq, v_seq, q_own, attn_all, use_eps):
    t_cons = [None, None]
    t_att = None
    for h in range(NH):
        hb = h % 2
        kh, vh, qh = bufs["kh"][hb], bufs["vh"][hb], bufs["qh"][hb]
        t_kl = P.dma("sync", lambda s, h=h, kh=kh: s.dma_start(
            out=kh[:], in_=k_seq[h * 128:(h + 1) * 128, :]),
            waits=[t_cons[hb]], sem=f"kh{hb}")
        t_vl = P.dma("sync", lambda s, h=h, vh=vh: s.dma_start(
            out=vh[:].rearrange("p (st d) -> p st d", d=128),
            in_=v_seq.rearrange("(st p) d -> p st d", p=128)[:, :, h * 128:(h + 1) * 128]),
            waits=[t_cons[hb]], sem=f"vh{hb}")
        t_ql = P.dma("sync", lambda s, h=h, qh=qh: s.dma_start(
            out=qh[:], in_=q_own[h * 128:(h + 1) * 128, :]),
            waits=[t_att], sem="qh")
        t_ks = P.op("scalar", lambda s, kh=kh: s.activation(bufs["nsq"][:], kh[:], AF.Square),
                    waits=[t_kl, bufs.get("t_nsq_rd")])
        t_km = None
        t_pd = bufs.get("t_psD_rd")
        for j in range(8):
            t_nm = P.op("tensor", lambda t, j=j: t.matmul(
                bufs["psD"][0:1, :], C["ones128_f"][:], bufs["nsq"][:, j * 512:(j + 1) * 512],
                start=True, stop=True), waits=[t_ks, t_pd])
            t_dc = P.op("vector", lambda v: v.tensor_copy(bufs["drow"][:], bufs["psD"][0:1, :]),
                        waits=[t_nm])
            t_pd = t_dc
            t_mx = P.op("vector", lambda v: v.reduce_max(bufs["bmx"][:], bufs["drow"][:], axis=AX.X))
            if j == 0:
                t_km = P.op("vector", lambda v: v.tensor_copy(bufs["km"][:], bufs["bmx"][:]))
            else:
                t_km = P.op("vector", lambda v: v.tensor_max(bufs["km"][:], bufs["km"][:], bufs["bmx"][:]))
        t_km = P.op("scalar", lambda s: s.activation(bufs["km"][:], bufs["km"][:], AF.Sqrt),
                    waits=[t_km])
        t_qs = P.op("scalar", lambda s, qh=qh: s.activation(
            bufs["nsq"][:, 0:TOK], qh[:], AF.Square), waits=[t_ql, t_km])
        bufs["t_nsq_rd"] = None
        bufs["t_psD_rd"] = t_pd
        for g in range(NG):
            t_nm = P.op("tensor", lambda t, g=g: t.matmul(
                bufs["psD"][0:1, :], C["ones128_f"][:], bufs["nsq"][:, g * 512:(g + 1) * 512],
                start=True, stop=True), waits=[t_qs, bufs["t_psD_rd"]])
            t_dc = P.op("vector", lambda v: v.tensor_copy(bufs["drow"][:], bufs["psD"][0:1, :]),
                        waits=[t_nm])
            bufs["t_psD_rd"] = t_dc
            P.op("vector", lambda v: v.reduce_max(bufs["bmx"][:], bufs["drow"][:], axis=AX.X))
            t_bs = P.op("scalar", lambda s: s.activation(bufs["bmx"][:], bufs["bmx"][:], AF.Sqrt),
                        waits=[P.now("sV")])
            t_nb = P.op("vector", lambda v: v.tensor_scalar(
                bufs["bmx"][:], bufs["bmx"][:], bufs["km"][:], -1.001, op0=ALU.mult, op1=ALU.mult),
                waits=[t_bs])
            t_nbb = P.op("tensor", lambda t: t.matmul(
                bufs["psBr"][:, 0:1], C["ones1_f"][:], bufs["bmx"][:], start=True, stop=True),
                waits=[t_nb, bufs.get("t_nb_rd")])
            t_nb128 = P.op("vector", lambda v: v.tensor_copy(bufs["negbt"][:], bufs["psBr"][:, 0:1]),
                           waits=[t_nbb])
            t_exp = None
            t_exp = None
            t_avp = None
            for st in range(NST):
                sb = st % 2
                psS = bufs["psSc"][sb]
                pex = bufs["pexp"][st % 3]
                P.op("tensor", lambda t, st=st, g=g, psS=psS, kh=kh, qh=qh: t.matmul(
                    psS[:], kh[:, st * 128:(st + 1) * 128], qh[:, g * TG:(g + 1) * TG],
                    start=True, stop=False, skip_group_check=True),
                    waits=[bufs["t_exp_done"][sb], t_ql], inc=False)
                t_sc = P.op("tensor", lambda t, st=st, g=g, psS=psS: t.matmul(
                    psS[:], C["negI"][:], bufs["tmpl"][:, CONST_C + TG * g - 128 * st:
                                                       CONST_C + TG * g - 128 * st + TG],
                    start=False, stop=True, skip_group_check=True))
                t_exp = P.op("scalar", lambda s, psS=psS, pex=pex, st=st: s.activation(
                    pex[:], psS[:], AF.Exp, bias=bufs["negbt"][:]),
                    waits=[t_sc, t_nb128, bufs["t_pex_rd"][st % 3]])
                bufs["t_exp_done"][sb] = t_exp
                t_av = P.op("tensor", lambda t, st=st, vh=vh, pex=pex: t.matmul(
                    bufs["psO"][:], vh[:].rearrange("p (st d) -> p st d", d=128)[:, st, :], pex[:],
                    start=(st == 0), stop=(st == NST - 1), skip_group_check=True),
                    waits=[t_exp, t_vl] + ([bufs.get("t_psO_rd")] if st == 0 else []), inc=False)
                t_dn = P.op("tensor", lambda t, st=st, pex=pex: t.matmul(
                    bufs["psD"][0:1, :], C["ones128_b"][:], pex[:],
                    start=(st == 0), stop=(st == NST - 1), skip_group_check=True),
                    waits=([bufs.get("t_psD_rd")] if st == 0 else ()), inc=(st == NST - 1))
                bufs["t_pex_rd"][st % 3] = t_dn
            t_dcp = P.op("vector", lambda v: v.tensor_copy(bufs["drow"][:], bufs["psD"][0:1, :]),
                         waits=[P.now("sT")])
            bufs["t_psD_rd"] = t_dcp
            if use_eps:
                P.op("vector", lambda v: v.tensor_scalar_add(bufs["drow"][:], bufs["drow"][:], 1e-8))
            t_ri = P.op("vector", lambda v: v.reciprocal(bufs["drow"][:], bufs["drow"][:]))
            t_rb = P.op("tensor", lambda t: t.matmul(
                bufs["psBr"][:], C["ones1_f"][:], bufs["drow"][:], start=True, stop=True),
                waits=[t_ri, t_nb128])
            t_rb128 = P.op("vector", lambda v: v.tensor_copy(bufs["rb128"][:], bufs["psBr"][:]),
                           waits=[t_rb])
            bufs["t_nb_rd"] = t_rb128
            t_att = P.op("vector", lambda v, h=h, g=g: v.tensor_tensor(
                attn_all[:, h, g * TG:(g + 1) * TG], bufs["psO"][:], bufs["rb128"][:], op=ALU.mult),
                waits=[t_rb128])
            bufs["t_psO_rd"] = t_att
            t_cons[hb] = t_att
    return t_att


def build_L3():
    nc = bass.Bass("TRN2", target_bir_lowering=False)
    xT1 = nc.dram_tensor("xT1", [D, TOK], F32, kind="ExternalInput").ap()
    q0 = nc.dram_tensor("q0", [D, TOK], BF16, kind="ExternalInput").ap()
    k_seq = nc.dram_tensor("k_seq", [D, SEQ], BF16, kind="ExternalInput").ap()
    v_seq = nc.dram_tensor("v_seq", [SEQ, D], BF16, kind="ExternalInput").ap()
    tmpl = nc.dram_tensor("tmpl", [128, TMPLW], BF16, kind="ExternalInput").ap()
    r0w = nc.dram_tensor("r0w", [D, NEXP], F32, kind="ExternalInput").ap()
    r0b = nc.dram_tensor("r0b", [1, NEXP], F32, kind="ExternalInput").ap()
    xT2 = nc.dram_tensor("xT2", [D, TOK], F32, kind="ExternalOutput").ap()
    logits = nc.dram_tensor("logits", [TOK, NEXP], F32, kind="ExternalOutput").ap()
    lnx = nc.dram_tensor("lnx", [D, TOK], BF16, kind="ExternalOutput").ap()

    with ExitStack() as ctx:
        E = ctx.enter_context
        P = Plan(nc)
        attn_all = E(nc.sbuf_tensor("attn_all", [128, NH, TOK], BF16))
        kh = [E(nc.sbuf_tensor(f"kh{i}", [128, SEQ], BF16)) for i in range(2)]
        vh = [E(nc.sbuf_tensor(f"vh{i}", [128, SEQ], BF16)) for i in range(2)]
        _qh = E(nc.sbuf_tensor("qh", [128, TOK], BF16))
        qh = [_qh, _qh]
        km = E(nc.sbuf_tensor("km", [1, 1], F32))
        bmx = E(nc.sbuf_tensor("bmx", [1, 1], F32))
        negbt = E(nc.sbuf_tensor("negbt", [128, 1], F32))
        drow = E(nc.sbuf_tensor("drow", [1, TG], F32))
        pexp = [E(nc.sbuf_tensor(f"pexp{i}", [128, TG], BF16)) for i in range(3)]
        tmpl_sb = E(nc.sbuf_tensor("tmpl_sb", [128, TMPLW], BF16))
        negI = E(nc.sbuf_tensor("negI", [128, 128], BF16))
        xg = E(nc.sbuf_tensor("xg", [128, NCH, TG], F32))
        nsq = xg[:].rearrange("p c t -> p (c t)")[:, 0:SEQ]
        sq2 = [E(nc.sbuf_tensor(f"sq{i}", [128, TG], F32)) for i in range(2)]
        rb128 = sq2[0]
        lng = E(nc.sbuf_tensor("lng", [128, NCH, TG], BF16))
        rows = ln_rows(nc, ctx)
        srw = E(nc.sbuf_tensor("srw", [128, NCH, NEXP], BF16))
        srb = E(nc.sbuf_tensor("srb", [1, NEXP], BF16))
        slog = E(nc.sbuf_tensor("slog", [128, TG // 128, NEXP], F32))
        psSc = [E(nc.psum_tensor(f"psSc{i}", [128, TG], F32)) for i in range(2)]
        psO = E(nc.psum_tensor("psO", [128, TG], F32))
        psD = E(nc.psum_tensor("psD", [128, TG], F32))
        psBr = E(nc.psum_tensor("psBr", [128, TG], F32))
        psL = E(nc.psum_tensor("psL", [128, NEXP], F32))
        block = E(nc.Block())

        C = setup_consts(nc, P, ctx)
        C["negI"] = negI
        P.op("gpsimd", lambda g: g.memset(negI[:], 0.0))
        P.op("gpsimd", lambda g: g.affine_select(
            out=negI[:], in_=negI[:], compare_op=ALU.not_equal, fill=MASKVAL,
            base=0, pattern=[[-1, 128]], channel_multiplier=1))
        P.dma("sync", lambda s: s.dma_start(out=tmpl_sb[:], in_=tmpl), sem="cst")
        P.dma("gpsimd", lambda g: g.dma_start(
            out=srw[:], in_=r0w.rearrange("(c p) e -> p c e", p=128)), sem="cstg")
        P.dma("gpsimd", lambda g: g.dma_start(out=srb[:], in_=r0b), sem="cstg")
        P.fence()

        abufs = dict(kh=kh, vh=vh, qh=qh, nsq=nsq, km=km,
                     bmx=bmx, negbt=negbt, drow=drow, rb128=rb128, pexp=pexp,
                     tmpl=tmpl_sb, psSc=psSc, psO=psO, psD=psD, psBr=psBr,
                     t_exp_done=[None, None], t_pex_rd=[None, None, None])
        emit_attention(nc, P, C, abufs, k_seq, v_seq, q0, attn_all[:], use_eps=False)
        P.fence()

        for g in range(NG):
            t_x = P.dma("sync", lambda s, g=g: s.dma_start(out=xg[:], in_=fm_view(xT1, g)), sem="xg")
            t_res = P.op("vector", lambda v, g=g: v.tensor_tensor(
                xg[:], xg[:], attn_all[:][:, :, g * TG:(g + 1) * TG],
                op=ALU.add), waits=[t_x])
            t_st0 = P.dma("sync", lambda s, g=g: s.dma_start(out=fm_view(xT2, g), in_=xg[:]),
                          waits=[t_res], sem="xg")
            t_ln = emit_ln_group(nc, P, C, xg, sq2, lng, rows, psBr, psD, psSc[0], psSc[1], t_res)
            t_cp = None
            for tt in range(TG // 128):
                for i in range(NCH):
                    P.op("tensor", lambda t, i=i, tt=tt: t.matmul(
                        psL[:], lng[:, i, tt * 128:(tt + 1) * 128], srw[:, i, :],
                        start=(i == 0), stop=False),
                        waits=([t_ln, t_cp] if i == 0 else ()), inc=False)
                t_mm = P.op("tensor", lambda t: t.matmul(
                    psL[:], C["ones1_b"][:], srb[:], start=False, stop=True))
                t_cp = P.op("vector", lambda v, tt=tt: v.tensor_copy(slog[:, tt, :], psL[:]),
                            waits=[t_mm])
            t_st1 = P.dma("sync", lambda s, g=g: s.dma_start(out=fm_view(lnx, g), in_=lng[:]),
                          waits=[t_ln], sem="lng")
            t_st2 = P.dma("sync", lambda s, g=g: s.dma_start(
                out=logits.rearrange("(g tt p) e -> p (g tt) e", p=128, g=NG)[:, g * 4:(g + 1) * 4, :],
                in_=slog[:]), waits=[t_cp], sem="slog")
            P.fence()
        P.op("sync", lambda s: s.nop(), inc=False)
        P.emit(block)
    return nc




def emit_router1_top4(nc, P, C, lnx3g, r1w_sb, cw_sb, bufs, g, t_ln):
    t_cp = bufs.get("t_r1_last")
    for tt4 in range(4):
        tt = 4 * g + tt4
        for i in range(NCH):
            P.op("tensor", lambda t, i=i, tt4=tt4: t.matmul(
                bufs["psL"][:], lnx3g[:, i, tt4 * 128:(tt4 + 1) * 128], r1w_sb[:, i, :],
                start=(i == 0), stop=(i == NCH - 1)),
                waits=([t_ln, t_cp, bufs["t_r1w"]] if i == 0 else ()), inc=(i == NCH - 1))
        t_mm = P.now("sT")
        t_ng = P.op("vector", lambda v: v.reduce_max(bufs["negm"][:], bufs["psL"][:], axis=AX.X,
                                                     negate=True), waits=[t_mm])
        t_ex = P.op("scalar", lambda s: s.activation(bufs["probs16"][:], bufs["psL"][:], AF.Exp,
                                                     bias=bufs["negm"][:]), waits=[t_ng])
        P.op("vector", lambda v: v.max(out=bufs["m8"][:], in_=bufs["probs16"][:]), waits=[t_ex])
        P.op("vector", lambda v: v.tensor_scalar(bufs["sel16"][:], bufs["probs16"][:],
                                                 bufs["m8"][:, 3:4], None, op0=ALU.is_ge))
        P.op("vector", lambda v: v.tensor_mul(bufs["sel16"][:], bufs["sel16"][:], bufs["probs16"][:]))
        P.op("vector", lambda v: v.reduce_sum(bufs["negm"][:], bufs["sel16"][:], axis=AX.X))
        P.op("vector", lambda v: v.reciprocal(bufs["negm"][:], bufs["negm"][:]))
        t_cp = P.op("vector", lambda v, tt=tt: v.tensor_scalar(
            cw_sb[:, tt, :], bufs["sel16"][:], bufs["negm"][:], None, op0=ALU.mult))
    bufs["t_r1_last"] = t_cp
    return t_cp


def emit_ffn1_dense(nc, P, C, B, lnx_scr, x_scr, x_out,
                    we1, we3, we2, ws1, ws3, ws2, t_cwt, cwT, cw_row, r1bufs):
    lnx3g_v = B["lnx3g_v"]
    we2s = B["we2s"]
    sh_all = B["sh_all"]
    acc = B["acc"]
    XA = B["XA"]
    westr = B["westr"]
    ws2cs = B["ws2cs"]
    g_e = B["g_e"]
    ssb = B["ssb"]
    xch = B["xch"]
    ps = B["ps"]
    t_acc_rd = None
    t_combine = None
    we1r = we1.rearrange("e (c p) h -> e p c h", p=128)
    we3r = we3.rearrange("e (c p) h -> e p c h", p=128)
    we2r = we2.rearrange("e (j p) o -> e p j o", p=128)
    ws1r = ws1.rearrange("(c p) h -> p c h", p=128)
    ws3r = ws3.rearrange("(c p) h -> p c h", p=128)
    ws2r = ws2.rearrange("(hc p) o -> p hc o", p=128)
    t_w1 = [None, None]
    t_w3 = [None, None]
    t_w2 = None
    t_ws2 = [None, None]
    t_lnx3_ld = None
    t_cwrow_rd = None
    t_psBr_rd = None
    t_ps0_rd = None
    t_ps1_rd = None
    t_moe_rd = [None, None]
    t_sh = None
    t_ssb_rd = None
    for g in range(NG):
        t_lnx3_ld = P.dma("sync", lambda s, g=g: s.dma_start(
            out=lnx3g_v[:], in_=fm_view(lnx_scr, g)),
            waits=[t_combine if g > 0 else P.now("sT"), P.now("sV")], sem="lnx3g")
        t_accz = P.op("vector", lambda v: v.memset(XA[:], 0.0),
                      waits=[t_combine, t_acc_rd])
        for e in range(NEXP):
            t_cwr = P.dma("gpsimd", lambda gg, e=e: gg.dma_start(
                out=cw_row, in_=cwT[e:e + 1, :]), waits=[t_cwrow_rd, t_cwt], sem="cwrow")
            t_br = P.op("tensor", lambda t, g=g: t.matmul(
                ps[4][:], C["ones1_f"][:], cw_row[0:1, g * TG:(g + 1) * TG],
                start=True, stop=True), waits=[t_cwr, t_psBr_rd])
            t_cwrow_rd = t_br
            t_ge = None
            for j in range(8):
                jb = j % 2
                w1 = westr[0][jb]
                w3 = westr[1][jb]
                t_w1[jb] = P.dma("gpsimd", lambda gg, e=e, j=j, w1=w1: gg.dma_start(
                    out=w1[:], in_=we1r[e, :, :, j * 128:(j + 1) * 128]),
                    waits=[t_w1[jb]] + ([P.now("sT")] if True else []), sem=f"w1s{jb}")
                t_w3[jb] = P.dma("gpsimd", lambda gg, e=e, j=j, w3=w3: gg.dma_start(
                    out=w3[:], in_=we3r[e, :, :, j * 128:(j + 1) * 128]),
                    waits=[t_w3[jb]], sem=f"w3s{jb}")
                for i in range(NCH):
                    P.op("tensor", lambda t, i=i, w1=w1: t.matmul(
                        ps[0][:], w1[:, i, :], lnx3g_v[:, i, :],
                        start=(i == 0), stop=(i == NCH - 1)),
                        waits=([t_w1[jb], t_lnx3_ld, t_ps0_rd] if i == 0 else ()),
                        inc=(i == NCH - 1))
                t_p1 = P.now("sT")
                for i in range(NCH):
                    P.op("tensor", lambda t, i=i, w3=w3: t.matmul(
                        ps[1][:], w3[:, i, :], lnx3g_v[:, i, :],
                        start=(i == 0), stop=(i == NCH - 1)),
                        waits=([t_w3[jb], t_ps1_rd] if i == 0 else ()), inc=(i == NCH - 1))
                t_p3 = P.now("sT")
                t_si = P.op("scalar", lambda s: s.activation(ssb[:], ps[0][:], AF.Silu),
                            waits=[t_p1, t_ssb_rd])
                t_ps0_rd = t_si
                t_gm = P.op("vector", lambda v, j=j: v.tensor_tensor(
                    g_e[:, j, :], ssb[:], ps[1][:], op=ALU.mult),
                    waits=[t_si, t_p3] + ([P.now("sT")] if j == 0 else []))
                t_ps1_rd = t_gm
                t_ssb_rd = t_gm
                t_ge = P.op("vector", lambda v, j=j: v.tensor_tensor(
                    g_e[:, j, :], g_e[:, j, :], ps[4][:], op=ALU.mult),
                    waits=[t_gm, t_br])
            t_psBr_rd = t_ge
            t_w2_prev = t_w2
            t_w2 = P.dma("gpsimd", lambda gg, e=e: gg.dma_start(
                out=we2s[:, :, :], in_=we2r[e, :, :, :]),
                waits=[t_w2_prev, P.now("sT")], sem="we2s")
            for oc in range(NCH):
                pb = 2 + (oc % 2)
                for j in range(8):
                    P.op("tensor", lambda t, j=j, oc=oc, pb=pb: t.matmul(
                        ps[pb][:], we2s[:, j, oc * 128:(oc + 1) * 128], g_e[:, j, :],
                        start=(j == 0), stop=(j == 7)),
                        waits=([t_w2, t_ge, t_moe_rd[pb - 2]] if j == 0 else ()),
                        inc=(j == 7))
                t_mo = P.now("sT")
                t_acc_rd = P.op("vector", lambda v, oc=oc, pb=pb: v.tensor_add(
                    acc[:, oc, :], acc[:, oc, :], ps[pb][:]), waits=[t_mo, t_accz])
                t_moe_rd[pb - 2] = t_acc_rd
        for hc in range(32):
            jb = hc % 2
            w1 = westr[0][jb]
            w3 = westr[1][jb]
            t_w1[jb] = P.dma("gpsimd", lambda gg, hc=hc, w1=w1: gg.dma_start(
                out=w1[:], in_=ws1r[:, :, hc * 128:(hc + 1) * 128]),
                waits=[t_w1[jb], P.now("sT")], sem=f"w1s{jb}")
            t_w3[jb] = P.dma("gpsimd", lambda gg, hc=hc, w3=w3: gg.dma_start(
                out=w3[:], in_=ws3r[:, :, hc * 128:(hc + 1) * 128]),
                waits=[t_w3[jb]], sem=f"w3s{jb}")
            for i in range(NCH):
                P.op("tensor", lambda t, i=i, w1=w1: t.matmul(
                    ps[0][:], w1[:, i, :], lnx3g_v[:, i, :],
                    start=(i == 0), stop=(i == NCH - 1)),
                    waits=([t_w1[jb], t_ps0_rd] if i == 0 else ()), inc=(i == NCH - 1))
            t_p1 = P.now("sT")
            for i in range(NCH):
                P.op("tensor", lambda t, i=i, w3=w3: t.matmul(
                    ps[1][:], w3[:, i, :], lnx3g_v[:, i, :],
                    start=(i == 0), stop=(i == NCH - 1)),
                    waits=([t_w3[jb], t_ps1_rd] if i == 0 else ()), inc=(i == NCH - 1))
            t_p3 = P.now("sT")
            t_si = P.op("scalar", lambda s: s.activation(ssb[:], ps[0][:], AF.Silu),
                        waits=[t_p1, t_ssb_rd])
            t_ps0_rd = t_si
            t_sh = P.op("vector", lambda v, hc=hc: v.tensor_tensor(
                sh_all[:, hc, :], ssb[:], ps[1][:], op=ALU.mult), waits=[t_si, t_p3])
            t_ps1_rd = t_sh
            t_ssb_rd = t_sh
        t_xc = [None, None]
        for oc in range(NCH):
            cb = oc % 2
            pb = 2 + (oc % 2)
            t_ws2[cb] = P.dma("gpsimd", lambda gg, oc=oc, cb=cb: gg.dma_start(
                out=ws2cs[cb][:], in_=ws2r[:, :, oc * 128:(oc + 1) * 128]),
                waits=[t_ws2[cb], P.now("sT")], sem=f"ws2cs{cb}")
            t_xc[cb] = P.dma("sync", lambda s, oc=oc, cb=cb, g=g: s.dma_start(
                out=xch[cb][:], in_=x_scr.rearrange("(c p) t -> p c t", p=128)
                [:, oc, g * TG:(g + 1) * TG]), waits=[t_xc[cb]], sem=f"xch{cb}")
            for hc in range(32):
                P.op("tensor", lambda t, hc=hc, cb=cb, pb=pb: t.matmul(
                    ps[pb][:], ws2cs[cb][:, hc, :], sh_all[:, hc, :],
                    start=(hc == 0), stop=(hc == 31)),
                    waits=([t_ws2[cb], t_sh, t_moe_rd[pb - 2]] if hc == 0 else ()),
                    inc=(hc == 31))
            t_sm = P.now("sT")
            t_u = P.op("vector", lambda v, oc=oc, cb=cb: v.scalar_tensor_tensor(
                xch[cb][:], acc[:, oc, :], 0.4, xch[cb][:], op0=ALU.mult, op1=ALU.add),
                waits=[t_xc[cb], t_sm])
            t_combine = P.op("vector", lambda v, oc=oc, cb=cb, pb=pb: v.scalar_tensor_tensor(
                xch[cb][:], ps[pb][:], 0.1, xch[cb][:], op0=ALU.mult, op1=ALU.add))
            t_moe_rd[pb - 2] = t_combine
            t_xc[cb] = P.dma("sync", lambda s, oc=oc, cb=cb, g=g: s.dma_start(
                out=fm_view(x_out, g)[:, oc, :], in_=xch[cb][:]),
                waits=[t_combine], sem=f"xch{cb}")
        P.fence()

    return t_combine


def build_L4():
    nc = bass.Bass("TRN2", target_bir_lowering=False)
    xT2 = nc.dram_tensor("xT2", [D, TOK], F32, kind="ExternalInput").ap()
    lnx2 = nc.dram_tensor("lnx2", [D, TOK], BF16, kind="ExternalInput").ap()
    lgt_full = nc.dram_tensor("lgt_full", [128, 2048], F32, kind="ExternalInput").ap()
    e0w = nc.dram_tensor("e0w", [D, D], F32, kind="ExternalInput").ap()
    e0b = nc.dram_tensor("e0b", [1, D], F32, kind="ExternalInput").ap()
    r1w = nc.dram_tensor("r1w", [D, NEXP], F32, kind="ExternalInput").ap()
    we1 = nc.dram_tensor("we1", [NEXP, D, 1024], F32, kind="ExternalInput").ap()
    we3 = nc.dram_tensor("we3", [NEXP, D, 1024], F32, kind="ExternalInput").ap()
    we2 = nc.dram_tensor("we2", [NEXP, 1024, D], F32, kind="ExternalInput").ap()
    ws1 = nc.dram_tensor("ws1", [D, 4096], F32, kind="ExternalInput").ap()
    ws3 = nc.dram_tensor("ws3", [D, 4096], F32, kind="ExternalInput").ap()
    ws2 = nc.dram_tensor("ws2", [4096, D], F32, kind="ExternalInput").ap()
    wqT = nc.dram_tensor("wqT", [D, D], F32, kind="ExternalInput").ap()
    wkT = nc.dram_tensor("wkT", [D, D], F32, kind="ExternalInput").ap()
    wvT = nc.dram_tensor("wvT", [D, D], F32, kind="ExternalInput").ap()
    xT4 = nc.dram_tensor("xT4", [D, TOK], F32, kind="ExternalOutput").ap()
    q1 = nc.dram_tensor("q1", [D, TOK], BF16, kind="ExternalOutput").ap()
    k1 = nc.dram_tensor("k1", [D, TOK], BF16, kind="ExternalOutput").ap()
    v1 = nc.dram_tensor("v1", [TOK, D], BF16, kind="ExternalOutput").ap()

    x3_scr = nc.dram_tensor("x3_scr", [D, TOK], F32, kind="Internal").ap()
    lnx3_scr = nc.dram_tensor("lnx3_scr", [D, TOK], BF16, kind="Internal").ap()

    with ExitStack() as ctx:
        E = ctx.enter_context
        P = Plan(nc)
        U8a = E(nc.sbuf_tensor("U8a", [128, NCH * TOK], BF16))
        UB = E(nc.sbuf_tensor("UB", [128, 8192], F32))
        XA = E(nc.sbuf_tensor("XA", [128, 8192], F32))
        westr = [[E(nc.sbuf_tensor(f"we{w}s{i}", [128, NCH, 128], BF16)) for i in range(2)]
                 for w in (1, 3)]
        ws2cs = [E(nc.sbuf_tensor(f"ws2cs{i}", [128, 32, 128], BF16)) for i in range(2)]
        g_e = E(nc.sbuf_tensor("g_e", [128, 8, TG], BF16))
        ssb = E(nc.sbuf_tensor("ssb", [128, TG], BF16))
        xch = [E(nc.sbuf_tensor(f"xch{i}", [128, TG], F32)) for i in range(2)]
        cw_sb = E(nc.sbuf_tensor("cw_sb", [128, NCH, NEXP], F32))
        r1w_sb = E(nc.sbuf_tensor("r1w_sb", [128, NCH, NEXP], BF16))
        probs16 = E(nc.sbuf_tensor("probs16", [128, NEXP], F32))
        sel16 = E(nc.sbuf_tensor("sel16", [128, NEXP], F32))
        m8 = E(nc.sbuf_tensor("m8", [128, 8], F32))
        negm = E(nc.sbuf_tensor("negm", [128, 1], F32))
        mneg = E(nc.sbuf_tensor("mneg", [128, 128], F32))
        ssum = E(nc.sbuf_tensor("ssum", [128, 128], F32))
        small16 = {k: E(nc.sbuf_tensor("r_" + k, [128, NEXP], F32))
                   for k in ("lo", "hi", "mid", "cond", "cnt", "dd", "a05")}
        row16 = E(nc.sbuf_tensor("row16", [1, NEXP], F32))
        eb_row = E(nc.sbuf_tensor("eb_row", [1, D], BF16))
        identB = E(nc.sbuf_tensor("identB", [128, 128], BF16))
        identF = E(nc.sbuf_tensor("identF", [128, 128], F32))
        ytm = [E(nc.sbuf_tensor(f"ytm{i}", [128, 512], BF16)) for i in range(2)]
        sq2 = [E(nc.sbuf_tensor(f"sq{i}", [128, TG], F32)) for i in range(2)]
        hh = [E(nc.sbuf_tensor(f"hh{i}", [128, TOK], BF16)) for i in range(2)]
        vtm = [E(nc.sbuf_tensor(f"vtm{i}", [128, 512], BF16)) for i in range(2)]
        rows = ln_rows(nc, ctx)
        ps = [E(nc.psum_tensor(f"ps{i}", [128, 512], F32)) for i in range(6)]
        psL = E(nc.psum_tensor("psL", [128, NEXP], F32))
        psT = E(nc.psum_tensor("psT", [128, 512], BF16))
        block = E(nc.Block())

        C = setup_consts(nc, P, ctx)
        C["identB"] = identB
        P.op("gpsimd", lambda g: g.memset(identB[:], 0.0))
        P.op("gpsimd", lambda g: g.affine_select(
            out=identB[:], in_=identB[:], compare_op=ALU.not_equal, fill=1.0,
            base=0, pattern=[[-1, 128]], channel_multiplier=1))
        P.op("gpsimd", lambda g: g.memset(identF[:], 0.0))
        P.op("gpsimd", lambda g: g.affine_select(
            out=identF[:], in_=identF[:], compare_op=ALU.not_equal, fill=1.0,
            base=0, pattern=[[-1, 128]], channel_multiplier=1))

        lgt_t = UB[:, 0:2048]
        probs_t = UB[:, 2048:4096]
        scr_t = UB[:, 4096:6144]
        rbufs = dict(lgt=lgt_t, probs=probs_t, scr=scr_t, mneg=mneg, ssum=ssum, row16=row16,
                     **small16)
        rbufs["psC"] = ps[0]
        rbufs["psC2"] = ps[1]
        t_a05 = emit_routing0(nc, P, C, ctx, E, lgt_full, rbufs)
        e0w_sb = U8a[:].rearrange("p (c o) -> p c o", c=NCH)
        t_ew = P.dma("gpsimd", lambda g: g.dma_start(
            out=e0w_sb, in_=e0w.rearrange("(c p) o -> p c o", p=128)), sem="e0w")
        t_eb = P.dma("gpsimd", lambda g: g.dma_start(out=eb_row[:], in_=e0b), sem="cstg")
        t_r1w = P.dma("gpsimd", lambda g: g.dma_start(
            out=r1w_sb[:], in_=r1w.rearrange("(c p) e -> p c e", p=128)), sem="cstg")
        P.fence()

        lng_v = UB[:, 0:4096].bitcast(BF16).rearrange("p (c t) -> p c t", c=NCH)
        lnx3g_v = UB[:, 0:4096].bitcast(BF16).rearrange("p (c t) -> p c t", c=NCH)
        r1bufs = dict(psL=psL, negm=negm, probs16=probs16, m8=m8, sel16=sel16, t_r1w=t_r1w)
        xg3 = XA[:].rearrange("p (c t) -> p c t", c=NCH)

        def post_group(g, t_add):
            t_ln = emit_ln_group(nc, P, C, xg3, sq2, lnx3g_v, rows,
                                 ps[4], ps[5], ps[2], ps[3], t_add)
            t_st = P.dma("sync", lambda s, g=g: s.dma_start(
                out=fm_view(lnx3_scr, g), in_=lnx3g_v[:]), waits=[t_ln], sem="lnx3")
            t_r1 = emit_router1_top4(nc, P, C, lnx3g_v, r1w_sb, cw_sb[:], r1bufs, g, t_ln)
            return [t_st, t_r1]

        ebufs = dict(lng=lng_v, xg=xg3, ytm=ytm, psE=[ps[0], ps[1]], psT=psT, identB=identB,
                     t_evac=[None, None], t_tr=[None, None])
        emit_expert_apply(nc, P, C, lnx2, xT2, x3_scr, e0w_sb, eb_row, small16["a05"][:],
                          ebufs, t_a05, [t_ew, t_eb], post_group=post_group)
        P.fence()

        cwT = UB[0:16, 4096:6144]
        cw_row = UB[0:1, 6144:8192]
        t_cwt = None
        t_cwt_prev = None
        for tt in range(NTT):
            t_tr = P.op("tensor", lambda t, tt=tt: t.transpose(
                ps[4][0:16, 0:128], cw_sb[:, tt, :], identF[:]),
                waits=[t_cwt_prev, r1bufs.get("t_r1_last")])
            t_cwt = P.op("vector", lambda v, tt=tt: v.tensor_copy(
                cwT[:, tt * 128:(tt + 1) * 128], ps[4][0:16, 0:128]), waits=[t_tr])
            t_cwt_prev = t_cwt
        P.fence()

        B = dict(lnx3g_v=lnx3g_v,
                 we2s=U8a[:, 0:16384].bitcast(BF16).rearrange("p (j o) -> p j o", j=8),
                 sh_all=U8a[:, 16384:32768].bitcast(BF16).rearrange("p (hc t) -> p hc t", hc=32),
                 acc=XA[:].rearrange("p (c t) -> p c t", c=NCH), XA=XA,
                 westr=westr, ws2cs=ws2cs, g_e=g_e, ssb=ssb, xch=xch, ps=ps, psL=psL)
        emit_ffn1_dense(nc, P, C, B, lnx3_scr, x3_scr, xT4,
                        we1, we3, we2, ws1, ws3, ws2, t_cwt, cwT, cw_row, r1bufs)
        lnall4 = U8a[:].rearrange("p (c t) -> p c t", c=NCH)
        t_lnrdy = None
        for g in range(NG):
            t_x = P.dma("sync", lambda s, g=g: s.dma_start(out=xg3[:], in_=fm_view(xT4, g)), sem="xg")
            t_lnrdy = emit_ln_group(nc, P, C, xg3, sq2, lnall4[:, :, g * TG:(g + 1) * TG],
                                    rows, ps[4], ps[5], ps[2], ps[3], t_x)
            P.fence()
        vstrip4 = UB[:, 0:4096].bitcast(BF16).rearrange("p (c t) -> p c t", c=NCH)
        pbufs = dict(strips=[westr[0][0], westr[0][1]], hh=hh, swp=None, rsc=None,
                     psP=[ps[0], ps[1]], t_pev=[None, None], vstrip=vstrip4,
                     vtm=vtm, t_vev=[None, None])
        emit_proj_head_fm(nc, P, C, lnall4, wqT, 0, q1, NH, None, 1.0, False, None, None,
                          pbufs, t_lnrdy)
        P.fence()
        emit_proj_head_fm(nc, P, C, lnall4, wkT, 0, k1, NH, None, ATT_SCALE, False, None, None,
                          pbufs, t_lnrdy)
        P.fence()
        emit_proj_v_tm(nc, P, C, lnall4, wvT, 0, v1, None, pbufs, t_lnrdy)
        P.fence()
        P.op("sync", lambda s: s.nop(), inc=False)
        P.emit(block)
    return nc


def build_L5():
    nc = bass.Bass("TRN2", target_bir_lowering=False)
    xT4 = nc.dram_tensor("xT4", [D, TOK], F32, kind="ExternalInput").ap()
    q1 = nc.dram_tensor("q1", [D, TOK], BF16, kind="ExternalInput").ap()
    k_seq = nc.dram_tensor("k_seq", [D, SEQ], BF16, kind="ExternalInput").ap()
    v_seq = nc.dram_tensor("v_seq", [SEQ, D], BF16, kind="ExternalInput").ap()
    tmpl = nc.dram_tensor("tmpl", [128, TMPLW], BF16, kind="ExternalInput").ap()
    woT = nc.dram_tensor("woT", [D, D], F32, kind="ExternalInput").ap()
    r1w = nc.dram_tensor("r1w", [D, NEXP], F32, kind="ExternalInput").ap()
    we1 = nc.dram_tensor("we1", [NEXP, D, 1024], F32, kind="ExternalInput").ap()
    we3 = nc.dram_tensor("we3", [NEXP, D, 1024], F32, kind="ExternalInput").ap()
    we2 = nc.dram_tensor("we2", [NEXP, 1024, D], F32, kind="ExternalInput").ap()
    ws1 = nc.dram_tensor("ws1", [D, 4096], F32, kind="ExternalInput").ap()
    ws3 = nc.dram_tensor("ws3", [D, 4096], F32, kind="ExternalInput").ap()
    ws2 = nc.dram_tensor("ws2", [4096, D], F32, kind="ExternalInput").ap()
    xT6 = nc.dram_tensor("xT6", [D, TOK], F32, kind="ExternalOutput").ap()

    x5_scr = nc.dram_tensor("x5_scr", [D, TOK], F32, kind="Internal").ap()
    lnx5_scr = nc.dram_tensor("lnx5_scr", [D, TOK], BF16, kind="Internal").ap()

    with ExitStack() as ctx:
        E = ctx.enter_context
        P = Plan(nc)
        U8a = E(nc.sbuf_tensor("U8a", [128, NCH * TOK], BF16))
        UB = E(nc.sbuf_tensor("UB", [128, 8192], F32))
        XA = E(nc.sbuf_tensor("XA", [128, 8192], F32))
        khb = [E(nc.sbuf_tensor(f"khb{i}", [128, SEQ], BF16)) for i in range(2)]
        vhb = [E(nc.sbuf_tensor(f"vhb{i}", [128, SEQ], BF16)) for i in range(2)]
        _qh = E(nc.sbuf_tensor("qh", [128, TOK], BF16))
        qh2 = [_qh, _qh]
        tmpl_sb = E(nc.sbuf_tensor("tmpl_sb", [128, TMPLW], BF16))
        pexp = [E(nc.sbuf_tensor(f"pexp{i}", [128, TG], BF16)) for i in range(3)]
        km = E(nc.sbuf_tensor("km", [1, 1], F32))
        bmx = E(nc.sbuf_tensor("bmx", [1, 1], F32))
        negbt = E(nc.sbuf_tensor("negbt", [128, 1], F32))
        drow = E(nc.sbuf_tensor("drow", [1, TG], F32))
        negI = E(nc.sbuf_tensor("negI", [128, 128], BF16))
        identF = E(nc.sbuf_tensor("identF", [128, 128], F32))
        ssb = E(nc.sbuf_tensor("ssb", [128, TG], BF16))
        xch = [E(nc.sbuf_tensor(f"xch{i}", [128, TG], F32)) for i in range(2)]
        cw_sb = E(nc.sbuf_tensor("cw_sb", [128, NCH, NEXP], F32))
        r1w_sb = E(nc.sbuf_tensor("r1w_sb", [128, NCH, NEXP], BF16))
        probs16 = E(nc.sbuf_tensor("probs16", [128, NEXP], F32))
        sel16 = E(nc.sbuf_tensor("sel16", [128, NEXP], F32))
        m8 = E(nc.sbuf_tensor("m8", [128, 8], F32))
        negm = E(nc.sbuf_tensor("negm", [128, 1], F32))
        sq2 = [E(nc.sbuf_tensor(f"sq{i}", [128, TG], F32)) for i in range(2)]
        rb128 = sq2[0]
        rows = ln_rows(nc, ctx)
        ps = [E(nc.psum_tensor(f"ps{i}", [128, 512], F32)) for i in range(6)]
        psL = E(nc.psum_tensor("psL", [128, NEXP], F32))
        block = E(nc.Block())

        C = setup_consts(nc, P, ctx)
        P.op("gpsimd", lambda g: g.memset(negI[:], 0.0))
        P.op("gpsimd", lambda g: g.affine_select(
            out=negI[:], in_=negI[:], compare_op=ALU.not_equal, fill=MASKVAL,
            base=0, pattern=[[-1, 128]], channel_multiplier=1))
        P.op("gpsimd", lambda g: g.memset(identF[:], 0.0))
        P.op("gpsimd", lambda g: g.affine_select(
            out=identF[:], in_=identF[:], compare_op=ALU.not_equal, fill=1.0,
            base=0, pattern=[[-1, 128]], channel_multiplier=1))
        C["negI"] = negI
        P.dma("sync", lambda s: s.dma_start(out=tmpl_sb[:], in_=tmpl), sem="cst")
        t_r1w = P.dma("gpsimd", lambda g: g.dma_start(
            out=r1w_sb[:], in_=r1w.rearrange("(c p) e -> p c e", p=128)), sem="cstg")
        P.fence()

        attn_all = U8a[:].rearrange("p (h t) -> p h t", h=NH)
        abufs = dict(kh=khb, vh=vhb, qh=qh2, nsq=XA[:].rearrange("p a -> p a")[:, 0:SEQ],
                     km=km, bmx=bmx, negbt=negbt, drow=drow, rb128=rb128, pexp=pexp,
                     tmpl=tmpl_sb, psSc=[ps[0], ps[1]], psO=ps[2], psD=ps[3], psBr=ps[4],
                     t_exp_done=[None, None], t_pex_rd=[None, None, None])
        emit_attention(nc, P, C, abufs, k_seq, v_seq, q1, attn_all, use_eps=True)
        P.fence()

        lnx5g_v = UB[:, 0:4096].bitcast(BF16).rearrange("p (c t) -> p c t", c=NCH)
        xg5 = XA[:].rearrange("p (c t) -> p c t", c=NCH)
        woT6 = woT.rearrange("(c p) o -> p c o", p=128)
        wstr = [khb[0][:, 0:2048].rearrange("p (c o) -> p c o", c=NCH),
                khb[1][:, 0:2048].rearrange("p (c o) -> p c o", c=NCH)]
        r1bufs = dict(psL=psL, negm=negm, probs16=probs16, m8=m8, sel16=sel16, t_r1w=t_r1w)
        t_wl = [None, None]
        t_pP_rd = [None, None]
        t_x5st = None
        for g in range(NG):
            t_resid = None
            for oc in range(NCH):
                cb = oc % 2
                t_wl[cb] = P.dma("gpsimd", lambda gg, oc=oc, cb=cb: gg.dma_start(
                    out=wstr[cb][:], in_=woT6[:, :, oc * 128:(oc + 1) * 128]),
                    waits=[t_wl[cb], P.now("sT")], sem=f"wstr{cb}")
                for h in range(NH):
                    P.op("tensor", lambda t, h=h, oc=oc, cb=cb, g=g: t.matmul(
                        ps[cb][:], wstr[cb][:, h, :], attn_all[:, h, g * TG:(g + 1) * TG],
                        start=(h == 0), stop=(h == NH - 1)),
                        waits=([t_wl[cb], t_pP_rd[cb]] if h == 0 else ()), inc=(h == NH - 1))
                t_mm = P.now("sT")
                t_xc = P.dma("sync", lambda s, oc=oc, cb=cb, g=g: s.dma_start(
                    out=xch[cb][:], in_=fm_view(xT4, g)[:, oc, :]),
                    waits=[P.now("sV")], sem=f"xch{cb}")
                t_resid = P.op("vector", lambda v, oc=oc, cb=cb: v.tensor_add(
                    xg5[:, oc, :], xch[cb][:], ps[cb][:]),
                    waits=[t_mm, t_xc] + ([P.now("sC"), P.now("sV")] if (g == 0 and oc == 0) else []))
                t_pP_rd[cb] = t_resid
            t_x5st = P.dma("sync", lambda s, g=g: s.dma_start(
                out=fm_view(x5_scr, g), in_=xg5[:]), waits=[t_resid], sem="x5")
            t_ln = emit_ln_group(nc, P, C, xg5, sq2, lnx5g_v, rows,
                                 ps[4], ps[5], ps[2], ps[3], t_resid)
            t_lnst = P.dma("sync", lambda s, g=g: s.dma_start(
                out=fm_view(lnx5_scr, g), in_=lnx5g_v[:]), waits=[t_ln], sem="lnx5")
            t_r1 = emit_router1_top4(nc, P, C, lnx5g_v, r1w_sb, cw_sb[:], r1bufs, g, t_ln)
            P.fence()

        cwT = UB[0:16, 4096:6144]
        cw_row = UB[0:1, 6144:8192]
        t_cwt = None
        t_cwt_prev = None
        for tt in range(NTT):
            t_tr = P.op("tensor", lambda t, tt=tt: t.transpose(
                ps[4][0:16, 0:128], cw_sb[:, tt, :], identF[:]),
                waits=[t_cwt_prev, r1bufs.get("t_r1_last")])
            t_cwt = P.op("vector", lambda v, tt=tt: v.tensor_copy(
                cwT[:, tt * 128:(tt + 1) * 128], ps[4][0:16, 0:128]), waits=[t_tr])
            t_cwt_prev = t_cwt
        P.fence()

        westr5 = [[khb[0][:, 0:2048].rearrange("p (c o) -> p c o", c=NCH),
                   khb[0][:, 2048:4096].rearrange("p (c o) -> p c o", c=NCH)],
                  [khb[1][:, 0:2048].rearrange("p (c o) -> p c o", c=NCH),
                   khb[1][:, 2048:4096].rearrange("p (c o) -> p c o", c=NCH)]]
        ws2cs5 = [vhb[0][:].rearrange("p (hc o) -> p hc o", hc=32),
                  vhb[1][:].rearrange("p (hc o) -> p hc o", hc=32)]
        g_e5 = tmpl_sb[:, 0:4096].rearrange("p (j t) -> p j t", j=8)
        B = dict(lnx3g_v=lnx5g_v,
                 we2s=U8a[:, 0:16384].bitcast(BF16).rearrange("p (j o) -> p j o", j=8),
                 sh_all=U8a[:, 16384:32768].bitcast(BF16).rearrange("p (hc t) -> p hc t", hc=32),
                 acc=XA[:].rearrange("p (c t) -> p c t", c=NCH), XA=XA,
                 westr=westr5, ws2cs=ws2cs5, g_e=g_e5, ssb=ssb, xch=xch, ps=ps, psL=psL)
        emit_ffn1_dense(nc, P, C, B, lnx5_scr, x5_scr, xT6,
                        we1, we3, we2, ws1, ws3, ws2, t_cwt, cwT, cw_row, r1bufs)
        P.fence()
        P.op("sync", lambda s: s.nop(), inc=False)
        P.emit(block)
    return nc



def _shard_xT(x):
    XT = np.ascontiguousarray(x.reshape(16384, D).T)
    return [np.ascontiguousarray(XT[:, c * TOK:(c + 1) * TOK]) for c in range(8)]


def _rope_tables(core):
    q0 = 2048 * (core % 2)
    pos = np.arange(q0, q0 + TOK, dtype=np.float64)
    inv = 1.0 / (10000.0 ** (np.arange(0, 32, 2, dtype=np.float64) / 32.0))
    ang = pos[None, :] * inv[:, None]
    cos = np.cos(ang)
    sin = np.sin(ang)
    cos32 = np.concatenate([cos, cos], axis=0).astype(np.float32)
    sinm = np.concatenate([-sin, sin], axis=0).astype(np.float32)
    return cos32, sinm


def _rot_logits(logits_full, core):
    own = logits_full[core * TOK:(core + 1) * TOK]
    rest = np.concatenate([logits_full[:core * TOK], logits_full[(core + 1) * TOK:]], axis=0)
    rot = np.concatenate([own, rest], axis=0)
    return np.ascontiguousarray(rot.reshape(128, 128, 16).transpose(1, 0, 2).reshape(128, 2048))


def prep_L2_maps(inputs, logits_full):
    WqkvT = np.ascontiguousarray(inputs["L0_attn_Wqkv_w"].T)
    qkvb = inputs["L0_attn_Wqkv_b"]
    qkvb_pp = np.ascontiguousarray(qkvb.reshape(48, 128).T)
    vb_row = np.ascontiguousarray(qkvb[4096:].reshape(1, 2048))
    e0b = np.ascontiguousarray(inputs["L0_ffn_expert_b"].reshape(1, 2048))
    maps = []
    for c in range(8):
        cos32, sinm = _rope_tables(c)
        maps.append({
            "lgt_full": _rot_logits(logits_full, c),
            "e0w": inputs["L0_ffn_expert_w"],
            "e0b": e0b,
            "WqkvT": WqkvT,
            "qkvb_pp": qkvb_pp,
            "vb_row": vb_row,
            "cosb": cos32,
            "sinm": sinm,
        })
    return maps


def _bfc(a):
    return np.ascontiguousarray(a).astype(ml_dtypes.bfloat16)


def _mk_tmpl(core):
    c0 = 4096 - 2048 * (core % 2)
    p = np.arange(128)[:, None]
    c = np.arange(TMPLW)[None, :]
    return (c < p + c0).astype(ml_dtypes.bfloat16)


def kernel(**inputs):
    x = np.asarray(inputs["L0_attn_x"], dtype=np.float32)
    shards = _shard_xT(x)
    r0w = np.ascontiguousarray(np.asarray(inputs["L0_ffn_router_w"], np.float32))
    r0b = np.ascontiguousarray(np.asarray(inputs["L0_ffn_router_b"], np.float32).reshape(1, NEXP))

    nc1 = build_L1()
    maps1 = [{"xT": shards[c], "r0w": r0w, "r0b": r0b} for c in range(8)]
    r1 = run_bass_kernel_spmd(nc1, maps1, core_ids=list(range(8))).results
    logits0 = np.concatenate([r["logits"] for r in r1], axis=0)
    lnx0 = [np.asarray(r["lnx"]) for r in r1]

    nc2 = build_L2()
    maps2 = prep_L2_maps(inputs, logits0)
    for c in range(8):
        maps2[c]["xT"] = shards[c]
        maps2[c]["lnx0"] = lnx0[c].view(ml_dtypes.bfloat16) if lnx0[c].dtype != ml_dtypes.bfloat16 else lnx0[c]
    r2 = run_bass_kernel_spmd(nc2, maps2, core_ids=list(range(8))).results
    xT1 = [np.asarray(r["xT1"]) for r in r2]
    q0 = [np.asarray(r["q0"]) for r in r2]
    k0 = [np.asarray(r["k0"]) for r in r2]
    v0 = [np.asarray(r["v0"]) for r in r2]

    nc3 = build_L3()
    maps3 = []
    for c in range(8):
        b = c // 2
        maps3.append({
            "xT1": xT1[c], "q0": q0[c],
            "k_seq": np.ascontiguousarray(np.concatenate([k0[2 * b], k0[2 * b + 1]], axis=1)),
            "v_seq": np.ascontiguousarray(np.concatenate([v0[2 * b], v0[2 * b + 1]], axis=0)),
            "tmpl": _mk_tmpl(c), "r0w": r0w, "r0b": r0b,
        })
    r3 = run_bass_kernel_spmd(nc3, maps3, core_ids=list(range(8))).results
    xT2 = [np.asarray(r["xT2"]) for r in r3]
    logits2 = np.concatenate([r["logits"] for r in r3], axis=0)
    lnx2 = [np.asarray(r["lnx"]) for r in r3]

    nc4 = build_L4()
    e0b = np.ascontiguousarray(np.asarray(inputs["L0_ffn_expert_b"], np.float32).reshape(1, D))
    maps4 = []
    for c in range(8):
        maps4.append({
            "xT2": xT2[c], "lnx2": lnx2[c],
            "lgt_full": _rot_logits(logits2, c),
            "e0w": np.asarray(inputs["L0_ffn_expert_w"], np.float32), "e0b": e0b,
            "r1w": np.ascontiguousarray(np.asarray(inputs["L1_ffn_w_router"], np.float32)),
            "we1": np.asarray(inputs["L1_ffn_we1"], np.float32),
            "we3": np.asarray(inputs["L1_ffn_we3"], np.float32),
            "we2": np.asarray(inputs["L1_ffn_we2"], np.float32),
            "ws1": np.asarray(inputs["L1_ffn_ws1"], np.float32),
            "ws3": np.asarray(inputs["L1_ffn_ws3"], np.float32),
            "ws2": np.asarray(inputs["L1_ffn_ws2"], np.float32),
            "wqT": np.ascontiguousarray(np.asarray(inputs["L1_attn_wq"], np.float32).T),
            "wkT": np.ascontiguousarray(np.asarray(inputs["L1_attn_wk"], np.float32).T),
            "wvT": np.ascontiguousarray(np.asarray(inputs["L1_attn_wv"], np.float32).T),
        })
    r4 = run_bass_kernel_spmd(nc4, maps4, core_ids=list(range(8))).results
    xT4 = [np.asarray(r["xT4"]) for r in r4]
    q1 = [np.asarray(r["q1"]) for r in r4]
    k1 = [np.asarray(r["k1"]) for r in r4]
    v1 = [np.asarray(r["v1"]) for r in r4]

    nc5 = build_L5()
    maps5 = []
    for c in range(8):
        b = c // 2
        maps5.append({
            "xT4": xT4[c], "q1": q1[c],
            "k_seq": np.ascontiguousarray(np.concatenate([k1[2 * b], k1[2 * b + 1]], axis=1)),
            "v_seq": np.ascontiguousarray(np.concatenate([v1[2 * b], v1[2 * b + 1]], axis=0)),
            "tmpl": _mk_tmpl(c),
            "woT": np.ascontiguousarray(np.asarray(inputs["L1_attn_wo"], np.float32).T),
            "r1w": np.ascontiguousarray(np.asarray(inputs["L1_ffn_w_router"], np.float32)),
            "we1": np.asarray(inputs["L1_ffn_we1"], np.float32),
            "we3": np.asarray(inputs["L1_ffn_we3"], np.float32),
            "we2": np.asarray(inputs["L1_ffn_we2"], np.float32),
            "ws1": np.asarray(inputs["L1_ffn_ws1"], np.float32),
            "ws3": np.asarray(inputs["L1_ffn_ws3"], np.float32),
            "ws2": np.asarray(inputs["L1_ffn_ws2"], np.float32),
        })
    r5 = run_bass_kernel_spmd(nc5, maps5, core_ids=list(range(8))).results
    out = np.concatenate([np.asarray(r["xT6"]) for r in r5], axis=1)
    return np.ascontiguousarray(out.T).reshape(4, 4096, D)


# version 06
# speedup vs baseline: 1.1628x; 1.1628x over previous
import math
from contextlib import ExitStack
import numpy as np
import ml_dtypes
import concourse.bass as bass
import concourse.mybir as mybir
from concourse.bass_utils import run_bass_kernel_spmd

F32 = mybir.dt.float32
BF16 = mybir.dt.bfloat16
AF = mybir.ActivationFunctionType
ALU = mybir.AluOpType
AX = mybir.AxisListType

D = 2048
TOK = 2048
NCH = 16
NG = 4
TG = 512
NH = 16
HD = 128
SEQ = 4096
NST = SEQ // 128
NEXP = 16
EC_K = 1024
LN_EPS = 1e-5
ATT_SCALE = 1.0 / math.sqrt(HD)
MASKVAL = -30000.0
TMPLW = 6144
CONST_C = 4096

ENGS = ("sync", "gpsimd", "vector", "scalar", "tensor")
ESEM = {"sync": "sSY", "gpsimd": "sG", "vector": "sV", "scalar": "sC", "tensor": "sT"}
DSEM = {"sync": "sDS", "gpsimd": "sDG", "scalar": "sDC"}
AUTO_SERIAL = {"vector", "scalar", "gpsimd"}


class Plan:

    def __init__(self, nc):
        self.nc = nc
        self.ops = []
        self.tot = {}
        self._sems = {}
        for s in ESEM.values():
            self.sem(s)

    def sem(self, name):
        if name not in self._sems:
            self._sems[name] = self.nc.semaphore(name).__enter__()
            self.tot[name] = 0
        return self._sems[name]

    def _norm(self, waits):
        w = []
        flat = []
        def _fl(x):
            if x is None:
                return
            if isinstance(x, list):
                for y in x:
                    _fl(y)
            else:
                flat.append(x)
        for t in waits:
            _fl(t)
        for t in flat:
            if t is None:
                continue
            s, v = t
            if s.startswith("d_"):
                v = self.tot[s]
            w.append((s, v))
        return w

    def op(self, eng, fn, waits=(), inc=True):
        w = self._norm(waits)
        sem = ESEM[eng]
        if eng in AUTO_SERIAL and self.tot[sem] > 0:
            w.append((sem, self.tot[sem]))
        if inc:
            self.tot[sem] += 1
            self.ops.append((eng, fn, w, (sem, 1), "op"))
            return (sem, self.tot[sem])
        self.ops.append((eng, fn, w, None, "op"))
        return None

    def dma(self, eng, fn, waits=(), sem="misc"):
        w = self._norm(waits)
        sem = "d_" + sem
        self.sem(sem)
        self.tot[sem] += 16
        self.ops.append((eng, fn, w, (sem, 16), "dma"))
        return (sem, self.tot[sem])

    def now(self, sem):
        return (sem, self.tot[sem])

    def fence(self):
        snap = [(s, v) for s, v in self.tot.items() if v > 0]
        for e in ENGS:
            self.ops.append((e, None, list(snap), None, "op"))

    def emit(self, block):
        waited = {e: {} for e in ENGS}

        def mk(engname):
            def section(eng):
                for (e, fn, waits, inc, kind) in self.ops:
                    if e != engname:
                        continue
                    byname = {}
                    for (s, v) in waits:
                        byname[s] = max(byname.get(s, -1), v)
                    pending = []
                    for s, v in byname.items():
                        if waited[e].get(s, -1) < v:
                            pending.append((s, v))
                            waited[e][s] = v
                    if fn is None:
                        for (s, v) in pending:
                            eng.wait_ge(self._sems[s], v)
                        continue
                    if kind == "op":
                        for (s, v) in pending[:-1]:
                            eng.wait_ge(self._sems[s], v)
                        ins = fn(eng)
                        if pending:
                            s, v = pending[-1]
                            ins._wait_ge(self._sems[s], v)
                    else:
                        for (s, v) in pending:
                            eng.wait_ge(self._sems[s], v)
                        ins = fn(eng)
                    if inc is not None:
                        ins.then_inc(self._sems[inc[0]], inc[1])
            return section

        for name in ENGS:
            if any(o[0] == name for o in self.ops):
                getattr(block, name)(mk(name))


def setup_consts(nc, P, ctx):
    c = {}
    c["ones1_f"] = ctx.enter_context(nc.sbuf_tensor("ones1_f", [1, 128], F32))
    c["ones1_b"] = ctx.enter_context(nc.sbuf_tensor("ones1_b", [1, 128], BF16))
    c["ones128_f"] = ctx.enter_context(nc.sbuf_tensor("ones128_f", [128, 1], F32))
    c["ones128_b"] = ctx.enter_context(nc.sbuf_tensor("ones128_b", [128, 1], BF16))
    c["eps"] = ctx.enter_context(nc.sbuf_tensor("eps_c", [128, 1], F32))
    P.op("vector", lambda v: v.memset(c["eps"][:], LN_EPS))
    P.op("vector", lambda v: v.memset(c["ones1_f"][:], 1.0))
    P.op("vector", lambda v: v.memset(c["ones1_b"][:], 1.0))
    P.op("vector", lambda v: v.memset(c["ones128_f"][:], 1.0))
    c["ready"] = P.op("vector", lambda v: v.memset(c["ones128_b"][:], 1.0))
    return c


def emit_ln_group(nc, P, C, xg, sq2, lng, rows, psA, psB, psS, psQ, t_x):
    ones128 = C["ones128_f"]
    ones1 = C["ones1_f"]
    for i in range(NCH):
        P.op("tensor", lambda t, i=i: t.matmul(psS[0:1, :], ones128[:], xg[:, i, :],
                                               start=(i == 0), stop=(i == NCH - 1)),
             waits=[t_x, C["ready"], rows.get("_prev_ln")] if i == 0 else (), inc=(i == NCH - 1))
    t_mmS = P.now("sT")
    t_qmm = [None, None]
    for i in range(NCH):
        sq = sq2[i % 2]
        t_sq = P.op("scalar", lambda s, i=i, sq=sq: s.activation(sq[:], xg[:, i, :], AF.Square),
                    waits=[t_x, t_qmm[i % 2]])
        t_qmm[i % 2] = P.op("tensor", lambda t, i=i, sq=sq: t.matmul(
            psQ[0:1, :], ones128[:], sq[:], start=(i == 0), stop=(i == NCH - 1)),
            waits=[t_sq])
    t_mmQ = t_qmm[(NCH - 1) % 2]
    P.op("vector", lambda v: v.tensor_scalar_mul(rows["mrow"][:], psS[0:1, :], 1.0 / D), waits=[t_mmS])
    P.op("vector", lambda v: v.tensor_scalar_mul(rows["var"][:], psQ[0:1, :], 1.0 / D), waits=[t_mmQ])
    P.op("vector", lambda v: v.tensor_tensor(rows["brow"][:], rows["mrow"][:], rows["mrow"][:], op=ALU.mult))
    t_var = P.op("vector", lambda v: v.tensor_sub(rows["var"][:], rows["var"][:], rows["brow"][:]))
    P.op("scalar", lambda s: s.activation(rows["rrow"][:], rows["var"][:], AF.Ln, bias=C["eps"][0:1, :]),
         waits=[t_var])
    t_rstd = P.op("scalar", lambda s: s.activation(rows["rrow"][:], rows["rrow"][:], AF.Exp, scale=-0.5))
    t_brow = P.op("vector", lambda v: v.scalar_tensor_tensor(rows["brow"][:], rows["mrow"][:], -1.0,
                                                             rows["rrow"][:], op0=ALU.mult, op1=ALU.mult),
                  waits=[t_rstd])
    P.op("tensor", lambda t: t.matmul(psA[:], ones1[:], rows["rrow"][:], start=True, stop=True),
         waits=[t_brow], inc=False)
    t_bc = P.op("tensor", lambda t: t.matmul(psB[:], ones1[:], rows["brow"][:], start=True, stop=True))
    P.op("vector", lambda v: v.tensor_tensor(lng[:], xg[:],
                                             psA[:].unsqueeze(1).to_broadcast([128, NCH, TG]),
                                             op=ALU.mult), waits=[t_bc])
    t_ln = P.op("vector", lambda v: v.tensor_tensor(lng[:], lng[:],
                                                    psB[:].unsqueeze(1).to_broadcast([128, NCH, TG]),
                                                    op=ALU.add))
    rows["_prev_ln"] = t_ln
    return t_ln


def ln_rows(nc, ctx, prefix=""):
    return {k: ctx.enter_context(nc.sbuf_tensor(prefix + k, [1, TG], F32))
            for k in ("mrow", "var", "rrow", "brow")}


def fm_view(hbm_ap, g):
    return hbm_ap.rearrange("(c p) t -> p c t", p=128)[:, :, g * TG:(g + 1) * TG]


def build_L1():
    nc = bass.Bass("TRN2", target_bir_lowering=False)
    xT = nc.dram_tensor("xT", [D, TOK], F32, kind="ExternalInput").ap()
    r0w = nc.dram_tensor("r0w", [D, NEXP], F32, kind="ExternalInput").ap()
    r0b = nc.dram_tensor("r0b", [1, NEXP], F32, kind="ExternalInput").ap()
    logits = nc.dram_tensor("logits", [TOK, NEXP], F32, kind="ExternalOutput").ap()
    lnx = nc.dram_tensor("lnx", [D, TOK], BF16, kind="ExternalOutput").ap()

    with ExitStack() as ctx:
        E = ctx.enter_context
        P = Plan(nc)
        xg = E(nc.sbuf_tensor("xg", [128, NCH, TG], F32))
        nsq = xg[:].rearrange("p c t -> p (c t)")[:, 0:SEQ]
        sq2 = [E(nc.sbuf_tensor(f"sq{i}", [128, TG], F32)) for i in range(2)]
        rb128 = sq2[0]
        lng = E(nc.sbuf_tensor("lng", [128, NCH, TG], BF16))
        rows = ln_rows(nc, ctx)
        srw = E(nc.sbuf_tensor("srw", [128, NCH, NEXP], BF16))
        srb = E(nc.sbuf_tensor("srb", [1, NEXP], BF16))
        slog = E(nc.sbuf_tensor("slog", [128, TG // 128, NEXP], F32))
        psS = E(nc.psum_tensor("psS", [128, TG], F32))
        psQ = E(nc.psum_tensor("psQ", [128, TG], F32))
        psA = E(nc.psum_tensor("psA", [128, TG], F32))
        psB = E(nc.psum_tensor("psB", [128, TG], F32))
        psL = E(nc.psum_tensor("psL", [128, NEXP], F32))
        block = E(nc.Block())

        C = setup_consts(nc, P, ctx)
        t_rw = P.dma("gpsimd", lambda g: g.dma_start(
            out=srw[:], in_=r0w.rearrange("(c p) e -> p c e", p=128)), sem="cst")
        t_rb = P.dma("gpsimd", lambda g: g.dma_start(out=srb[:], in_=r0b), sem="cst")

        for g in range(NG):
            t_x = P.dma("sync", lambda s, g=g: s.dma_start(out=xg[:], in_=fm_view(xT, g)), sem="xg")
            t_ln = emit_ln_group(nc, P, C, xg, sq2, lng, rows, psA, psB, psS, psQ, t_x)
            t_cp = None
            for tt in range(TG // 128):
                for i in range(NCH):
                    P.op("tensor", lambda t, i=i, tt=tt: t.matmul(
                        psL[:], lng[:, i, tt * 128:(tt + 1) * 128], srw[:, i, :],
                        start=(i == 0), stop=False),
                        waits=([t_ln, t_rw, t_rb, t_cp] if i == 0 else ()), inc=False)
                t_mm = P.op("tensor", lambda t: t.matmul(
                    psL[:], C["ones1_b"][:], srb[:], start=False, stop=True))
                t_cp = P.op("vector", lambda v, tt=tt: v.tensor_copy(slog[:, tt, :], psL[:]),
                            waits=[t_mm])
            t_st1 = P.dma("sync", lambda s, g=g: s.dma_start(out=fm_view(lnx, g), in_=lng[:]),
                          waits=[t_ln], sem="lng")
            t_st2 = P.dma("sync", lambda s, g=g: s.dma_start(
                out=logits.rearrange("(g tt p) e -> p (g tt) e", p=128, g=NG)[:, g * 4:(g + 1) * 4, :],
                in_=slog[:]), waits=[t_cp], sem="slog")
            P.fence()

        P.op("sync", lambda s: s.nop(), inc=False)
        P.emit(block)
    return nc


def emit_routing0(nc, P, C, ctx, E, logits_hbm, bufs):
    lgt3 = bufs["lgt"][:].rearrange("p (j e) -> p j e", j=128, e=NEXP)
    probs3 = bufs["probs"][:].rearrange("p (j e) -> p j e", j=128, e=NEXP)
    scr3 = bufs["scr"][:, 0:2048].rearrange("p (j e) -> p j e", j=128, e=NEXP)
    mneg = bufs["mneg"]
    ssum = bufs["ssum"]
    if logits_hbm.ndim == 3:
        t_lg = P.dma("sync", lambda s: s.dma_start(
            out=bufs["lgt"][:].rearrange("p (j e) -> p j e", e=NEXP), in_=logits_hbm), sem="lgt")
    else:
        t_lg = P.dma("sync", lambda s: s.dma_start(out=bufs["lgt"][:], in_=logits_hbm), sem="lgt")
    P.op("vector", lambda v: v.reduce_max(mneg[:], lgt3, axis=AX.X, negate=True), waits=[t_lg])
    t1 = P.op("vector", lambda v: v.tensor_tensor(
        scr3, lgt3, mneg[:].unsqueeze(2).to_broadcast([128, 128, NEXP]), op=ALU.add))
    t2 = P.op("scalar", lambda s: s.activation(bufs["probs"][:], bufs["scr"][:, 0:2048], AF.Exp), waits=[t1])
    P.op("vector", lambda v: v.reduce_sum(ssum[:], probs3, axis=AX.X), waits=[t2])
    P.op("vector", lambda v: v.reciprocal(ssum[:], ssum[:]))
    t_pr = P.op("vector", lambda v: v.tensor_tensor(
        probs3, probs3, ssum[:].unsqueeze(2).to_broadcast([128, 128, NEXP]), op=ALU.mult))
    lo, hi, mid, cond, cnt, dd = (bufs[k] for k in ("lo", "hi", "mid", "cond", "cnt", "dd"))
    P.op("vector", lambda v: v.memset(lo[:], 0.0))
    t_init = P.op("vector", lambda v: v.memset(hi[:], 1.0))
    for it in range(45):
        P.op("vector", lambda v: v.tensor_add(mid[:], lo[:], hi[:]))
        t_mid = P.op("vector", lambda v: v.tensor_scalar_mul(mid[:], mid[:], 0.5))
        P.op("vector", lambda v: v.tensor_tensor(
            scr3, probs3, mid[:].unsqueeze(1).to_broadcast([128, 128, NEXP]), op=ALU.is_ge),
            waits=[t_pr, t_mid])
        t_cnt = P.op("vector", lambda v: v.reduce_sum(cnt[:], scr3.transpose([0, 2, 1]), axis=AX.X))
        t_mm = P.op("tensor", lambda t: t.matmul(bufs["psC"][0:1, 0:NEXP], C["ones128_f"][:], cnt[:],
                                                 start=True, stop=True),
                    waits=[t_cnt, bufs.get("t_row_prev"), bufs.get("t_cond_prev")])
        t_row = P.op("vector", lambda v: v.tensor_copy(bufs["row16"][:], bufs["psC"][0:1, 0:NEXP]), waits=[t_mm])
        bufs["t_row_prev"] = t_row
        t_bc = P.op("tensor", lambda t: t.matmul(bufs["psC2"][:, 0:NEXP], C["ones1_f"][:], bufs["row16"][:],
                                                 start=True, stop=True), waits=[t_row])
        t_cond = P.op("vector", lambda v: v.tensor_scalar(cond[:], bufs["psC2"][:, 0:NEXP], float(EC_K), None,
                                                 op0=ALU.is_gt), waits=[t_bc])
        bufs["t_cond_prev"] = t_cond
        P.op("vector", lambda v: v.tensor_sub(dd[:], mid[:], lo[:]))
        P.op("vector", lambda v: v.tensor_mul(dd[:], cond[:], dd[:]))
        P.op("vector", lambda v: v.tensor_add(lo[:], lo[:], dd[:]))
        P.op("vector", lambda v: v.tensor_sub(dd[:], hi[:], mid[:]))
        P.op("vector", lambda v: v.tensor_mul(dd[:], cond[:], dd[:]))
        P.op("vector", lambda v: v.tensor_add(hi[:], mid[:], dd[:]))
    P.op("vector", lambda v: v.tensor_tensor(
        scr3, probs3, hi[:].unsqueeze(1).to_broadcast([128, 128, NEXP]), op=ALU.is_ge))
    P.op("vector", lambda v: v.tensor_tensor(scr3, probs3, scr3, op=ALU.mult))
    if "mask8" in bufs:
        P.op("vector", lambda v: v.reduce_sum(bufs["ssum"][:], scr3, axis=AX.X))
        a3 = bufs["ssum"][:].rearrange("p (r j) -> p r j", r=8)
        P.op("vector", lambda v: v.tensor_tensor(
            bufs["mneg"][:].rearrange("p (r j) -> p r j", r=8), a3,
            bufs["mask8"][:].unsqueeze(2).to_broadcast([128, 8, NEXP]), op=ALU.mult))
        P.op("vector", lambda v: v.reduce_sum(
            bufs["a05"][:], bufs["mneg"][:].rearrange("p (r j) -> p r j", r=8).transpose([0, 2, 1]),
            axis=AX.X))
    else:
        P.op("vector", lambda v: v.reduce_sum(bufs["a05"][:], scr3[:, 0:NCH, :], axis=AX.X))
    t_a = P.op("vector", lambda v: v.tensor_scalar_mul(bufs["a05"][:], bufs["a05"][:], 0.5))
    return t_a


def emit_expert_apply(nc, P, C, lnsrc, xsrc, xdst, ew_sb, eb_row, a05, bufs, t_ready, t_w,
                      post_group=None):
    t_prev_st = None
    for g in range(NG):
        t_lnld = P.dma("sync", lambda s, g=g: s.dma_start(out=bufs["lng"][:], in_=fm_view(lnsrc, g)),
                       waits=[t_prev_st, P.now("sT")], sem="lng")
        t_xld = P.dma("sync", lambda s, g=g: s.dma_start(out=bufs["xg"][:], in_=fm_view(xsrc, g)),
                      waits=[t_prev_st], sem="xg")
        t_add = None
        for tt4 in range(4):
            tt = 4 * g + tt4
            for og in range(4):
                pb = (tt4 * 4 + og) % 2
                psE = bufs["psE"][pb]
                ytm = bufs["ytm"][pb]
                for i in range(NCH):
                    P.op("tensor", lambda t, i=i, tt4=tt4, og=og, psE=psE: t.matmul(
                        psE[:], bufs["lng"][:, i, tt4 * 128:(tt4 + 1) * 128],
                        ew_sb[:, i, og * 512:(og + 1) * 512],
                        start=(i == 0), stop=False),
                        waits=([t_lnld, bufs["t_evac"][pb]] + t_w) if i == 0 else (), inc=False)
                t_mm = P.op("tensor", lambda t, og=og, psE=psE: t.matmul(
                    psE[:], C["ones1_b"][:], eb_row[0:1, og * 512:(og + 1) * 512],
                    start=False, stop=True))
                t_ev = P.op("vector", lambda v, psE=psE, ytm=ytm, tt=tt: v.tensor_scalar(
                    ytm[:], psE[:], a05[:, tt:tt + 1], None, op0=ALU.mult),
                    waits=[t_mm, bufs["t_tr"][pb], t_ready])
                bufs["t_evac"][pb] = t_ev
                t_trs = []
                for k in range(4):
                    t_tr = P.op("tensor", lambda t, k=k, ytm=ytm: t.transpose(
                        bufs["psT"][:, k * 128:(k + 1) * 128], ytm[:, k * 128:(k + 1) * 128],
                        C["identB"][:]),
                        waits=[t_ev, t_add] if k == 0 else ())
                    t_trs.append(t_tr)
                bufs["t_tr"][pb] = t_trs[-1]
                for k in range(4):
                    t_add = P.op("vector", lambda v, k=k, og=og, tt4=tt4: v.tensor_add(
                        bufs["xg"][:, og * 4 + k, tt4 * 128:(tt4 + 1) * 128],
                        bufs["xg"][:, og * 4 + k, tt4 * 128:(tt4 + 1) * 128],
                        bufs["psT"][:, k * 128:(k + 1) * 128]),
                        waits=[t_trs[-1], t_xld])
        t_prev_st = P.dma("sync", lambda s, g=g: s.dma_start(out=fm_view(xdst, g), in_=bufs["xg"][:]),
                          waits=[t_add], sem="xg")
        if post_group is not None:
            t_prev_st = [t_prev_st, post_group(g, t_add)]
    return t_prev_st


def emit_proj_head_fm(nc, P, C, lnall, wT_hbm, o0_w, head_out_hbm, nheads, bias_pp, scale,
                      rope, cosb, sinm, bufs, t_lnrdy, bias_col0=0, w_scr=None):
    wT6 = wT_hbm.rearrange("(c p) o -> p c o", p=128)
    t_h = [None, None]
    t_ld = [None, None]
    for h in range(nheads):
        sb = h % 2
        strip = bufs["strips"][sb]
        hh = bufs["hh"][sb]
        if w_scr is None:
            t_ld[sb] = P.dma("gpsimd", lambda g, h=h, strip=strip: g.dma_start(
                out=strip[:], in_=wT6[:, :, o0_w + h * 128: o0_w + (h + 1) * 128]),
                waits=[t_h[sb]], sem=f"strip{sb}")
        else:
            t_ld[sb] = P.dma("sync", lambda s, h=h, sb=sb: s.dma_start(
                out=bufs["strips2d"][sb], in_=w_scr[0][o0_w // 128 + h]),
                waits=[t_h[sb], w_scr[1]], sem=f"strip{sb}")
        t_ev = None
        for g in range(NG):
            pb = g % 2
            psP = bufs["psP"][pb]
            for i in range(NCH):
                P.op("tensor", lambda t, i=i, g=g, strip=strip, psP=psP: t.matmul(
                    psP[:], strip[:, i, :], lnall[:, i, g * TG:(g + 1) * TG],
                    start=(i == 0), stop=(i == NCH - 1)),
                    waits=[t_ld[sb], t_lnrdy, bufs["t_pev"][pb], t_h[sb]] if i == 0 else (),
                    inc=(i == NCH - 1))
            t_mm = (ESEM["tensor"], P.tot[ESEM["tensor"]])
            if bias_pp is not None:
                t_ev = P.op("scalar", lambda s, g=g, hh=hh, psP=psP, h=h: s.activation(
                    hh[:, g * TG:(g + 1) * TG], psP[:], AF.Identity,
                    bias=bias_pp[:, bias_col0 + h:bias_col0 + h + 1], scale=scale), waits=[t_mm])
            else:
                if scale != 1.0:
                    t_ev = P.op("scalar", lambda s, g=g, hh=hh, psP=psP: s.activation(
                        hh[:, g * TG:(g + 1) * TG], psP[:], AF.Identity, scale=scale), waits=[t_mm])
                else:
                    t_ev = P.op("scalar", lambda s, g=g, hh=hh, psP=psP: s.copy(
                        hh[:, g * TG:(g + 1) * TG], psP[:]), waits=[t_mm])
            bufs["t_pev"][pb] = t_ev
        if rope:
            t_s1 = P.dma("gpsimd", lambda g, hh=hh: g.dma_start(out=bufs["swp"][0:16, :], in_=hh[16:32, :]),
                         waits=[t_ev, bufs.get("t_swp_rd")], sem="swp")
            t_s2 = P.dma("gpsimd", lambda g, hh=hh: g.dma_start(out=bufs["swp"][16:32, :], in_=hh[0:16, :]),
                         sem="swp")
            t_r1 = P.op("vector", lambda v, hh=hh: v.tensor_tensor(
                bufs["rsc"][:], hh[0:32, :], cosb[:], op=ALU.mult), waits=[t_ev])
            t_r2 = P.op("vector", lambda v: v.tensor_tensor(
                bufs["swp"][:], bufs["swp"][:], sinm[:], op=ALU.mult), waits=[t_s1, t_s2])
            t_ev = P.op("vector", lambda v, hh=hh: v.tensor_add(
                hh[0:32, :], bufs["rsc"][:], bufs["swp"][:]), waits=[t_r1, t_r2])
            bufs["t_swp_rd"] = t_ev
        t_h[sb] = P.dma("sync", lambda s, h=h, hh=hh: s.dma_start(
            out=head_out_hbm[h * 128:(h + 1) * 128, :], in_=hh[:]), waits=[t_ev], sem=f"hh{sb}")
    return t_h


def emit_proj_v_tm(nc, P, C, lnall, wT_hbm, o0_w, v_out_hbm, bias_row, bufs, t_lnrdy):
    wT6 = wT_hbm.rearrange("(c p) o -> p c o", p=128)
    t_st = None
    t_ld = None
    t_vst = [None, None]
    for vog in range(4):
        t_ld = P.dma("gpsimd", lambda g, vog=vog: g.dma_start(
            out=bufs["vstrip"][:], in_=wT6[:, :, o0_w + vog * 512: o0_w + (vog + 1) * 512]),
            waits=[t_st, P.now("sT")], sem="vstrip")
        for tt in range(NTT):
            pb = tt % 2
            psP = bufs["psP"][pb]
            vtm = bufs["vtm"][pb]
            for i in range(NCH):
                P.op("tensor", lambda t, i=i, tt=tt, psP=psP: t.matmul(
                    psP[:], lnall[:, i, tt * 128:(tt + 1) * 128], bufs["vstrip"][:, i, :],
                    start=(i == 0), stop=(bias_row is None and i == NCH - 1)),
                    waits=[t_ld, t_lnrdy, bufs["t_vev"][pb]] if i == 0 else (),
                    inc=(bias_row is None and i == NCH - 1))
            if bias_row is not None:
                P.op("tensor", lambda t, vog=vog, psP=psP: t.matmul(
                    psP[:], C["ones1_b"][:], bias_row[0:1, vog * 512:(vog + 1) * 512],
                    start=False, stop=True))
            t_mm = (ESEM["tensor"], P.tot[ESEM["tensor"]])
            t_ev = P.op("vector", lambda v, psP=psP, vtm=vtm: v.tensor_copy(vtm[:], psP[:]),
                        waits=[t_mm, t_vst[pb]])
            bufs["t_vev"][pb] = t_ev
            t_st = P.dma("sync", lambda s, tt=tt, vog=vog, vtm=vtm: s.dma_start(
                out=v_out_hbm[tt * 128:(tt + 1) * 128, vog * 512:(vog + 1) * 512], in_=vtm[:]),
                waits=[t_ev], sem=f"vtm{pb}")
            t_vst[pb] = t_st
    return t_st


NTT = 16


def build_L2():
    nc = bass.Bass("TRN2", target_bir_lowering=False)
    xT = nc.dram_tensor("xT", [D, TOK], F32, kind="ExternalInput").ap()
    lnx0 = nc.dram_tensor("lnx0", [D, TOK], BF16, kind="ExternalInput").ap()
    lgt_full = nc.dram_tensor("lgt_full", [128, 2048], F32, kind="ExternalInput").ap()
    e0w = nc.dram_tensor("e0w", [D, D], F32, kind="ExternalInput").ap()
    e0b = nc.dram_tensor("e0b", [1, D], F32, kind="ExternalInput").ap()
    WqkvT = nc.dram_tensor("WqkvT", [D, 3 * D], F32, kind="ExternalInput").ap()
    qkvb_pp = nc.dram_tensor("qkvb_pp", [128, 48], F32, kind="ExternalInput").ap()
    vb_row = nc.dram_tensor("vb_row", [1, D], F32, kind="ExternalInput").ap()
    cosb = nc.dram_tensor("cosb", [32, TOK], F32, kind="ExternalInput").ap()
    sinm = nc.dram_tensor("sinm", [32, TOK], F32, kind="ExternalInput").ap()
    xT1 = nc.dram_tensor("xT1", [D, TOK], F32, kind="ExternalOutput").ap()
    q0 = nc.dram_tensor("q0", [D, TOK], BF16, kind="ExternalOutput").ap()
    k0 = nc.dram_tensor("k0", [D, TOK], BF16, kind="ExternalOutput").ap()
    v0 = nc.dram_tensor("v0", [TOK, D], BF16, kind="ExternalOutput").ap()

    with ExitStack() as ctx:
        E = ctx.enter_context
        P = Plan(nc)
        U8 = E(nc.sbuf_tensor("U8", [128, NCH * TOK], BF16))
        xg = E(nc.sbuf_tensor("xg", [128, NCH, TG], F32))
        scr_r = E(nc.sbuf_tensor("scr_r", [128, 2048], F32))
        sq2 = [E(nc.sbuf_tensor(f"sq{i}", [128, TG], F32)) for i in range(2)]
        lng = E(nc.sbuf_tensor("lng2", [128, NCH, TG], BF16))
        lgt = E(nc.sbuf_tensor("lgt", [128, 2048], F32))
        probs = E(nc.sbuf_tensor("probs", [128, 2048], F32))
        mneg = E(nc.sbuf_tensor("mneg", [128, 128], F32))
        ssum = E(nc.sbuf_tensor("ssum", [128, 128], F32))
        small16 = {k: E(nc.sbuf_tensor("r_" + k, [128, NEXP], F32))
                   for k in ("lo", "hi", "mid", "cond", "cnt", "dd", "a05")}
        row16 = E(nc.sbuf_tensor("row16", [1, NEXP], F32))
        eb_row = E(nc.sbuf_tensor("eb_row", [1, D], BF16))
        qkvb_sb = E(nc.sbuf_tensor("qkvb_sb", [128, 48], F32))
        qkvbs = E(nc.sbuf_tensor("qkvbs", [128, 48], F32))
        vb_row_sb = E(nc.sbuf_tensor("vb_row_sb", [1, D], BF16))
        cos_sb = E(nc.sbuf_tensor("cos_sb", [32, TOK], F32))
        sinm_sb = E(nc.sbuf_tensor("sinm_sb", [32, TOK], F32))
        identB = E(nc.sbuf_tensor("identB", [128, 128], BF16))
        ytm = [E(nc.sbuf_tensor(f"ytm{i}", [128, 512], BF16)) for i in range(2)]
        strips = [E(nc.sbuf_tensor(f"strip{i}", [128, NCH, 128], BF16)) for i in range(2)]
        hh = [E(nc.sbuf_tensor(f"hh{i}", [128, TOK], BF16)) for i in range(2)]
        swp = E(nc.sbuf_tensor("swp", [32, TOK], BF16))
        rsc = E(nc.sbuf_tensor("rsc", [32, TOK], BF16))
        vtm = [E(nc.sbuf_tensor(f"vtm{i}", [128, 512], BF16)) for i in range(2)]
        rows = ln_rows(nc, ctx)
        psE = [E(nc.psum_tensor(f"psE{i}", [128, 512], F32)) for i in range(2)]
        psT = E(nc.psum_tensor("psT", [128, 512], BF16))
        psS = E(nc.psum_tensor("psS", [128, TG], F32))
        psQ = E(nc.psum_tensor("psQ", [128, TG], F32))
        psA = E(nc.psum_tensor("psA", [128, TG], F32))
        psB = E(nc.psum_tensor("psB", [128, TG], F32))
        block = E(nc.Block())

        C = setup_consts(nc, P, ctx)
        C["identB"] = identB
        P.op("gpsimd", lambda g: g.memset(identB[:], 0.0))
        t_id = P.op("gpsimd", lambda g: g.affine_select(
            out=identB[:], in_=identB[:], compare_op=ALU.not_equal, fill=1.0,
            base=0, pattern=[[-1, 128]], channel_multiplier=1))

        e0w_sb = U8[:].rearrange("p (c o) -> p c o", c=NCH)
        t_ew = P.dma("gpsimd", lambda g: g.dma_start(
            out=e0w_sb, in_=e0w.rearrange("(c p) o -> p c o", p=128)), sem="e0w")
        t_eb = P.dma("gpsimd", lambda g: g.dma_start(out=eb_row[:], in_=e0b), sem="cstg")
        t_qb = P.dma("sync", lambda s: s.dma_start(out=qkvb_sb[:], in_=qkvb_pp), sem="cst")
        t_qbr = P.dma("gpsimd", lambda g: g.dma_start(out=vb_row_sb[:], in_=vb_row), sem="cstg")
        t_cos = P.dma("sync", lambda s: s.dma_start(out=cos_sb[:], in_=cosb), sem="cst")
        t_sin = P.dma("sync", lambda s: s.dma_start(out=sinm_sb[:], in_=sinm), sem="cst")
        t_qbs = P.op("vector", lambda v: v.tensor_scalar_mul(qkvbs[:], qkvb_sb[:], ATT_SCALE),
                     waits=[t_qb])

        rbufs = dict(lgt=lgt, probs=probs, scr=scr_r, mneg=mneg, ssum=ssum, row16=row16, **small16)
        rbufs["psC"] = psS
        rbufs["psC2"] = psA
        t_a05 = emit_routing0(nc, P, C, ctx, E, lgt_full, rbufs)
        P.fence()

        ebufs = dict(lng=lng, xg=xg, ytm=ytm, psE=psE, psT=psT, identB=identB,
                     t_evac=[None, None], t_tr=[None, None])
        emit_expert_apply(nc, P, C, lnx0, xT, xT1, e0w_sb, eb_row, small16["a05"][:],
                          ebufs, t_a05, [t_ew, t_eb])
        P.fence()

        lnall = U8[:].rearrange("p (c t) -> p c t", c=NCH)
        t_lnrdy = None
        for g in range(NG):
            t_x = P.dma("sync", lambda s, g=g: s.dma_start(out=xg[:], in_=fm_view(xT1, g)))
            t_lnrdy = emit_ln_group(nc, P, C, xg, sq2, lnall[:, :, g * TG:(g + 1) * TG],
                                    rows, psA, psB, psS, psQ, t_x)
            P.fence()

        vstrip = lng
        pbufs = dict(strips=strips, hh=hh, swp=swp, rsc=rsc, psP=psE,
                     t_pev=[None, None], vstrip=vstrip, vtm=vtm, t_vev=[None, None])
        emit_proj_head_fm(nc, P, C, lnall, WqkvT, 0, q0, NH, qkvb_sb, 1.0,
                          True, cos_sb[:], sinm_sb[:], pbufs, t_lnrdy)
        P.fence()
        emit_proj_head_fm(nc, P, C, lnall, WqkvT, D, k0, NH, qkvbs, ATT_SCALE,
                          True, cos_sb[:], sinm_sb[:], pbufs, t_lnrdy, bias_col0=16)
        P.fence()
        emit_proj_v_tm(nc, P, C, lnall, WqkvT, 2 * D, v0, vb_row_sb[:], pbufs, t_lnrdy)
        P.fence()
        P.op("sync", lambda s: s.nop(), inc=False)
        P.emit(block)
    return nc


def emit_attention(nc, P, C, bufs, k_seq, v_seq, q_own, attn_all, use_eps,
                   k_src=None, v_src=None):
    t_cons = [None, None]
    t_att = None
    for h in range(NH):
        hb = h % 2
        kh, vh, qh = bufs["kh"][hb], bufs["vh"][hb], bufs["qh"][hb]
        if k_src is None:
            k_in = k_seq[h * 128:(h + 1) * 128, :]
            k_out = kh[:]
        else:
            k_in = k_src(h)
            k_out = kh[:].rearrange("p (r t) -> p r t", r=2)
        t_kl = P.dma("sync", lambda s, kh=kh, k_in=k_in, k_out=k_out: s.dma_start(
            out=k_out, in_=k_in), waits=[t_cons[hb]], sem=f"kh{hb}")
        if v_src is None:
            v_in = v_seq.rearrange("(st p) d -> p st d", p=128)[:, :, h * 128:(h + 1) * 128]
            v_out = vh[:].rearrange("p (st d) -> p st d", d=128)
            t_vl = P.dma("sync", lambda s, v_in=v_in, v_out=v_out: s.dma_start(
                out=v_out, in_=v_in), waits=[t_cons[hb]], sem=f"vh{hb}")
        else:
            for r in range(2):
                v_in = v_src(h)[:, r, :, :]
                v_out = vh[:].rearrange("p (st d) -> p st d", d=128)[:, r * 16:(r + 1) * 16, :]
                t_vl = P.dma("sync", lambda s, v_in=v_in, v_out=v_out: s.dma_start(
                    out=v_out, in_=v_in), waits=[t_cons[hb]], sem=f"vh{hb}")
        t_ql = P.dma("sync", lambda s, h=h, qh=qh: s.dma_start(
            out=qh[:], in_=q_own[h * 128:(h + 1) * 128, :]),
            waits=[t_att], sem="qh")
        t_ks = P.op("scalar", lambda s, kh=kh: s.activation(bufs["nsq"][:], kh[:], AF.Square),
                    waits=[t_kl, bufs.get("t_nsq_rd")])
        t_km = None
        t_pd = bufs.get("t_psD_rd")
        for j in range(8):
            t_nm = P.op("tensor", lambda t, j=j: t.matmul(
                bufs["psD"][0:1, :], C["ones128_f"][:], bufs["nsq"][:, j * 512:(j + 1) * 512],
                start=True, stop=True), waits=[t_ks, t_pd])
            t_dc = P.op("vector", lambda v: v.tensor_copy(bufs["drow"][:], bufs["psD"][0:1, :]),
                        waits=[t_nm])
            t_pd = t_dc
            t_mx = P.op("vector", lambda v: v.reduce_max(bufs["bmx"][:], bufs["drow"][:], axis=AX.X))
            if j == 0:
                t_km = P.op("vector", lambda v: v.tensor_copy(bufs["km"][:], bufs["bmx"][:]))
            else:
                t_km = P.op("vector", lambda v: v.tensor_max(bufs["km"][:], bufs["km"][:], bufs["bmx"][:]))
        t_km = P.op("scalar", lambda s: s.activation(bufs["km"][:], bufs["km"][:], AF.Sqrt),
                    waits=[t_km])
        t_qs = P.op("scalar", lambda s, qh=qh: s.activation(
            bufs["nsq"][:, 0:TOK], qh[:], AF.Square), waits=[t_ql, t_km])
        bufs["t_nsq_rd"] = None
        bufs["t_psD_rd"] = t_pd
        for g in range(NG):
            t_nm = P.op("tensor", lambda t, g=g: t.matmul(
                bufs["psD"][0:1, :], C["ones128_f"][:], bufs["nsq"][:, g * 512:(g + 1) * 512],
                start=True, stop=True), waits=[t_qs, bufs["t_psD_rd"]])
            t_dc = P.op("vector", lambda v: v.tensor_copy(bufs["drow"][:], bufs["psD"][0:1, :]),
                        waits=[t_nm])
            bufs["t_psD_rd"] = t_dc
            P.op("vector", lambda v: v.reduce_max(bufs["bmx"][:], bufs["drow"][:], axis=AX.X))
            t_bs = P.op("scalar", lambda s: s.activation(bufs["bmx"][:], bufs["bmx"][:], AF.Sqrt),
                        waits=[P.now("sV")])
            t_nb = P.op("vector", lambda v: v.tensor_scalar(
                bufs["bmx"][:], bufs["bmx"][:], bufs["km"][:], -1.001, op0=ALU.mult, op1=ALU.mult),
                waits=[t_bs])
            t_nbb = P.op("tensor", lambda t: t.matmul(
                bufs["psBr"][:, 0:1], C["ones1_f"][:], bufs["bmx"][:], start=True, stop=True),
                waits=[t_nb, bufs.get("t_nb_rd")])
            t_nb128 = P.op("vector", lambda v: v.tensor_copy(bufs["negbt"][:], bufs["psBr"][:, 0:1]),
                           waits=[t_nbb])
            t_exp = None
            t_exp = None
            t_avp = None
            for st in range(NST):
                sb = st % 2
                psS = bufs["psSc"][sb]
                pex = bufs["pexp"][st % 3]
                P.op("tensor", lambda t, st=st, g=g, psS=psS, kh=kh, qh=qh: t.matmul(
                    psS[:], kh[:, st * 128:(st + 1) * 128], qh[:, g * TG:(g + 1) * TG],
                    start=True, stop=False, skip_group_check=True),
                    waits=[bufs["t_exp_done"][sb], t_ql], inc=False)
                t_sc = P.op("tensor", lambda t, st=st, g=g, psS=psS: t.matmul(
                    psS[:], C["negI"][:], bufs["tmpl"][:, CONST_C + TG * g - 128 * st:
                                                       CONST_C + TG * g - 128 * st + TG],
                    start=False, stop=True, skip_group_check=True))
                t_exp = P.op("scalar", lambda s, psS=psS, pex=pex, st=st: s.activation(
                    pex[:], psS[:], AF.Exp, bias=bufs["negbt"][:]),
                    waits=[t_sc, t_nb128, bufs["t_pex_rd"][st % 3]])
                bufs["t_exp_done"][sb] = t_exp
                t_av = P.op("tensor", lambda t, st=st, vh=vh, pex=pex: t.matmul(
                    bufs["psO"][:], vh[:].rearrange("p (st d) -> p st d", d=128)[:, st, :], pex[:],
                    start=(st == 0), stop=(st == NST - 1), skip_group_check=True),
                    waits=[t_exp, t_vl] + ([bufs.get("t_psO_rd")] if st == 0 else []), inc=False)
                t_dn = P.op("tensor", lambda t, st=st, pex=pex: t.matmul(
                    bufs["psD"][0:1, :], C["ones128_b"][:], pex[:],
                    start=(st == 0), stop=(st == NST - 1), skip_group_check=True),
                    waits=([bufs.get("t_psD_rd")] if st == 0 else ()), inc=(st == NST - 1))
                bufs["t_pex_rd"][st % 3] = t_dn
            t_dcp = P.op("vector", lambda v: v.tensor_copy(bufs["drow"][:], bufs["psD"][0:1, :]),
                         waits=[P.now("sT")])
            bufs["t_psD_rd"] = t_dcp
            if use_eps:
                P.op("vector", lambda v: v.tensor_scalar_add(bufs["drow"][:], bufs["drow"][:], 1e-8))
            t_ri = P.op("vector", lambda v: v.reciprocal(bufs["drow"][:], bufs["drow"][:]))
            t_rb = P.op("tensor", lambda t: t.matmul(
                bufs["psBr"][:], C["ones1_f"][:], bufs["drow"][:], start=True, stop=True),
                waits=[t_ri, t_nb128])
            t_rb128 = P.op("vector", lambda v: v.tensor_copy(bufs["rb128"][:], bufs["psBr"][:]),
                           waits=[t_rb])
            bufs["t_nb_rd"] = t_rb128
            t_att = P.op("vector", lambda v, h=h, g=g: v.tensor_tensor(
                attn_all[:, h, g * TG:(g + 1) * TG], bufs["psO"][:], bufs["rb128"][:], op=ALU.mult),
                waits=[t_rb128])
            bufs["t_psO_rd"] = t_att
            t_cons[hb] = t_att
    return t_att


def build_L3():
    nc = bass.Bass("TRN2", target_bir_lowering=False)
    xT1 = nc.dram_tensor("xT1", [D, TOK], F32, kind="ExternalInput").ap()
    q0 = nc.dram_tensor("q0", [D, TOK], BF16, kind="ExternalInput").ap()
    k_seq = nc.dram_tensor("k_seq", [D, SEQ], BF16, kind="ExternalInput").ap()
    v_seq = nc.dram_tensor("v_seq", [SEQ, D], BF16, kind="ExternalInput").ap()
    tmpl = nc.dram_tensor("tmpl", [128, TMPLW], BF16, kind="ExternalInput").ap()
    r0w = nc.dram_tensor("r0w", [D, NEXP], F32, kind="ExternalInput").ap()
    r0b = nc.dram_tensor("r0b", [1, NEXP], F32, kind="ExternalInput").ap()
    xT2 = nc.dram_tensor("xT2", [D, TOK], F32, kind="ExternalOutput").ap()
    logits = nc.dram_tensor("logits", [TOK, NEXP], F32, kind="ExternalOutput").ap()
    lnx = nc.dram_tensor("lnx", [D, TOK], BF16, kind="ExternalOutput").ap()

    with ExitStack() as ctx:
        E = ctx.enter_context
        P = Plan(nc)
        attn_all = E(nc.sbuf_tensor("attn_all", [128, NH, TOK], BF16))
        kh = [E(nc.sbuf_tensor(f"kh{i}", [128, SEQ], BF16)) for i in range(2)]
        vh = [E(nc.sbuf_tensor(f"vh{i}", [128, SEQ], BF16)) for i in range(2)]
        _qh = E(nc.sbuf_tensor("qh", [128, TOK], BF16))
        qh = [_qh, _qh]
        km = E(nc.sbuf_tensor("km", [1, 1], F32))
        bmx = E(nc.sbuf_tensor("bmx", [1, 1], F32))
        negbt = E(nc.sbuf_tensor("negbt", [128, 1], F32))
        drow = E(nc.sbuf_tensor("drow", [1, TG], F32))
        pexp = [E(nc.sbuf_tensor(f"pexp{i}", [128, TG], BF16)) for i in range(3)]
        tmpl_sb = E(nc.sbuf_tensor("tmpl_sb", [128, TMPLW], BF16))
        negI = E(nc.sbuf_tensor("negI", [128, 128], BF16))
        xg = E(nc.sbuf_tensor("xg", [128, NCH, TG], F32))
        nsq = xg[:].rearrange("p c t -> p (c t)")[:, 0:SEQ]
        sq2 = [E(nc.sbuf_tensor(f"sq{i}", [128, TG], F32)) for i in range(2)]
        rb128 = sq2[0]
        lng = E(nc.sbuf_tensor("lng", [128, NCH, TG], BF16))
        rows = ln_rows(nc, ctx)
        srw = E(nc.sbuf_tensor("srw", [128, NCH, NEXP], BF16))
        srb = E(nc.sbuf_tensor("srb", [1, NEXP], BF16))
        slog = E(nc.sbuf_tensor("slog", [128, TG // 128, NEXP], F32))
        psSc = [E(nc.psum_tensor(f"psSc{i}", [128, TG], F32)) for i in range(2)]
        psO = E(nc.psum_tensor("psO", [128, TG], F32))
        psD = E(nc.psum_tensor("psD", [128, TG], F32))
        psBr = E(nc.psum_tensor("psBr", [128, TG], F32))
        psL = E(nc.psum_tensor("psL", [128, NEXP], F32))
        block = E(nc.Block())

        C = setup_consts(nc, P, ctx)
        C["negI"] = negI
        P.op("gpsimd", lambda g: g.memset(negI[:], 0.0))
        P.op("gpsimd", lambda g: g.affine_select(
            out=negI[:], in_=negI[:], compare_op=ALU.not_equal, fill=MASKVAL,
            base=0, pattern=[[-1, 128]], channel_multiplier=1))
        P.dma("sync", lambda s: s.dma_start(out=tmpl_sb[:], in_=tmpl), sem="cst")
        P.dma("gpsimd", lambda g: g.dma_start(
            out=srw[:], in_=r0w.rearrange("(c p) e -> p c e", p=128)), sem="cstg")
        P.dma("gpsimd", lambda g: g.dma_start(out=srb[:], in_=r0b), sem="cstg")
        P.fence()

        abufs = dict(kh=kh, vh=vh, qh=qh, nsq=nsq, km=km,
                     bmx=bmx, negbt=negbt, drow=drow, rb128=rb128, pexp=pexp,
                     tmpl=tmpl_sb, psSc=psSc, psO=psO, psD=psD, psBr=psBr,
                     t_exp_done=[None, None], t_pex_rd=[None, None, None])
        emit_attention(nc, P, C, abufs, k_seq, v_seq, q0, attn_all[:], use_eps=False)
        P.fence()

        for g in range(NG):
            t_x = P.dma("sync", lambda s, g=g: s.dma_start(out=xg[:], in_=fm_view(xT1, g)), sem="xg")
            t_res = P.op("vector", lambda v, g=g: v.tensor_tensor(
                xg[:], xg[:], attn_all[:][:, :, g * TG:(g + 1) * TG],
                op=ALU.add), waits=[t_x])
            t_st0 = P.dma("sync", lambda s, g=g: s.dma_start(out=fm_view(xT2, g), in_=xg[:]),
                          waits=[t_res], sem="xg")
            t_ln = emit_ln_group(nc, P, C, xg, sq2, lng, rows, psBr, psD, psSc[0], psSc[1], t_res)
            t_cp = None
            for tt in range(TG // 128):
                for i in range(NCH):
                    P.op("tensor", lambda t, i=i, tt=tt: t.matmul(
                        psL[:], lng[:, i, tt * 128:(tt + 1) * 128], srw[:, i, :],
                        start=(i == 0), stop=False),
                        waits=([t_ln, t_cp] if i == 0 else ()), inc=False)
                t_mm = P.op("tensor", lambda t: t.matmul(
                    psL[:], C["ones1_b"][:], srb[:], start=False, stop=True))
                t_cp = P.op("vector", lambda v, tt=tt: v.tensor_copy(slog[:, tt, :], psL[:]),
                            waits=[t_mm])
            t_st1 = P.dma("sync", lambda s, g=g: s.dma_start(out=fm_view(lnx, g), in_=lng[:]),
                          waits=[t_ln], sem="lng")
            t_st2 = P.dma("sync", lambda s, g=g: s.dma_start(
                out=logits.rearrange("(g tt p) e -> p (g tt) e", p=128, g=NG)[:, g * 4:(g + 1) * 4, :],
                in_=slog[:]), waits=[t_cp], sem="slog")
            P.fence()
        P.op("sync", lambda s: s.nop(), inc=False)
        P.emit(block)
    return nc




def emit_router1_top4(nc, P, C, lnx3g, r1w_sb, cw_sb, bufs, g, t_ln):
    t_cp = bufs.get("t_r1_last")
    for tt4 in range(4):
        tt = 4 * g + tt4
        for i in range(NCH):
            P.op("tensor", lambda t, i=i, tt4=tt4: t.matmul(
                bufs["psL"][:], lnx3g[:, i, tt4 * 128:(tt4 + 1) * 128], r1w_sb[:, i, :],
                start=(i == 0), stop=(i == NCH - 1)),
                waits=([t_ln, t_cp, bufs["t_r1w"]] if i == 0 else ()), inc=(i == NCH - 1))
        t_mm = P.now("sT")
        t_ng = P.op("vector", lambda v: v.reduce_max(bufs["negm"][:], bufs["psL"][:], axis=AX.X,
                                                     negate=True), waits=[t_mm])
        t_ex = P.op("scalar", lambda s: s.activation(bufs["probs16"][:], bufs["psL"][:], AF.Exp,
                                                     bias=bufs["negm"][:]), waits=[t_ng])
        P.op("vector", lambda v: v.max(out=bufs["m8"][:], in_=bufs["probs16"][:]), waits=[t_ex])
        P.op("vector", lambda v: v.tensor_scalar(bufs["sel16"][:], bufs["probs16"][:],
                                                 bufs["m8"][:, 3:4], None, op0=ALU.is_ge))
        P.op("vector", lambda v: v.tensor_mul(bufs["sel16"][:], bufs["sel16"][:], bufs["probs16"][:]))
        P.op("vector", lambda v: v.reduce_sum(bufs["negm"][:], bufs["sel16"][:], axis=AX.X))
        P.op("vector", lambda v: v.reciprocal(bufs["negm"][:], bufs["negm"][:]))
        t_cp = P.op("vector", lambda v, tt=tt: v.tensor_scalar(
            cw_sb[:, tt, :], bufs["sel16"][:], bufs["negm"][:], None, op0=ALU.mult))
    bufs["t_r1_last"] = t_cp
    return t_cp


def emit_ffn1_dense(nc, P, C, B, lnx_scr, x_scr, x_out,
                    we1, we3, we2, ws1, ws3, ws2, t_cwt, cwT, cw_row, r1bufs):
    lnx3g_v = B["lnx3g_v"]
    we2s = B["we2s"]
    sh_all = B["sh_all"]
    acc = B["acc"]
    XA = B["XA"]
    westr = B["westr"]
    ws2cs = B["ws2cs"]
    g_e = B["g_e"]
    ssb = B["ssb"]
    xch = B["xch"]
    ps = B["ps"]
    scr = B["scr"]
    westr2d = B["westr2d"]
    ws2cs2d = B["ws2cs2d"]
    t_acc_rd = None
    t_combine = None
    we1r = we1.rearrange("e (c p) h -> e p c h", p=128)
    we3r = we3.rearrange("e (c p) h -> e p c h", p=128)
    we2r = we2.rearrange("e (j p) o -> e p j o", p=128)
    ws1r = ws1.rearrange("(c p) h -> p c h", p=128)
    ws3r = ws3.rearrange("(c p) h -> p c h", p=128)
    ws2r = ws2.rearrange("(hc p) o -> p hc o", p=128)
    t_w1 = [None, None]
    t_w3 = [None, None]
    t_w2 = None
    t_ws2 = [None, None]
    t_lnx3_ld = None
    t_cwrow_rd = None
    t_psBr_rd = None
    t_ps0_rd = None
    t_ps1_rd = None
    t_moe_rd = [None, None]
    t_sh = None
    t_ssb_rd = None
    t_wread = [None, None]
    t_we2s_read = None
    t_ws2_rd = [None, None]
    scr = B["scr"]
    westr2d = B["westr2d"]
    ws2cs2d = B["ws2cs2d"]
    for g in range(NG):
        t_lnx3_ld = P.dma("sync", lambda s, g=g: s.dma_start(
            out=lnx3g_v[:], in_=fm_view(lnx_scr, g)),
            waits=[t_combine if g > 0 else P.now("sT"), P.now("sV")], sem="lnx3g")
        t_accz = P.op("vector", lambda v: v.memset(XA[:], 0.0),
                      waits=[t_combine, t_acc_rd])
        for e in range(NEXP):
            t_cwr = P.dma("gpsimd", lambda gg, e=e: gg.dma_start(
                out=cw_row, in_=cwT[e:e + 1, :]), waits=[t_cwrow_rd, t_cwt], sem="cwrow")
            t_br = P.op("tensor", lambda t, g=g: t.matmul(
                ps[4][:], C["ones1_f"][:], cw_row[0:1, g * TG:(g + 1) * TG],
                start=True, stop=True), waits=[t_cwr, t_psBr_rd])
            t_cwrow_rd = t_br
            t_w2_prev = t_w2
            t_w2 = P.dma("gpsimd", lambda gg, e=e: gg.dma_start(
                out=we2s[:, :, :], in_=scr["we2b"][0][e].rearrange("(j p) o -> p j o", p=128)),
                waits=[t_w2_prev, t_we2s_read, scr["we2b"][1]], sem="we2s")
            t_ge = None
            for j in range(8):
                jb = j % 2
                w1 = westr[0][jb]
                w3 = westr[1][jb]
                t_w1[jb] = P.dma("sync", lambda s, e=e, j=j, jb=jb: s.dma_start(
                    out=westr2d[0][jb], in_=scr[f"we1_{e}"][0][j]),
                    waits=[t_w1[jb], t_wread[jb], scr[f"we1_{e}"][1]], sem=f"w1s{jb}")
                t_w3[jb] = P.dma("sync", lambda s, e=e, j=j, jb=jb: s.dma_start(
                    out=westr2d[1][jb], in_=scr[f"we3_{e}"][0][j]),
                    waits=[t_w3[jb], t_wread[jb], scr[f"we3_{e}"][1]], sem=f"w3s{jb}")
                for i in range(NCH):
                    P.op("tensor", lambda t, i=i, w1=w1: t.matmul(
                        ps[0][:], w1[:, i, :], lnx3g_v[:, i, :],
                        start=(i == 0), stop=(i == NCH - 1)),
                        waits=([t_w1[jb], t_lnx3_ld, t_ps0_rd] if i == 0 else ()),
                        inc=(i == NCH - 1))
                t_p1 = P.now("sT")
                for i in range(NCH):
                    P.op("tensor", lambda t, i=i, w3=w3: t.matmul(
                        ps[1][:], w3[:, i, :], lnx3g_v[:, i, :],
                        start=(i == 0), stop=(i == NCH - 1)),
                        waits=([t_w3[jb], t_ps1_rd] if i == 0 else ()), inc=(i == NCH - 1))
                t_p3 = P.now("sT")
                t_wread[jb] = t_p3
                t_si = P.op("scalar", lambda s: s.activation(ssb[:], ps[0][:], AF.Silu),
                            waits=[t_p1, t_ssb_rd])
                t_ps0_rd = t_si
                t_gm = P.op("vector", lambda v, j=j: v.tensor_tensor(
                    g_e[:, j, :], ssb[:], ps[1][:], op=ALU.mult),
                    waits=[t_si, t_p3] + ([P.now("sT")] if j == 0 else []))
                t_ps1_rd = t_gm
                t_ssb_rd = t_gm
                t_ge = P.op("vector", lambda v, j=j: v.tensor_tensor(
                    g_e[:, j, :], g_e[:, j, :], ps[4][:], op=ALU.mult),
                    waits=[t_gm, t_br])
            t_psBr_rd = t_ge
            for oc in range(NCH):
                pb = 2 + (oc % 2)
                for j in range(8):
                    P.op("tensor", lambda t, j=j, oc=oc, pb=pb: t.matmul(
                        ps[pb][:], we2s[:, j, oc * 128:(oc + 1) * 128], g_e[:, j, :],
                        start=(j == 0), stop=(j == 7)),
                        waits=([t_w2, t_ge, t_moe_rd[pb - 2]] if j == 0 else ()),
                        inc=(j == 7))
                t_mo = P.now("sT")
                t_we2s_read = t_mo
                t_acc_rd = P.op("vector", lambda v, oc=oc, pb=pb: v.tensor_add(
                    acc[:, oc, :], acc[:, oc, :], ps[pb][:]), waits=[t_mo, t_accz])
                t_moe_rd[pb - 2] = t_acc_rd
        for hc in range(32):
            jb = hc % 2
            w1 = westr[0][jb]
            w3 = westr[1][jb]
            t_w1[jb] = P.dma("sync", lambda s, hc=hc, jb=jb: s.dma_start(
                out=westr2d[0][jb], in_=scr["ws1"][0][hc]),
                waits=[t_w1[jb], t_wread[jb], scr["ws1"][1]], sem=f"w1s{jb}")
            t_w3[jb] = P.dma("sync", lambda s, hc=hc, jb=jb: s.dma_start(
                out=westr2d[1][jb], in_=scr["ws3"][0][hc]),
                waits=[t_w3[jb], t_wread[jb], scr["ws3"][1]], sem=f"w3s{jb}")
            for i in range(NCH):
                P.op("tensor", lambda t, i=i, w1=w1: t.matmul(
                    ps[0][:], w1[:, i, :], lnx3g_v[:, i, :],
                    start=(i == 0), stop=(i == NCH - 1)),
                    waits=([t_w1[jb], t_ps0_rd] if i == 0 else ()), inc=(i == NCH - 1))
            t_p1 = P.now("sT")
            for i in range(NCH):
                P.op("tensor", lambda t, i=i, w3=w3: t.matmul(
                    ps[1][:], w3[:, i, :], lnx3g_v[:, i, :],
                    start=(i == 0), stop=(i == NCH - 1)),
                    waits=([t_w3[jb], t_ps1_rd] if i == 0 else ()), inc=(i == NCH - 1))
            t_p3 = P.now("sT")
            t_wread[jb] = t_p3
            t_si = P.op("scalar", lambda s: s.activation(ssb[:], ps[0][:], AF.Silu),
                        waits=[t_p1, t_ssb_rd])
            t_ps0_rd = t_si
            t_sh = P.op("vector", lambda v, hc=hc: v.tensor_tensor(
                sh_all[:, hc, :], ssb[:], ps[1][:], op=ALU.mult), waits=[t_si, t_p3])
            t_ps1_rd = t_sh
            t_ssb_rd = t_sh
        t_xc = [None, None]
        for oc in range(NCH):
            cb = oc % 2
            pb = 2 + (oc % 2)
            t_ws2[cb] = P.dma("sync", lambda s, oc=oc, cb=cb: s.dma_start(
                out=ws2cs2d[cb], in_=scr["ws2"][0][oc]),
                waits=[t_ws2[cb], t_ws2_rd[cb], scr["ws2"][1]], sem=f"ws2cs{cb}")
            t_xc[cb] = P.dma("sync", lambda s, oc=oc, cb=cb, g=g: s.dma_start(
                out=xch[cb][:], in_=x_scr.rearrange("(c p) t -> p c t", p=128)
                [:, oc, g * TG:(g + 1) * TG]), waits=[t_xc[cb]], sem=f"xch{cb}")
            for hc in range(32):
                P.op("tensor", lambda t, hc=hc, cb=cb, pb=pb: t.matmul(
                    ps[pb][:], ws2cs[cb][:, hc, :], sh_all[:, hc, :],
                    start=(hc == 0), stop=(hc == 31)),
                    waits=([t_ws2[cb], t_sh, t_moe_rd[pb - 2]] if hc == 0 else ()),
                    inc=(hc == 31))
            t_sm = P.now("sT")
            t_ws2_rd[cb] = t_sm
            t_u = P.op("vector", lambda v, oc=oc, cb=cb: v.scalar_tensor_tensor(
                xch[cb][:], acc[:, oc, :], 0.4, xch[cb][:], op0=ALU.mult, op1=ALU.add),
                waits=[t_xc[cb], t_sm])
            t_combine = P.op("vector", lambda v, oc=oc, cb=cb, pb=pb: v.scalar_tensor_tensor(
                xch[cb][:], ps[pb][:], 0.1, xch[cb][:], op0=ALU.mult, op1=ALU.add))
            t_moe_rd[pb - 2] = t_combine
            t_xc[cb] = P.dma("sync", lambda s, oc=oc, cb=cb, g=g: s.dma_start(
                out=fm_view(x_out, g)[:, oc, :], in_=xch[cb][:]),
                waits=[t_combine], sem=f"xch{cb}")

    return t_combine


def build_L4():
    nc = bass.Bass("TRN2", target_bir_lowering=False)
    xT2 = nc.dram_tensor("xT2", [D, TOK], F32, kind="ExternalInput").ap()
    lnx2 = nc.dram_tensor("lnx2", [D, TOK], BF16, kind="ExternalInput").ap()
    lgt_full = nc.dram_tensor("lgt_full", [128, 2048], F32, kind="ExternalInput").ap()
    e0w = nc.dram_tensor("e0w", [D, D], F32, kind="ExternalInput").ap()
    e0b = nc.dram_tensor("e0b", [1, D], F32, kind="ExternalInput").ap()
    r1w = nc.dram_tensor("r1w", [D, NEXP], F32, kind="ExternalInput").ap()
    we1 = nc.dram_tensor("we1", [NEXP, D, 1024], F32, kind="ExternalInput").ap()
    we3 = nc.dram_tensor("we3", [NEXP, D, 1024], F32, kind="ExternalInput").ap()
    we2 = nc.dram_tensor("we2", [NEXP, 1024, D], F32, kind="ExternalInput").ap()
    ws1 = nc.dram_tensor("ws1", [D, 4096], F32, kind="ExternalInput").ap()
    ws3 = nc.dram_tensor("ws3", [D, 4096], F32, kind="ExternalInput").ap()
    ws2 = nc.dram_tensor("ws2", [4096, D], F32, kind="ExternalInput").ap()
    wqT = nc.dram_tensor("wqT", [D, D], F32, kind="ExternalInput").ap()
    wkT = nc.dram_tensor("wkT", [D, D], F32, kind="ExternalInput").ap()
    wvT = nc.dram_tensor("wvT", [D, D], F32, kind="ExternalInput").ap()
    xT4 = nc.dram_tensor("xT4", [D, TOK], F32, kind="ExternalOutput").ap()
    q1 = nc.dram_tensor("q1", [D, TOK], BF16, kind="ExternalOutput").ap()
    k1 = nc.dram_tensor("k1", [D, TOK], BF16, kind="ExternalOutput").ap()
    v1 = nc.dram_tensor("v1", [TOK, D], BF16, kind="ExternalOutput").ap()

    x3_scr = nc.dram_tensor("x3_scr", [D, TOK], F32, kind="Internal").ap()
    lnx3_scr = nc.dram_tensor("lnx3_scr", [D, TOK], BF16, kind="Internal").ap()

    with ExitStack() as ctx:
        E = ctx.enter_context
        P = Plan(nc)
        U8a = E(nc.sbuf_tensor("U8a", [128, NCH * TOK], BF16))
        UB = E(nc.sbuf_tensor("UB", [128, 8192], F32))
        XA = E(nc.sbuf_tensor("XA", [128, 8192], F32))
        westr = [[E(nc.sbuf_tensor(f"we{w}s{i}", [128, NCH, 128], BF16)) for i in range(2)]
                 for w in (1, 3)]
        ws2cs = [E(nc.sbuf_tensor(f"ws2cs{i}", [128, 32, 128], BF16)) for i in range(2)]
        g_e = E(nc.sbuf_tensor("g_e", [128, 8, TG], BF16))
        ssb = E(nc.sbuf_tensor("ssb", [128, TG], BF16))
        xch = [E(nc.sbuf_tensor(f"xch{i}", [128, TG], F32)) for i in range(2)]
        cw_sb = E(nc.sbuf_tensor("cw_sb", [128, NCH, NEXP], F32))
        r1w_sb = E(nc.sbuf_tensor("r1w_sb", [128, NCH, NEXP], BF16))
        probs16 = E(nc.sbuf_tensor("probs16", [128, NEXP], F32))
        sel16 = E(nc.sbuf_tensor("sel16", [128, NEXP], F32))
        m8 = E(nc.sbuf_tensor("m8", [128, 8], F32))
        negm = E(nc.sbuf_tensor("negm", [128, 1], F32))
        mneg = E(nc.sbuf_tensor("mneg", [128, 128], F32))
        ssum = E(nc.sbuf_tensor("ssum", [128, 128], F32))
        small16 = {k: E(nc.sbuf_tensor("r_" + k, [128, NEXP], F32))
                   for k in ("lo", "hi", "mid", "cond", "cnt", "dd", "a05")}
        row16 = E(nc.sbuf_tensor("row16", [1, NEXP], F32))
        eb_row = E(nc.sbuf_tensor("eb_row", [1, D], BF16))
        identB = E(nc.sbuf_tensor("identB", [128, 128], BF16))
        identF = E(nc.sbuf_tensor("identF", [128, 128], F32))
        ytm = [E(nc.sbuf_tensor(f"ytm{i}", [128, 512], BF16)) for i in range(2)]
        sq2 = [E(nc.sbuf_tensor(f"sq{i}", [128, TG], F32)) for i in range(2)]
        hh = [E(nc.sbuf_tensor(f"hh{i}", [128, TOK], BF16)) for i in range(2)]
        vtm = [E(nc.sbuf_tensor(f"vtm{i}", [128, 512], BF16)) for i in range(2)]
        rows = ln_rows(nc, ctx)
        ps = [E(nc.psum_tensor(f"ps{i}", [128, 512], F32)) for i in range(6)]
        psL = E(nc.psum_tensor("psL", [128, NEXP], F32))
        psT = E(nc.psum_tensor("psT", [128, 512], BF16))
        block = E(nc.Block())

        C = setup_consts(nc, P, ctx)
        C["identB"] = identB
        P.op("gpsimd", lambda g: g.memset(identB[:], 0.0))
        P.op("gpsimd", lambda g: g.affine_select(
            out=identB[:], in_=identB[:], compare_op=ALU.not_equal, fill=1.0,
            base=0, pattern=[[-1, 128]], channel_multiplier=1))
        P.op("gpsimd", lambda g: g.memset(identF[:], 0.0))
        P.op("gpsimd", lambda g: g.affine_select(
            out=identF[:], in_=identF[:], compare_op=ALU.not_equal, fill=1.0,
            base=0, pattern=[[-1, 128]], channel_multiplier=1))

        lgt_t = UB[:, 0:2048]
        probs_t = UB[:, 2048:4096]
        scr_t = UB[:, 4096:6144]
        rbufs = dict(lgt=lgt_t, probs=probs_t, scr=scr_t, mneg=mneg, ssum=ssum, row16=row16,
                     **small16)
        rbufs["psC"] = ps[0]
        rbufs["psC2"] = ps[1]
        t_a05 = emit_routing0(nc, P, C, ctx, E, lgt_full, rbufs)
        e0w_sb = U8a[:].rearrange("p (c o) -> p c o", c=NCH)
        t_ew = P.dma("gpsimd", lambda g: g.dma_start(
            out=e0w_sb, in_=e0w.rearrange("(c p) o -> p c o", p=128)), sem="e0w")
        t_eb = P.dma("gpsimd", lambda g: g.dma_start(out=eb_row[:], in_=e0b), sem="cstg")
        t_r1w = P.dma("gpsimd", lambda g: g.dma_start(
            out=r1w_sb[:], in_=r1w.rearrange("(c p) e -> p c e", p=128)), sem="cstg")
        P.fence()

        lng_v = UB[:, 0:4096].bitcast(BF16).rearrange("p (c t) -> p c t", c=NCH)
        lnx3g_v = UB[:, 0:4096].bitcast(BF16).rearrange("p (c t) -> p c t", c=NCH)
        r1bufs = dict(psL=psL, negm=negm, probs16=probs16, m8=m8, sel16=sel16, t_r1w=t_r1w)
        xg3 = XA[:].rearrange("p (c t) -> p c t", c=NCH)

        def post_group(g, t_add):
            t_ln = emit_ln_group(nc, P, C, xg3, sq2, lnx3g_v, rows,
                                 ps[4], ps[5], ps[2], ps[3], t_add)
            t_st = P.dma("sync", lambda s, g=g: s.dma_start(
                out=fm_view(lnx3_scr, g), in_=lnx3g_v[:]), waits=[t_ln], sem="lnx3")
            t_r1 = emit_router1_top4(nc, P, C, lnx3g_v, r1w_sb, cw_sb[:], r1bufs, g, t_ln)
            return [t_st, t_r1]

        ebufs = dict(lng=lng_v, xg=xg3, ytm=ytm, psE=[ps[0], ps[1]], psT=psT, identB=identB,
                     t_evac=[None, None], t_tr=[None, None])
        emit_expert_apply(nc, P, C, lnx2, xT2, x3_scr, e0w_sb, eb_row, small16["a05"][:],
                          ebufs, t_a05, [t_ew, t_eb], post_group=post_group)
        P.fence()

        cwT = UB[0:16, 4096:6144]
        cw_row = UB[0:1, 6144:8192]
        t_cwt = None
        t_cwt_prev = None
        for tt in range(NTT):
            t_tr = P.op("tensor", lambda t, tt=tt: t.transpose(
                ps[4][0:16, 0:128], cw_sb[:, tt, :], identF[:]),
                waits=[t_cwt_prev, r1bufs.get("t_r1_last")])
            t_cwt = P.op("vector", lambda v, tt=tt: v.tensor_copy(
                cwT[:, tt * 128:(tt + 1) * 128], ps[4][0:16, 0:128]), waits=[t_tr])
            t_cwt_prev = t_cwt
        P.fence()

        B = dict(lnx3g_v=lnx3g_v,
                 we2s=U8a[:, 0:16384].bitcast(BF16).rearrange("p (j o) -> p j o", j=8),
                 sh_all=U8a[:, 16384:32768].bitcast(BF16).rearrange("p (hc t) -> p hc t", hc=32),
                 acc=XA[:].rearrange("p (c t) -> p c t", c=NCH), XA=XA,
                 westr=westr, ws2cs=ws2cs, g_e=g_e, ssb=ssb, xch=xch, ps=ps, psL=psL)
        emit_ffn1_dense(nc, P, C, B, lnx3_scr, x3_scr, xT4,
                        we1, we3, we2, ws1, ws3, ws2, t_cwt, cwT, cw_row, r1bufs)
        lnall4 = U8a[:].rearrange("p (c t) -> p c t", c=NCH)
        t_lnrdy = None
        for g in range(NG):
            t_x = P.dma("sync", lambda s, g=g: s.dma_start(out=xg3[:], in_=fm_view(xT4, g)), sem="xg")
            t_lnrdy = emit_ln_group(nc, P, C, xg3, sq2, lnall4[:, :, g * TG:(g + 1) * TG],
                                    rows, ps[4], ps[5], ps[2], ps[3], t_x)
            P.fence()
        vstrip4 = UB[:, 0:4096].bitcast(BF16).rearrange("p (c t) -> p c t", c=NCH)
        pbufs = dict(strips=[westr[0][0], westr[0][1]], hh=hh, swp=None, rsc=None,
                     psP=[ps[0], ps[1]], t_pev=[None, None], vstrip=vstrip4,
                     vtm=vtm, t_vev=[None, None])
        emit_proj_head_fm(nc, P, C, lnall4, wqT, 0, q1, NH, None, 1.0, False, None, None,
                          pbufs, t_lnrdy)
        P.fence()
        emit_proj_head_fm(nc, P, C, lnall4, wkT, 0, k1, NH, None, ATT_SCALE, False, None, None,
                          pbufs, t_lnrdy)
        P.fence()
        emit_proj_v_tm(nc, P, C, lnall4, wvT, 0, v1, None, pbufs, t_lnrdy)
        P.fence()
        P.op("sync", lambda s: s.nop(), inc=False)
        P.emit(block)
    return nc


def build_L5():
    nc = bass.Bass("TRN2", target_bir_lowering=False)
    xT4 = nc.dram_tensor("xT4", [D, TOK], F32, kind="ExternalInput").ap()
    q1 = nc.dram_tensor("q1", [D, TOK], BF16, kind="ExternalInput").ap()
    k_seq = nc.dram_tensor("k_seq", [D, SEQ], BF16, kind="ExternalInput").ap()
    v_seq = nc.dram_tensor("v_seq", [SEQ, D], BF16, kind="ExternalInput").ap()
    tmpl = nc.dram_tensor("tmpl", [128, TMPLW], BF16, kind="ExternalInput").ap()
    woT = nc.dram_tensor("woT", [D, D], F32, kind="ExternalInput").ap()
    r1w = nc.dram_tensor("r1w", [D, NEXP], F32, kind="ExternalInput").ap()
    we1 = nc.dram_tensor("we1", [NEXP, D, 1024], F32, kind="ExternalInput").ap()
    we3 = nc.dram_tensor("we3", [NEXP, D, 1024], F32, kind="ExternalInput").ap()
    we2 = nc.dram_tensor("we2", [NEXP, 1024, D], F32, kind="ExternalInput").ap()
    ws1 = nc.dram_tensor("ws1", [D, 4096], F32, kind="ExternalInput").ap()
    ws3 = nc.dram_tensor("ws3", [D, 4096], F32, kind="ExternalInput").ap()
    ws2 = nc.dram_tensor("ws2", [4096, D], F32, kind="ExternalInput").ap()
    xT6 = nc.dram_tensor("xT6", [D, TOK], F32, kind="ExternalOutput").ap()

    x5_scr = nc.dram_tensor("x5_scr", [D, TOK], F32, kind="Internal").ap()
    lnx5_scr = nc.dram_tensor("lnx5_scr", [D, TOK], BF16, kind="Internal").ap()

    with ExitStack() as ctx:
        E = ctx.enter_context
        P = Plan(nc)
        U8a = E(nc.sbuf_tensor("U8a", [128, NCH * TOK], BF16))
        UB = E(nc.sbuf_tensor("UB", [128, 8192], F32))
        XA = E(nc.sbuf_tensor("XA", [128, 8192], F32))
        khb = [E(nc.sbuf_tensor(f"khb{i}", [128, SEQ], BF16)) for i in range(2)]
        vhb = [E(nc.sbuf_tensor(f"vhb{i}", [128, SEQ], BF16)) for i in range(2)]
        _qh = E(nc.sbuf_tensor("qh", [128, TOK], BF16))
        qh2 = [_qh, _qh]
        tmpl_sb = E(nc.sbuf_tensor("tmpl_sb", [128, TMPLW], BF16))
        pexp = [E(nc.sbuf_tensor(f"pexp{i}", [128, TG], BF16)) for i in range(3)]
        km = E(nc.sbuf_tensor("km", [1, 1], F32))
        bmx = E(nc.sbuf_tensor("bmx", [1, 1], F32))
        negbt = E(nc.sbuf_tensor("negbt", [128, 1], F32))
        drow = E(nc.sbuf_tensor("drow", [1, TG], F32))
        negI = E(nc.sbuf_tensor("negI", [128, 128], BF16))
        identF = E(nc.sbuf_tensor("identF", [128, 128], F32))
        ssb = E(nc.sbuf_tensor("ssb", [128, TG], BF16))
        xch = [E(nc.sbuf_tensor(f"xch{i}", [128, TG], F32)) for i in range(2)]
        cw_sb = E(nc.sbuf_tensor("cw_sb", [128, NCH, NEXP], F32))
        r1w_sb = E(nc.sbuf_tensor("r1w_sb", [128, NCH, NEXP], BF16))
        probs16 = E(nc.sbuf_tensor("probs16", [128, NEXP], F32))
        sel16 = E(nc.sbuf_tensor("sel16", [128, NEXP], F32))
        m8 = E(nc.sbuf_tensor("m8", [128, 8], F32))
        negm = E(nc.sbuf_tensor("negm", [128, 1], F32))
        sq2 = [E(nc.sbuf_tensor(f"sq{i}", [128, TG], F32)) for i in range(2)]
        rb128 = sq2[0]
        rows = ln_rows(nc, ctx)
        ps = [E(nc.psum_tensor(f"ps{i}", [128, 512], F32)) for i in range(6)]
        psL = E(nc.psum_tensor("psL", [128, NEXP], F32))
        block = E(nc.Block())

        C = setup_consts(nc, P, ctx)
        P.op("gpsimd", lambda g: g.memset(negI[:], 0.0))
        P.op("gpsimd", lambda g: g.affine_select(
            out=negI[:], in_=negI[:], compare_op=ALU.not_equal, fill=MASKVAL,
            base=0, pattern=[[-1, 128]], channel_multiplier=1))
        P.op("gpsimd", lambda g: g.memset(identF[:], 0.0))
        P.op("gpsimd", lambda g: g.affine_select(
            out=identF[:], in_=identF[:], compare_op=ALU.not_equal, fill=1.0,
            base=0, pattern=[[-1, 128]], channel_multiplier=1))
        C["negI"] = negI
        P.dma("sync", lambda s: s.dma_start(out=tmpl_sb[:], in_=tmpl), sem="cst")
        t_r1w = P.dma("gpsimd", lambda g: g.dma_start(
            out=r1w_sb[:], in_=r1w.rearrange("(c p) e -> p c e", p=128)), sem="cstg")
        P.fence()

        attn_all = U8a[:].rearrange("p (h t) -> p h t", h=NH)
        abufs = dict(kh=khb, vh=vhb, qh=qh2, nsq=XA[:].rearrange("p a -> p a")[:, 0:SEQ],
                     km=km, bmx=bmx, negbt=negbt, drow=drow, rb128=rb128, pexp=pexp,
                     tmpl=tmpl_sb, psSc=[ps[0], ps[1]], psO=ps[2], psD=ps[3], psBr=ps[4],
                     t_exp_done=[None, None], t_pex_rd=[None, None, None])
        emit_attention(nc, P, C, abufs, k_seq, v_seq, q1, attn_all, use_eps=True)
        P.fence()

        lnx5g_v = UB[:, 0:4096].bitcast(BF16).rearrange("p (c t) -> p c t", c=NCH)
        xg5 = XA[:].rearrange("p (c t) -> p c t", c=NCH)
        woT6 = woT.rearrange("(c p) o -> p c o", p=128)
        wstr = [khb[0][:, 0:2048].rearrange("p (c o) -> p c o", c=NCH),
                khb[1][:, 0:2048].rearrange("p (c o) -> p c o", c=NCH)]
        r1bufs = dict(psL=psL, negm=negm, probs16=probs16, m8=m8, sel16=sel16, t_r1w=t_r1w)
        t_wl = [None, None]
        t_pP_rd = [None, None]
        t_x5st = None
        for g in range(NG):
            t_resid = None
            for oc in range(NCH):
                cb = oc % 2
                t_wl[cb] = P.dma("gpsimd", lambda gg, oc=oc, cb=cb: gg.dma_start(
                    out=wstr[cb][:], in_=woT6[:, :, oc * 128:(oc + 1) * 128]),
                    waits=[t_wl[cb], P.now("sT")], sem=f"wstr{cb}")
                for h in range(NH):
                    P.op("tensor", lambda t, h=h, oc=oc, cb=cb, g=g: t.matmul(
                        ps[cb][:], wstr[cb][:, h, :], attn_all[:, h, g * TG:(g + 1) * TG],
                        start=(h == 0), stop=(h == NH - 1)),
                        waits=([t_wl[cb], t_pP_rd[cb]] if h == 0 else ()), inc=(h == NH - 1))
                t_mm = P.now("sT")
                t_xc = P.dma("sync", lambda s, oc=oc, cb=cb, g=g: s.dma_start(
                    out=xch[cb][:], in_=fm_view(xT4, g)[:, oc, :]),
                    waits=[P.now("sV")], sem=f"xch{cb}")
                t_resid = P.op("vector", lambda v, oc=oc, cb=cb: v.tensor_add(
                    xg5[:, oc, :], xch[cb][:], ps[cb][:]),
                    waits=[t_mm, t_xc] + ([P.now("sC"), P.now("sV")] if (g == 0 and oc == 0) else []))
                t_pP_rd[cb] = t_resid
            t_x5st = P.dma("sync", lambda s, g=g: s.dma_start(
                out=fm_view(x5_scr, g), in_=xg5[:]), waits=[t_resid], sem="x5")
            t_ln = emit_ln_group(nc, P, C, xg5, sq2, lnx5g_v, rows,
                                 ps[4], ps[5], ps[2], ps[3], t_resid)
            t_lnst = P.dma("sync", lambda s, g=g: s.dma_start(
                out=fm_view(lnx5_scr, g), in_=lnx5g_v[:]), waits=[t_ln], sem="lnx5")
            t_r1 = emit_router1_top4(nc, P, C, lnx5g_v, r1w_sb, cw_sb[:], r1bufs, g, t_ln)
            P.fence()

        cwT = UB[0:16, 4096:6144]
        cw_row = UB[0:1, 6144:8192]
        t_cwt = None
        t_cwt_prev = None
        for tt in range(NTT):
            t_tr = P.op("tensor", lambda t, tt=tt: t.transpose(
                ps[4][0:16, 0:128], cw_sb[:, tt, :], identF[:]),
                waits=[t_cwt_prev, r1bufs.get("t_r1_last")])
            t_cwt = P.op("vector", lambda v, tt=tt: v.tensor_copy(
                cwT[:, tt * 128:(tt + 1) * 128], ps[4][0:16, 0:128]), waits=[t_tr])
            t_cwt_prev = t_cwt
        P.fence()

        westr5 = [[khb[0][:, 0:2048].rearrange("p (c o) -> p c o", c=NCH),
                   khb[0][:, 2048:4096].rearrange("p (c o) -> p c o", c=NCH)],
                  [khb[1][:, 0:2048].rearrange("p (c o) -> p c o", c=NCH),
                   khb[1][:, 2048:4096].rearrange("p (c o) -> p c o", c=NCH)]]
        ws2cs5 = [vhb[0][:].rearrange("p (hc o) -> p hc o", hc=32),
                  vhb[1][:].rearrange("p (hc o) -> p hc o", hc=32)]
        g_e5 = tmpl_sb[:, 0:4096].rearrange("p (j t) -> p j t", j=8)
        B = dict(lnx3g_v=lnx5g_v,
                 we2s=U8a[:, 0:16384].bitcast(BF16).rearrange("p (j o) -> p j o", j=8),
                 sh_all=U8a[:, 16384:32768].bitcast(BF16).rearrange("p (hc t) -> p hc t", hc=32),
                 acc=XA[:].rearrange("p (c t) -> p c t", c=NCH), XA=XA,
                 westr=westr5, ws2cs=ws2cs5, g_e=g_e5, ssb=ssb, xch=xch, ps=ps, psL=psL)
        emit_ffn1_dense(nc, P, C, B, lnx5_scr, x5_scr, xT6,
                        we1, we3, we2, ws1, ws3, ws2, t_cwt, cwT, cw_row, r1bufs)
        P.fence()
        P.op("sync", lambda s: s.nop(), inc=False)
        P.emit(block)
    return nc



def _shard_xT(x):
    XT = np.ascontiguousarray(x.reshape(16384, D).T)
    return [np.ascontiguousarray(XT[:, c * TOK:(c + 1) * TOK]) for c in range(8)]


def _rope_tables(core):
    q0 = 2048 * (core % 2)
    pos = np.arange(q0, q0 + TOK, dtype=np.float64)
    inv = 1.0 / (10000.0 ** (np.arange(0, 32, 2, dtype=np.float64) / 32.0))
    ang = pos[None, :] * inv[:, None]
    cos = np.cos(ang)
    sin = np.sin(ang)
    cos32 = np.concatenate([cos, cos], axis=0).astype(np.float32)
    sinm = np.concatenate([-sin, sin], axis=0).astype(np.float32)
    return cos32, sinm


def _rot_logits(logits_full, core):
    own = logits_full[core * TOK:(core + 1) * TOK]
    rest = np.concatenate([logits_full[:core * TOK], logits_full[(core + 1) * TOK:]], axis=0)
    rot = np.concatenate([own, rest], axis=0)
    return np.ascontiguousarray(rot.reshape(128, 128, 16).transpose(1, 0, 2).reshape(128, 2048))


def prep_L2_maps(inputs, logits_full):
    WqkvT = np.ascontiguousarray(inputs["L0_attn_Wqkv_w"].T)
    qkvb = inputs["L0_attn_Wqkv_b"]
    qkvb_pp = np.ascontiguousarray(qkvb.reshape(48, 128).T)
    vb_row = np.ascontiguousarray(qkvb[4096:].reshape(1, 2048))
    e0b = np.ascontiguousarray(inputs["L0_ffn_expert_b"].reshape(1, 2048))
    maps = []
    for c in range(8):
        cos32, sinm = _rope_tables(c)
        maps.append({
            "lgt_full": _rot_logits(logits_full, c),
            "e0w": inputs["L0_ffn_expert_w"],
            "e0b": e0b,
            "WqkvT": WqkvT,
            "qkvb_pp": qkvb_pp,
            "vb_row": vb_row,
            "cosb": cos32,
            "sinm": sinm,
        })
    return maps


def _bfc(a):
    return np.ascontiguousarray(a).astype(ml_dtypes.bfloat16)


def _mk_tmpl(core):
    c0 = 4096 - 2048 * (core % 2)
    p = np.arange(128)[:, None]
    c = np.arange(TMPLW)[None, :]
    return (c < p + c0).astype(ml_dtypes.bfloat16)


def kernel_multilaunch(**inputs):
    x = np.asarray(inputs["L0_attn_x"], dtype=np.float32)
    shards = _shard_xT(x)
    r0w = np.ascontiguousarray(np.asarray(inputs["L0_ffn_router_w"], np.float32))
    r0b = np.ascontiguousarray(np.asarray(inputs["L0_ffn_router_b"], np.float32).reshape(1, NEXP))

    nc1 = build_L1()
    maps1 = [{"xT": shards[c], "r0w": r0w, "r0b": r0b} for c in range(8)]
    r1 = run_bass_kernel_spmd(nc1, maps1, core_ids=list(range(8))).results
    logits0 = np.concatenate([r["logits"] for r in r1], axis=0)
    lnx0 = [np.asarray(r["lnx"]) for r in r1]

    nc2 = build_L2()
    maps2 = prep_L2_maps(inputs, logits0)
    for c in range(8):
        maps2[c]["xT"] = shards[c]
        maps2[c]["lnx0"] = lnx0[c].view(ml_dtypes.bfloat16) if lnx0[c].dtype != ml_dtypes.bfloat16 else lnx0[c]
    r2 = run_bass_kernel_spmd(nc2, maps2, core_ids=list(range(8))).results
    xT1 = [np.asarray(r["xT1"]) for r in r2]
    q0 = [np.asarray(r["q0"]) for r in r2]
    k0 = [np.asarray(r["k0"]) for r in r2]
    v0 = [np.asarray(r["v0"]) for r in r2]

    nc3 = build_L3()
    maps3 = []
    for c in range(8):
        b = c // 2
        maps3.append({
            "xT1": xT1[c], "q0": q0[c],
            "k_seq": np.ascontiguousarray(np.concatenate([k0[2 * b], k0[2 * b + 1]], axis=1)),
            "v_seq": np.ascontiguousarray(np.concatenate([v0[2 * b], v0[2 * b + 1]], axis=0)),
            "tmpl": _mk_tmpl(c), "r0w": r0w, "r0b": r0b,
        })
    r3 = run_bass_kernel_spmd(nc3, maps3, core_ids=list(range(8))).results
    xT2 = [np.asarray(r["xT2"]) for r in r3]
    logits2 = np.concatenate([r["logits"] for r in r3], axis=0)
    lnx2 = [np.asarray(r["lnx"]) for r in r3]

    nc4 = build_L4()
    e0b = np.ascontiguousarray(np.asarray(inputs["L0_ffn_expert_b"], np.float32).reshape(1, D))
    maps4 = []
    for c in range(8):
        maps4.append({
            "xT2": xT2[c], "lnx2": lnx2[c],
            "lgt_full": _rot_logits(logits2, c),
            "e0w": np.asarray(inputs["L0_ffn_expert_w"], np.float32), "e0b": e0b,
            "r1w": np.ascontiguousarray(np.asarray(inputs["L1_ffn_w_router"], np.float32)),
            "we1": np.asarray(inputs["L1_ffn_we1"], np.float32),
            "we3": np.asarray(inputs["L1_ffn_we3"], np.float32),
            "we2": np.asarray(inputs["L1_ffn_we2"], np.float32),
            "ws1": np.asarray(inputs["L1_ffn_ws1"], np.float32),
            "ws3": np.asarray(inputs["L1_ffn_ws3"], np.float32),
            "ws2": np.asarray(inputs["L1_ffn_ws2"], np.float32),
            "wqT": np.ascontiguousarray(np.asarray(inputs["L1_attn_wq"], np.float32).T),
            "wkT": np.ascontiguousarray(np.asarray(inputs["L1_attn_wk"], np.float32).T),
            "wvT": np.ascontiguousarray(np.asarray(inputs["L1_attn_wv"], np.float32).T),
        })
    r4 = run_bass_kernel_spmd(nc4, maps4, core_ids=list(range(8))).results
    xT4 = [np.asarray(r["xT4"]) for r in r4]
    q1 = [np.asarray(r["q1"]) for r in r4]
    k1 = [np.asarray(r["k1"]) for r in r4]
    v1 = [np.asarray(r["v1"]) for r in r4]

    nc5 = build_L5()
    maps5 = []
    for c in range(8):
        b = c // 2
        maps5.append({
            "xT4": xT4[c], "q1": q1[c],
            "k_seq": np.ascontiguousarray(np.concatenate([k1[2 * b], k1[2 * b + 1]], axis=1)),
            "v_seq": np.ascontiguousarray(np.concatenate([v1[2 * b], v1[2 * b + 1]], axis=0)),
            "tmpl": _mk_tmpl(c),
            "woT": np.ascontiguousarray(np.asarray(inputs["L1_attn_wo"], np.float32).T),
            "r1w": np.ascontiguousarray(np.asarray(inputs["L1_ffn_w_router"], np.float32)),
            "we1": np.asarray(inputs["L1_ffn_we1"], np.float32),
            "we3": np.asarray(inputs["L1_ffn_we3"], np.float32),
            "we2": np.asarray(inputs["L1_ffn_we2"], np.float32),
            "ws1": np.asarray(inputs["L1_ffn_ws1"], np.float32),
            "ws3": np.asarray(inputs["L1_ffn_ws3"], np.float32),
            "ws2": np.asarray(inputs["L1_ffn_ws2"], np.float32),
        })
    r5 = run_bass_kernel_spmd(nc5, maps5, core_ids=list(range(8))).results
    out = np.concatenate([np.asarray(r["xT6"]) for r in r5], axis=1)
    return np.ascontiguousarray(out.T).reshape(4, 4096, D)




def emit_weight_tiling(nc, P, scr_map, conv_srcs, conv_f32, conv_b16):
    t_ld = None
    t_cp = None
    t_st = None
    for (name, srcm, scrm, din) in conv_srcs:
        nch_in = din // 128
        cols = srcm.shape[1]
        ncb = cols // 512
        for q in range(nch_in // 4):
            for cb in range(ncb):
                t_ld = P.dma("sync", lambda s, srcm=srcm, q=q, cb=cb: s.dma_start(
                    out=conv_f32[:].rearrange("p (c w) -> p c w", c=4),
                    in_=srcm.rearrange("(c p) w -> p c w", p=128)[:, 4 * q:4 * q + 4,
                                                                 cb * 512:(cb + 1) * 512]),
                    waits=[t_cp], sem="convf")
                t_cp = P.op("gpsimd", lambda g, t_ld=t_ld: g.tensor_copy(
                    conv_b16[:].rearrange("p (j c h) -> p j c h", j=4, c=4),
                    conv_f32[:].rearrange("p (c j h) -> p c j h", c=4, j=4).transpose([0, 2, 1, 3])),
                    waits=[t_ld, t_st])
                t_st = P.dma("gpsimd", lambda g, scrm=scrm, q=q, cb=cb, t_cp=t_cp: g.dma_start(
                    out=scrm[:, :, 4 * q * 128:(4 * q + 4) * 128].rearrange(
                        "b p w -> b p w")[cb * 4:(cb + 1) * 4, :, :].transpose([1, 0, 2]),
                    in_=conv_b16[:].rearrange("p (j w) -> p j w", j=4)),
                    waits=[t_cp], sem="cv")
        scr_map[name] = (scrm, P.now("d_cv"))
    return scr_map


def build_mega():
    nc = bass.Bass("TRN2", target_bir_lowering=False)
    xT = nc.dram_tensor("xT", [D, TOK], F32, kind="ExternalInput").ap()
    r0w = nc.dram_tensor("r0w", [D, NEXP], F32, kind="ExternalInput").ap()
    r0b = nc.dram_tensor("r0b", [1, NEXP], F32, kind="ExternalInput").ap()
    e0w = nc.dram_tensor("e0w", [D, D], F32, kind="ExternalInput").ap()
    e0b = nc.dram_tensor("e0b", [1, D], F32, kind="ExternalInput").ap()
    WqkvT = nc.dram_tensor("WqkvT", [D, 3 * D], F32, kind="ExternalInput").ap()
    qkvb_pp = nc.dram_tensor("qkvb_pp", [128, 48], F32, kind="ExternalInput").ap()
    vb_row = nc.dram_tensor("vb_row", [1, D], F32, kind="ExternalInput").ap()
    cosb_i = nc.dram_tensor("cosb", [32, TOK], F32, kind="ExternalInput").ap()
    sinm_i = nc.dram_tensor("sinm", [32, TOK], F32, kind="ExternalInput").ap()
    tmpl = nc.dram_tensor("tmpl", [128, TMPLW], BF16, kind="ExternalInput").ap()
    maskr8 = nc.dram_tensor("maskr8", [128, 8], F32, kind="ExternalInput").ap()
    r1w = nc.dram_tensor("r1w", [D, NEXP], F32, kind="ExternalInput").ap()
    we1 = nc.dram_tensor("we1", [NEXP, D, 1024], F32, kind="ExternalInput").ap()
    we3 = nc.dram_tensor("we3", [NEXP, D, 1024], F32, kind="ExternalInput").ap()
    we2 = nc.dram_tensor("we2", [NEXP, 1024, D], F32, kind="ExternalInput").ap()
    ws1 = nc.dram_tensor("ws1", [D, 4096], F32, kind="ExternalInput").ap()
    ws3 = nc.dram_tensor("ws3", [D, 4096], F32, kind="ExternalInput").ap()
    ws2 = nc.dram_tensor("ws2", [4096, D], F32, kind="ExternalInput").ap()
    wqT = nc.dram_tensor("wqT", [D, D], F32, kind="ExternalInput").ap()
    wkT = nc.dram_tensor("wkT", [D, D], F32, kind="ExternalInput").ap()
    wvT = nc.dram_tensor("wvT", [D, D], F32, kind="ExternalInput").ap()
    woT = nc.dram_tensor("woT", [D, D], F32, kind="ExternalInput").ap()
    xOUT = nc.dram_tensor("xOUT", [D, TOK], F32, kind="ExternalOutput").ap()

    lgt_loc = nc.dram_tensor("lgt_loc", [TOK, NEXP], F32, kind="Internal").ap()
    lgt_full = nc.dram_tensor("lgt_full", [16384, NEXP], F32, kind="Internal").ap()
    kv_in = nc.dram_tensor("kv_in", [2 * TOK, D], BF16, kind="Internal").ap()
    kv_out = nc.dram_tensor("kv_out", [4 * TOK, D], BF16, kind="Internal").ap()
    q_scr = nc.dram_tensor("q_scr", [D, TOK], BF16, kind="Internal").ap()
    lnA = nc.dram_tensor("lnA", [D, TOK], BF16, kind="Internal").ap()
    lnB = nc.dram_tensor("lnB", [D, TOK], BF16, kind="Internal").ap()
    xTa = nc.dram_tensor("xTa", [D, TOK], F32, kind="Internal").ap()
    xTb = nc.dram_tensor("xTb", [D, TOK], F32, kind="Internal").ap()
    x_mid = nc.dram_tensor("x_mid", [D, TOK], F32, kind="Internal").ap()
    we1_scr = nc.dram_tensor("we1_scr", [NEXP, 8, 128, D], BF16, kind="Internal").ap()
    we3_scr = nc.dram_tensor("we3_scr", [NEXP, 8, 128, D], BF16, kind="Internal").ap()
    we2b_scr = nc.dram_tensor("we2b_scr", [NEXP, 1024, D], BF16, kind="Internal").ap()
    ws1_scr = nc.dram_tensor("ws1_scr", [32, 128, D], BF16, kind="Internal").ap()
    ws3_scr = nc.dram_tensor("ws3_scr", [32, 128, D], BF16, kind="Internal").ap()
    ws2_scr = nc.dram_tensor("ws2_scr", [NCH, 128, 4096], BF16, kind="Internal").ap()
    qkv_scr = nc.dram_tensor("qkv_scr", [32, 128, D], BF16, kind="Internal").ap()
    wq_scr = nc.dram_tensor("wq_scr", [NCH, 128, D], BF16, kind="Internal").ap()
    wk_scr = nc.dram_tensor("wk_scr", [NCH, 128, D], BF16, kind="Internal").ap()
    wo_scr = nc.dram_tensor("wo_scr", [NCH, 128, D], BF16, kind="Internal").ap()

    with ExitStack() as ctx:
        E = ctx.enter_context
        P = Plan(nc)
        U8a = E(nc.sbuf_tensor("U8a", [128, NCH * TOK], BF16))
        UB = E(nc.sbuf_tensor("UB", [128, 8192], F32))
        XA = E(nc.sbuf_tensor("XA", [128, 8192], F32))
        khb = [E(nc.sbuf_tensor(f"khb{i}", [128, SEQ], BF16)) for i in range(2)]
        vhb = [E(nc.sbuf_tensor(f"vhb{i}", [128, SEQ], BF16)) for i in range(2)]
        _qh = E(nc.sbuf_tensor("qh", [128, TOK], BF16))
        qh2 = [_qh, _qh]
        tmpl_sb = E(nc.sbuf_tensor("tmpl_sb", [128, TMPLW], BF16))
        pexp = [E(nc.sbuf_tensor(f"pexp{i}", [128, TG], BF16)) for i in range(3)]
        sq2 = [E(nc.sbuf_tensor(f"sq{i}", [128, TG], F32)) for i in range(2)]
        ssb = E(nc.sbuf_tensor("ssb", [128, TG], BF16))
        xch = [E(nc.sbuf_tensor(f"xch{i}", [128, TG], F32)) for i in range(2)]
        cw_sb = E(nc.sbuf_tensor("cw_sb", [128, NCH, NEXP], F32))
        srw = E(nc.sbuf_tensor("srw", [128, NCH, NEXP], BF16))
        r1w_sb = E(nc.sbuf_tensor("r1w_sb", [128, NCH, NEXP], BF16))
        srb = E(nc.sbuf_tensor("srb", [1, NEXP], BF16))
        slog = E(nc.sbuf_tensor("slog", [128, TG // 128, NEXP], F32))
        probs16 = E(nc.sbuf_tensor("probs16", [128, NEXP], F32))
        sel16 = E(nc.sbuf_tensor("sel16", [128, NEXP], F32))
        m8 = E(nc.sbuf_tensor("m8", [128, 8], F32))
        negm = E(nc.sbuf_tensor("negm", [128, 1], F32))
        mneg = E(nc.sbuf_tensor("mneg", [128, 128], F32))
        ssum = E(nc.sbuf_tensor("ssum", [128, 128], F32))
        mask8_sb = E(nc.sbuf_tensor("mask8_sb", [128, 8], F32))
        small16 = {k: E(nc.sbuf_tensor("r_" + k, [128, NEXP], F32))
                   for k in ("lo", "hi", "mid", "cond", "cnt", "dd", "a05")}
        row16 = E(nc.sbuf_tensor("row16", [1, NEXP], F32))
        qkvb_sb = E(nc.sbuf_tensor("qkvb_sb", [128, 48], F32))
        qkvbs = E(nc.sbuf_tensor("qkvbs", [128, 48], F32))
        km = E(nc.sbuf_tensor("km", [1, 1], F32))
        bmx = E(nc.sbuf_tensor("bmx", [1, 1], F32))
        negbt = E(nc.sbuf_tensor("negbt", [128, 1], F32))
        drow = E(nc.sbuf_tensor("drow", [1, TG], F32))
        identB = E(nc.sbuf_tensor("identB", [128, 128], BF16))
        identF = E(nc.sbuf_tensor("identF", [128, 128], F32))
        negI = E(nc.sbuf_tensor("negI", [128, 128], BF16))
        rows = ln_rows(nc, ctx)
        ps = [E(nc.psum_tensor(f"ps{i}", [128, 512], F32)) for i in range(6)]
        psL = E(nc.psum_tensor("psL", [128, NEXP], F32))
        psT = E(nc.psum_tensor("psT", [128, 512], BF16))
        block = E(nc.Block())

        xg = XA[:].rearrange("p (c t) -> p c t", c=NCH)
        nsq = XA[:][:, 0:SEQ]
        lng_v = UB[:, 0:4096].bitcast(BF16).rearrange("p (c t) -> p c t", c=NCH)
        cwT = UB[0:16, 4096:6144]
        cw_row = UB[0:1, 6144:8192]
        cos_sb = UB[0:32, 0:2048]
        sinm_sb = UB[0:32, 2048:4096]
        swp = UB[0:32, 4096:5120].bitcast(BF16)
        rsc = UB[0:32, 5120:6144].bitcast(BF16)
        eb_row = UB[0:1, 6144:7168].bitcast(BF16)
        vb_row_sb = UB[0:1, 7168:8192].bitcast(BF16)
        vstrip4 = UB[:, 0:4096].bitcast(BF16).rearrange("p (c t) -> p c t", c=NCH)
        hh = [khb[0][:, 0:2048], khb[1][:, 0:2048]]
        strips = [khb[0][:, 2048:4096].rearrange("p (c o) -> p c o", c=NCH),
                  khb[1][:, 2048:4096].rearrange("p (c o) -> p c o", c=NCH)]
        strips2d = [khb[0][:, 2048:4096], khb[1][:, 2048:4096]]
        westr2d = [[khb[0][:, 0:2048], khb[0][:, 2048:4096]],
                   [khb[1][:, 0:2048], khb[1][:, 2048:4096]]]
        ws2cs2d = [vhb[0][:], vhb[1][:]]
        ytm = [vhb[0][:, 0:512], vhb[0][:, 512:1024]]
        vtm = [vhb[1][:, 0:512], vhb[1][:, 512:1024]]
        westr = [[khb[0][:, 0:2048].rearrange("p (c o) -> p c o", c=NCH),
                  khb[0][:, 2048:4096].rearrange("p (c o) -> p c o", c=NCH)],
                 [khb[1][:, 0:2048].rearrange("p (c o) -> p c o", c=NCH),
                  khb[1][:, 2048:4096].rearrange("p (c o) -> p c o", c=NCH)]]
        ws2cs = [vhb[0][:].rearrange("p (hc o) -> p hc o", hc=32),
                 vhb[1][:].rearrange("p (hc o) -> p hc o", hc=32)]
        g_e = tmpl_sb[:, 0:4096].rearrange("p (j t) -> p j t", j=8)
        e0w_sb = U8a[:].rearrange("p (c o) -> p c o", c=NCH)
        lnall = U8a[:].rearrange("p (c t) -> p c t", c=NCH)
        attn_all = U8a[:].rearrange("p (h t) -> p h t", h=NH)
        kview = kv_out.rearrange("(r half dp p) t -> half dp p r t", r=2, half=2, dp=16)[0]
        vview = kv_out.rearrange("(r half sp p) d -> half p r sp d", r=2, half=2, sp=16)[1]

        C = setup_consts(nc, P, ctx)
        C["identB"] = identB
        C["negI"] = negI
        P.op("gpsimd", lambda g: g.memset(identB[:], 0.0))
        P.op("gpsimd", lambda g: g.affine_select(
            out=identB[:], in_=identB[:], compare_op=ALU.not_equal, fill=1.0,
            base=0, pattern=[[-1, 128]], channel_multiplier=1))
        P.op("gpsimd", lambda g: g.memset(identF[:], 0.0))
        P.op("gpsimd", lambda g: g.affine_select(
            out=identF[:], in_=identF[:], compare_op=ALU.not_equal, fill=1.0,
            base=0, pattern=[[-1, 128]], channel_multiplier=1))
        P.op("gpsimd", lambda g: g.memset(negI[:], 0.0))
        P.op("gpsimd", lambda g: g.affine_select(
            out=negI[:], in_=negI[:], compare_op=ALU.not_equal, fill=MASKVAL,
            base=0, pattern=[[-1, 128]], channel_multiplier=1))
        P.dma("gpsimd", lambda g: g.dma_start(
            out=srw[:], in_=r0w.rearrange("(c p) e -> p c e", p=128)), sem="cstg")
        P.dma("gpsimd", lambda g: g.dma_start(out=srb[:], in_=r0b), sem="cstg")
        t_r1w = P.dma("gpsimd", lambda g: g.dma_start(
            out=r1w_sb[:], in_=r1w.rearrange("(c p) e -> p c e", p=128)), sem="cstg")
        t_qb = P.dma("sync", lambda s: s.dma_start(out=qkvb_sb[:], in_=qkvb_pp), sem="cst")
        P.dma("sync", lambda s: s.dma_start(out=mask8_sb[:], in_=maskr8), sem="cst")
        P.op("vector", lambda v: v.tensor_scalar_mul(qkvbs[:], qkvb_sb[:], ATT_SCALE),
             waits=[t_qb])

        conv_f32 = khb[0][:].bitcast(F32)
        conv_b16 = khb[1][:, 0:2048]
        scr = {}
        conv_srcs = []
        for e in range(NEXP):
            conv_srcs.append((f"we1_{e}", we1[e], we1_scr[e], D))
            conv_srcs.append((f"we3_{e}", we3[e], we3_scr[e], D))
        conv_srcs.append(("ws1", ws1, ws1_scr, D))
        conv_srcs.append(("ws3", ws3, ws3_scr, D))
        conv_srcs.append(("ws2", ws2, ws2_scr, 4096))
        conv_srcs.append(("qkv", WqkvT[:, 0:4096], qkv_scr, D))
        conv_srcs.append(("wq", wqT, wq_scr, D))
        conv_srcs.append(("wk", wkT, wk_scr, D))
        conv_srcs.append(("wo", woT, wo_scr, D))
        emit_weight_tiling(nc, P, scr, conv_srcs, conv_f32, conv_b16)
        t_w2c = None
        for e in range(NEXP):
            t_w2c = P.dma("gpsimd", lambda g, e=e: g.dma_start(
                out=we2b_scr[e], in_=we2[e]), waits=[], sem="we2b")
        scr["we2b"] = (we2b_scr, P.now("d_we2b"))
        for k in ("we1", "we3"):
            pass
        P.fence()

        ncc = [0]

        def collective(fn, waits=()):
            P.sem("cc")
            ncc[0] += 1
            w = P._norm(list(waits) + [P.now("sG")])
            P.ops.append(("gpsimd", fn, w, ("cc", 1), "dma"))
            P.tot["cc"] = ncc[0]
            return ("cc", ncc[0])

        def emit_ln_logits(xsrc, lnx_dst, xadd=None, x_dst=None):
            t_last = None
            for g in range(NG):
                t_x = P.dma("sync", lambda s, g=g: s.dma_start(out=xg[:], in_=fm_view(xsrc, g)),
                            sem="xg")
                if xadd is not None:
                    t_x = P.op("vector", lambda v, g=g: v.tensor_tensor(
                        xg[:], xg[:], xadd[:, :, g * TG:(g + 1) * TG], op=ALU.add), waits=[t_x])
                    P.dma("sync", lambda s, g=g: s.dma_start(out=fm_view(x_dst, g), in_=xg[:]),
                          waits=[t_x], sem="xg")
                t_ln = emit_ln_group(nc, P, C, xg, sq2, lng_v, rows, ps[4], ps[5], ps[2], ps[3], t_x)
                t_cp = None
                for tt in range(TG // 128):
                    for i in range(NCH):
                        P.op("tensor", lambda t, i=i, tt=tt: t.matmul(
                            psL[:], lng_v[:, i, tt * 128:(tt + 1) * 128], srw[:, i, :],
                            start=(i == 0), stop=False),
                            waits=([t_ln, t_cp] if i == 0 else ()), inc=False)
                    t_mm = P.op("tensor", lambda t: t.matmul(
                        psL[:], C["ones1_b"][:], srb[:], start=False, stop=True))
                    t_cp = P.op("vector", lambda v, tt=tt: v.tensor_copy(slog[:, tt, :], psL[:]),
                                waits=[t_mm])
                P.dma("sync", lambda s, g=g: s.dma_start(out=fm_view(lnx_dst, g), in_=lng_v[:]),
                      waits=[t_ln], sem="lng")
                t_last = P.dma("sync", lambda s, g=g: s.dma_start(
                    out=lgt_loc.rearrange("(g tt p) e -> p (g tt) e", p=128, g=NG)
                    [:, g * 4:(g + 1) * 4, :], in_=slog[:]), waits=[t_cp], sem="slog")
                P.fence()
            return t_last

        def emit_routing_masked():
            rbufs = dict(lgt=UB[:, 0:2048], probs=UB[:, 2048:4096], scr=UB[:, 4096:6144],
                         mneg=mneg, ssum=ssum, row16=row16, mask8=mask8_sb, **small16)
            rbufs["psC"] = ps[0]
            rbufs["psC2"] = ps[1]
            return emit_routing0(nc, P, C, ctx, E, lgt_full.rearrange("(j p) e -> p j e", p=128),
                                 rbufs)

        def emit_qkv(lnrdy, which):
            if which == 0:
                t_cs = P.dma("sync", lambda s: s.dma_start(out=cos_sb, in_=cosb_i), sem="cst")
                t_sn = P.dma("sync", lambda s: s.dma_start(out=sinm_sb, in_=sinm_i), sem="cst")
                t_vb = P.dma("gpsimd", lambda g: g.dma_start(out=vb_row_sb, in_=vb_row), sem="cstg")
                P.fence()
                pbufs = dict(strips=strips, strips2d=strips2d, hh=hh, swp=swp, rsc=rsc,
                             psP=[ps[0], ps[1]],
                             t_pev=[None, None], vstrip=vstrip4, vtm=vtm, t_vev=[None, None])
                emit_proj_head_fm(nc, P, C, lnall, WqkvT, 0, q_scr, NH, qkvb_sb, 1.0,
                                  True, cos_sb, sinm_sb, pbufs, lnrdy, w_scr=scr["qkv"])
                P.fence()
                emit_proj_head_fm(nc, P, C, lnall, WqkvT, 16 * 128, kv_in[0:TOK, :], NH, qkvbs, ATT_SCALE,
                                  True, cos_sb, sinm_sb, pbufs, lnrdy, bias_col0=16, w_scr=scr["qkv"])
                P.fence()
                emit_proj_v_tm(nc, P, C, lnall, WqkvT, 2 * D, kv_in[TOK:2 * TOK, :], vb_row_sb,
                               pbufs, lnrdy)
            else:
                pbufs = dict(strips=strips, strips2d=strips2d, hh=hh, swp=None, rsc=None,
                             psP=[ps[0], ps[1]],
                             t_pev=[None, None], vstrip=vstrip4, vtm=vtm, t_vev=[None, None])
                emit_proj_head_fm(nc, P, C, lnall, wqT, 0, q_scr, NH, None, 1.0,
                                  False, None, None, pbufs, lnrdy, w_scr=scr["wq"])
                P.fence()
                emit_proj_head_fm(nc, P, C, lnall, wkT, 0, kv_in[0:TOK, :], NH, None, ATT_SCALE,
                                  False, None, None, pbufs, lnrdy, w_scr=scr["wk"])
                P.fence()
                emit_proj_v_tm(nc, P, C, lnall, wvT, 0, kv_in[TOK:2 * TOK, :], None, pbufs, lnrdy)
            P.fence()

        def k_src2(h):
            return kview[h, :, :]

        def v_src2(h):
            return vview[:, :, :, h * 128:(h + 1) * 128]

        def emit_attn(use_eps):
            t_tm = P.dma("sync", lambda s: s.dma_start(out=tmpl_sb[:], in_=tmpl), sem="cst")
            P.fence()
            abufs = dict(kh=khb, vh=vhb, qh=qh2, nsq=nsq, km=km, bmx=bmx, negbt=negbt,
                         drow=drow, rb128=sq2[0], pexp=pexp, tmpl=tmpl_sb,
                         psSc=[ps[0], ps[1]], psO=ps[2], psD=ps[3], psBr=ps[4],
                         t_exp_done=[None, None], t_pex_rd=[None, None, None])
            emit_attention(nc, P, C, abufs, None, None, q_scr, attn_all, use_eps=use_eps,
                           k_src=k_src2, v_src=v_src2)
            P.fence()

        emit_ln_logits(xT, lnA)
        collective(lambda g: g.collective_compute(
            "AllGather", ALU.bypass, replica_groups=[list(range(8))],
            ins=[lgt_loc.opt()], outs=[lgt_full.opt()]), waits=[P.now("d_slog")])
        P.fence()

        t_a05 = emit_routing_masked()
        t_ew = P.dma("gpsimd", lambda g: g.dma_start(
            out=e0w_sb, in_=e0w.rearrange("(c p) o -> p c o", p=128)), sem="e0w")
        t_eb = P.dma("gpsimd", lambda g: g.dma_start(out=eb_row, in_=e0b), sem="cstg")
        P.fence()
        ebufs = dict(lng=lng_v, xg=xg, ytm=ytm, psE=[ps[0], ps[1]], psT=psT, identB=identB,
                     t_evac=[None, None], t_tr=[None, None])
        emit_expert_apply(nc, P, C, lnA, xT, xTa, e0w_sb, eb_row, small16["a05"][:],
                          ebufs, t_a05, [t_ew, t_eb])
        P.fence()
        t_lnrdy = None
        for g in range(NG):
            t_x = P.dma("sync", lambda s, g=g: s.dma_start(out=xg[:], in_=fm_view(xTa, g)), sem="xg")
            t_lnrdy = emit_ln_group(nc, P, C, xg, sq2, lnall[:, :, g * TG:(g + 1) * TG],
                                    rows, ps[4], ps[5], ps[2], ps[3], t_x)
            P.fence()
        emit_qkv(t_lnrdy, 0)
        collective(lambda g: g.collective_compute(
            "AllGather", ALU.bypass, replica_groups=[[0, 1], [2, 3], [4, 5], [6, 7]],
            ins=[kv_in.opt()], outs=[kv_out.opt()]),
            waits=[P.now("d_hh0"), P.now("d_hh1"), P.now("d_vtm0"), P.now("d_vtm1")])
        P.fence()

        emit_attn(False)
        emit_ln_logits(xTa, lnA, xadd=attn_all, x_dst=xTb)
        collective(lambda g: g.collective_compute(
            "AllGather", ALU.bypass, replica_groups=[list(range(8))],
            ins=[lgt_loc.opt()], outs=[lgt_full.opt()]), waits=[P.now("d_slog")])
        P.fence()

        t_a05 = emit_routing_masked()
        t_ew = P.dma("gpsimd", lambda g: g.dma_start(
            out=e0w_sb, in_=e0w.rearrange("(c p) o -> p c o", p=128)), sem="e0w")
        t_eb = P.dma("gpsimd", lambda g: g.dma_start(out=eb_row, in_=e0b), sem="cstg")
        P.fence()
        r1bufs = dict(psL=psL, negm=negm, probs16=probs16, m8=m8, sel16=sel16, t_r1w=t_r1w)

        def post_group4(g, t_add):
            t_ln = emit_ln_group(nc, P, C, xg, sq2, lng_v, rows, ps[4], ps[5], ps[2], ps[3], t_add)
            t_st = P.dma("sync", lambda s, g=g: s.dma_start(
                out=fm_view(lnB, g), in_=lng_v[:]), waits=[t_ln], sem="lnx3")
            t_r1 = emit_router1_top4(nc, P, C, lng_v, r1w_sb, cw_sb[:], r1bufs, g, t_ln)
            return [t_st, t_r1]

        ebufs = dict(lng=lng_v, xg=xg, ytm=ytm, psE=[ps[0], ps[1]], psT=psT, identB=identB,
                     t_evac=[None, None], t_tr=[None, None])
        emit_expert_apply(nc, P, C, lnA, xTb, x_mid, e0w_sb, eb_row, small16["a05"][:],
                          ebufs, t_a05, [t_ew, t_eb], post_group=post_group4)
        P.fence()

        def emit_cwt():
            t_cwt = None
            t_prev = None
            for tt in range(NTT):
                t_tr = P.op("tensor", lambda t, tt=tt: t.transpose(
                    ps[4][0:16, 0:128], cw_sb[:, tt, :], identF[:]),
                    waits=[t_prev, r1bufs.get("t_r1_last")])
                t_cwt = P.op("vector", lambda v, tt=tt: v.tensor_copy(
                    cwT[:, tt * 128:(tt + 1) * 128], ps[4][0:16, 0:128]), waits=[t_tr])
                t_prev = t_cwt
            P.fence()
            return t_cwt

        t_cwt = emit_cwt()
        B = dict(lnx3g_v=lng_v,
                 we2s=U8a[:, 0:16384].bitcast(BF16).rearrange("p (j o) -> p j o", j=8),
                 sh_all=U8a[:, 16384:32768].bitcast(BF16).rearrange("p (hc t) -> p hc t", hc=32),
                 acc=xg, XA=XA, scr=scr, westr2d=westr2d, ws2cs2d=ws2cs2d,
                 westr=westr, ws2cs=ws2cs, g_e=g_e, ssb=ssb, xch=xch, ps=ps, psL=psL)
        emit_ffn1_dense(nc, P, C, B, lnB, x_mid, xTa,
                        we1, we3, we2, ws1, ws3, ws2, t_cwt, cwT, cw_row, r1bufs)
        P.fence()
        t_lnrdy = None
        for g in range(NG):
            t_x = P.dma("sync", lambda s, g=g: s.dma_start(out=xg[:], in_=fm_view(xTa, g)), sem="xg")
            t_lnrdy = emit_ln_group(nc, P, C, xg, sq2, lnall[:, :, g * TG:(g + 1) * TG],
                                    rows, ps[4], ps[5], ps[2], ps[3], t_x)
            P.fence()
        emit_qkv(t_lnrdy, 1)
        collective(lambda g: g.collective_compute(
            "AllGather", ALU.bypass, replica_groups=[[0, 1], [2, 3], [4, 5], [6, 7]],
            ins=[kv_in.opt()], outs=[kv_out.opt()]),
            waits=[P.now("d_hh0"), P.now("d_hh1"), P.now("d_vtm0"), P.now("d_vtm1")])
        P.fence()

        emit_attn(True)
        woT6 = woT.rearrange("(c p) o -> p c o", p=128)
        t_wl = [None, None]
        t_pP_rd = [None, None]
        r1bufs["t_r1_last"] = None
        for g in range(NG):
            t_resid = None
            for oc in range(NCH):
                cb = oc % 2
                t_wl[cb] = P.dma("sync", lambda s, oc=oc, cb=cb: s.dma_start(
                    out=strips2d[cb], in_=scr["wo"][0][oc]),
                    waits=[t_wl[cb], P.now("sT"), scr["wo"][1]], sem=f"strip{cb}")
                for h in range(NH):
                    P.op("tensor", lambda t, h=h, oc=oc, cb=cb, g=g: t.matmul(
                        ps[cb][:], strips[cb][:, h, :], attn_all[:, h, g * TG:(g + 1) * TG],
                        start=(h == 0), stop=(h == NH - 1)),
                        waits=([t_wl[cb], t_pP_rd[cb]] if h == 0 else ()), inc=(h == NH - 1))
                t_mm = P.now("sT")
                t_xc = P.dma("sync", lambda s, oc=oc, cb=cb, g=g: s.dma_start(
                    out=xch[cb][:], in_=fm_view(xTa, g)[:, oc, :]),
                    waits=[P.now("sV")], sem=f"xch{cb}")
                t_resid = P.op("vector", lambda v, oc=oc, cb=cb: v.tensor_add(
                    xg[:, oc, :], xch[cb][:], ps[cb][:]),
                    waits=[t_mm, t_xc] + ([P.now("sC"), P.now("sV")] if (g == 0 and oc == 0) else []))
                t_pP_rd[cb] = t_resid
            P.dma("sync", lambda s, g=g: s.dma_start(
                out=fm_view(x_mid, g), in_=xg[:]), waits=[t_resid], sem="x5")
            t_ln = emit_ln_group(nc, P, C, xg, sq2, lng_v, rows, ps[4], ps[5], ps[2], ps[3], t_resid)
            P.dma("sync", lambda s, g=g: s.dma_start(
                out=fm_view(lnB, g), in_=lng_v[:]), waits=[t_ln], sem="lnx3")
            emit_router1_top4(nc, P, C, lng_v, r1w_sb, cw_sb[:], r1bufs, g, t_ln)
            P.fence()
        t_cwt = emit_cwt()
        B = dict(lnx3g_v=lng_v,
                 we2s=U8a[:, 0:16384].bitcast(BF16).rearrange("p (j o) -> p j o", j=8),
                 sh_all=U8a[:, 16384:32768].bitcast(BF16).rearrange("p (hc t) -> p hc t", hc=32),
                 acc=xg, XA=XA, scr=scr, westr2d=westr2d, ws2cs2d=ws2cs2d,
                 westr=westr, ws2cs=ws2cs, g_e=g_e, ssb=ssb, xch=xch, ps=ps, psL=psL)
        emit_ffn1_dense(nc, P, C, B, lnB, x_mid, xOUT,
                        we1, we3, we2, ws1, ws3, ws2, t_cwt, cwT, cw_row, r1bufs)
        P.fence()
        P.op("sync", lambda s: s.nop(), inc=False)
        P.emit(block)
    return nc


def kernel_mega(**inputs):
    x = np.asarray(inputs["L0_attn_x"], dtype=np.float32)
    shards = _shard_xT(x)
    r0w = np.ascontiguousarray(np.asarray(inputs["L0_ffn_router_w"], np.float32))
    r0b = np.ascontiguousarray(np.asarray(inputs["L0_ffn_router_b"], np.float32).reshape(1, NEXP))
    qkvb = np.asarray(inputs["L0_attn_Wqkv_b"], np.float32)
    nc = build_mega()
    base = {
        "r0w": r0w, "r0b": r0b,
        "e0w": np.asarray(inputs["L0_ffn_expert_w"], np.float32),
        "e0b": np.ascontiguousarray(np.asarray(inputs["L0_ffn_expert_b"], np.float32).reshape(1, D)),
        "WqkvT": np.ascontiguousarray(np.asarray(inputs["L0_attn_Wqkv_w"], np.float32).T),
        "qkvb_pp": np.ascontiguousarray(qkvb.reshape(48, 128).T),
        "vb_row": np.ascontiguousarray(qkvb[4096:].reshape(1, D)),
        "r1w": np.ascontiguousarray(np.asarray(inputs["L1_ffn_w_router"], np.float32)),
        "we1": np.asarray(inputs["L1_ffn_we1"], np.float32),
        "we3": np.asarray(inputs["L1_ffn_we3"], np.float32),
        "we2": np.asarray(inputs["L1_ffn_we2"], np.float32),
        "ws1": np.asarray(inputs["L1_ffn_ws1"], np.float32),
        "ws3": np.asarray(inputs["L1_ffn_ws3"], np.float32),
        "ws2": np.asarray(inputs["L1_ffn_ws2"], np.float32),
        "wqT": np.ascontiguousarray(np.asarray(inputs["L1_attn_wq"], np.float32).T),
        "wkT": np.ascontiguousarray(np.asarray(inputs["L1_attn_wk"], np.float32).T),
        "wvT": np.ascontiguousarray(np.asarray(inputs["L1_attn_wv"], np.float32).T),
        "woT": np.ascontiguousarray(np.asarray(inputs["L1_attn_wo"], np.float32).T),
    }
    maps = []
    for c in range(8):
        cos32, sinm = _rope_tables(c)
        mk = np.zeros((128, 8), np.float32)
        mk[:, c] = 1.0
        m = dict(base)
        m.update({"xT": shards[c], "cosb": cos32, "sinm": sinm,
                  "tmpl": _mk_tmpl(c), "maskr8": mk})
        maps.append(m)
    res = run_bass_kernel_spmd(nc, maps, core_ids=list(range(8)))
    out = np.concatenate([np.asarray(r["xOUT"]) for r in res.results], axis=1)
    return np.ascontiguousarray(out.T).reshape(4, 4096, D)


def kernel(**inputs):
    return kernel_mega(**inputs)
